# version 09
# baseline (speedup 1.0000x reference)
import types

import numpy as np
import ml_dtypes
from contextlib import ExitStack

import concourse.bass as bass
import concourse.mybir as mybir
from concourse.bass_utils import run_bass_kernel_spmd

F32 = mybir.dt.float32
_NAMEC = [0]


def _uname(n):
    _NAMEC[0] += 1
    return f"sb_{n}_{_NAMEC[0]}"
BF16 = mybir.dt.bfloat16
AF = mybir.ActivationFunctionType
ALU = mybir.AluOpType
BF = ml_dtypes.bfloat16

BATCH, L, D = 4, 4096, 1024
T = 2048
NH, HD = 16, 64
NP = 8
DEPTH = 6
DI = 2730
NC_DI = 22
CW_LAST = DI - 21 * 128
WINDOW = 256
NEG = -1e30

NSUB = 4
SUB = 512
NKCOL = T // 128 + 2
RING = 6
CCGROUPS = [[0, 1], [2, 3], [4, 5], [6, 7]]


def _rotary_host():
    n = L
    half = np.arange(0, HD, 2, dtype=np.float64)
    inv_freq = 1.0 / 10000.0 ** (half / HD)
    t = np.arange(n, dtype=np.float64)
    freqs = t[:, None] * inv_freq[None, :]
    freqs = np.concatenate([freqs, freqs], -1)
    base_scale = (half + 0.4 * HD) / (1.4 * HD)
    power = (t - n // 2) / (WINDOW // 2)
    scale = base_scale[None, :] ** power[:, None]
    scale = np.concatenate([scale, scale], -1)
    cos = np.cos(freqs)
    sin = np.sin(freqs)
    rsign = np.where(np.arange(HD) < 32, -1.0, 1.0)
    QK = 8.0
    cosT = cos.T
    sinT = sin.T
    scT = scale.T
    csq = QK * cosT * scT
    ssq = QK * rsign[:, None] * sinT * scT
    csk = cosT / scT
    ssk = rsign[:, None] * sinT / scT
    out = []
    for tab in (csq, ssq, csk, ssk):
        dup = np.concatenate([tab, tab], 0)
        halves = np.stack([dup[:, :T], dup[:, T:]], 0)
        out.append(np.ascontiguousarray(halves.astype(np.float32)))
    return out


def _perm128():
    P = np.zeros((128, 128), dtype=np.float32)
    for m in range(128):
        k = (m + 32) % 64 + 64 * (m // 64)
        P[k, m] = 1.0
    return P.astype(BF)


def _tri_masks():
    c = np.arange(128)
    p = np.arange(128)[:, None]
    masku = np.where(p < c, NEG, 0.0).astype(BF)
    maskl = np.where(p > c, NEG, 0.0).astype(BF)
    return masku, maskl


def _hsel():
    hsel8 = np.zeros((128, 8, 16), dtype=np.float32)
    hselb = np.zeros((8, 16, 128), dtype=np.float32)
    for mc in range(8):
        for r in range(128):
            hsel8[r, mc, 2 * mc + (r >= 64)] = 1.0
            hselb[mc, 2 * mc + (r >= 64), r] = 1.0
    return hsel8.astype(BF), np.ascontiguousarray(hselb.transpose(1, 0, 2)).astype(BF)


def _sel2():
    s = np.zeros((2, 128), dtype=np.float32)
    s[0, :64] = 1.0
    s[1, 64:] = 1.0
    return s


def stage_inputs(inputs):
    x = np.asarray(inputs["L0_attn_x"])
    csq, ssq, csk, ssk = _rotary_host()
    p128 = _perm128()
    masku, maskl = _tri_masks()
    hsel8, hselb = _hsel()
    sel2 = _sel2()
    idn = np.eye(128, dtype=np.float32).astype(BF)
    onesrow = np.ones((1, 512), dtype=np.float32)

    qs = np.asarray(inputs["L0_ffn_q_scale"])
    ks = np.asarray(inputs["L0_ffn_k_scale"])
    def dup(v):
        return np.ascontiguousarray(np.tile(v, (1, 2))[:, :, None])
    def dup_perm(v):
        vp = v[:, (np.arange(64) + 32) % 64]
        return dup(vp)

    lng = np.asarray(inputs["L0_ffn_ln_g"])
    lnb = np.asarray(inputs["L0_ffn_ln_b"])
    b1 = np.asarray(inputs["L0_ffn_b1"])
    b2 = np.asarray(inputs["L0_ffn_b2"])
    bo = np.asarray(inputs["L0_attn_bo"])

    def featcol(v):
        return np.ascontiguousarray(
            v.reshape(v.shape[:-1] + (8, 128)).swapaxes(-1, -2))

    b1pad = np.zeros((6, 2, 22 * 128), dtype=np.float32)
    b1pad[:, 0, :DI] = b1[:, :DI]
    b1pad[:, 1, :DI] = b1[:, DI:]
    b1a = np.ascontiguousarray(
        b1pad[:, 0].reshape(6, 22, 128).swapaxes(-1, -2))
    b1g = np.ascontiguousarray(
        b1pad[:, 1].reshape(6, 22, 128).swapaxes(-1, -2))

    lngr = np.ascontiguousarray(lng.reshape(6, 8, 128))
    lnbr = np.ascontiguousarray(lnb.reshape(6, 8, 128))

    common = dict(
        wq=np.asarray(inputs["L0_attn_Wq"]), wk=np.asarray(inputs["L0_attn_Wk"]),
        wv=np.asarray(inputs["L0_attn_Wv"]), wo=np.asarray(inputs["L0_attn_Wo"]),
        bo_t=featcol(bo), wqkv=np.asarray(inputs["L0_ffn_Wqkv"]),
        wo_i=np.asarray(inputs["L0_ffn_Wo"]), w1=np.asarray(inputs["L0_ffn_W1"]),
        w2=np.asarray(inputs["L0_ffn_W2"]),
        qs_m=dup(qs), qs_p=dup_perm(qs), ks_m=dup(ks), ks_p=dup_perm(ks),
        lng_t=featcol(lng), lngr=lngr, lnbr=lnbr,
        b1a=b1a, b1g=b1g, b2_t=featcol(b2),
        p128=p128, masku=masku, maskl=maskl, idn=idn,
        sel2=sel2.astype(BF), sel2t=np.ascontiguousarray(sel2.T).astype(BF),
        selA=np.ascontiguousarray(sel2[0:1]), selB=np.ascontiguousarray(sel2[1:2]),
        qs_r=qs, ks_r=ks, onesrow=onesrow,
    )

    in_maps = []
    for c in range(8):
        b, h = c // 2, c % 2
        xb = x[b]
        xT = np.ascontiguousarray(xb.T)
        blocks = []
        order = list(range(4 * h, 4 * h + 4)) + list(range(4 * (1 - h), 4 * (1 - h) + 4))
        xr = xT.reshape(8, 128, 8, 512)
        for blk in order:
            blocks.append(np.ascontiguousarray(xr[:, :, blk, :].swapaxes(0, 1)))
        xs = np.stack(blocks, 0)
        m = dict(common)
        m["x"] = xs
        m["csq"] = csq[h]; m["ssq"] = ssq[h]; m["csk"] = csk[h]; m["ssk"] = ssk[h]
        m["flag"] = np.full((128, 1), float(h), dtype=np.float32)
        m["hmask"] = np.full((128, 1), 0.0 if h else -1e30, dtype=np.float32)
        in_maps.append(m)
    return in_maps


def unstage_output(results):
    out = np.empty((BATCH, L, D), dtype=np.float32)
    for c, r in enumerate(results):
        b, h = c // 2, c % 2
        o = np.asarray(r["out"])
        xT = o.transpose(1, 0, 2).reshape(D, T)
        out[b, 2048 * h: 2048 * h + 2048, :] = xT.T
    return out


class Rec:

    def __init__(self):
        self.fifo = {e: [] for e in ("t", "v", "s", "g", "y")}
        self.pos = {e: 0 for e in ("t", "v", "s", "g", "y")}
        self.sem_order = []
        self.barrier_dmasems = []


class KB:

    def __init__(self, nc, block, ctx, mode="direct", rec=None):
        self.nc = nc
        self.block = block
        self.ctx = ctx
        self.mode = mode
        self.rec = rec
        self._uid = 0
        self.cnt = {}
        self.sems = {}
        self.hwm = {e: {} for e in ("t", "v", "s", "g", "y")}
        self.last_w = {}
        self.readers = {}
        if mode == "replay":
            self.window = {e: [] for e in ("t", "v", "s", "g", "y")}
            for name in rec.sem_order:
                self.sems[name] = ctx.enter_context(nc.semaphore("sem_" + name))
                self.cnt[name] = 0
        else:
            for name in ("t", "v", "s", "g", "y", "cc"):
                self._mksem(name)

    def _mksem(self, name):
        self.sems[name] = self.ctx.enter_context(self.nc.semaphore("sem_" + name))
        self.cnt[name] = 0
        if self.mode == "record":
            self.rec.sem_order.append(name)

    def dsem(self, name):
        if name not in self.sems:
            self._mksem(name)
        return name

    def _wcompute(self, eng, sem_key, val):
        if val <= 0:
            return None
        h = self.hwm[eng]
        if h.get(sem_key, 0) >= val:
            return None
        h[sem_key] = val
        return (sem_key, val)

    def _emit(self, eng, engobj, pairs):
        if self.mode == "record":
            self.rec.fifo[eng].append(tuple(pairs))
        for sem_key, val in pairs:
            engobj.wait_ge(self.sems[sem_key], val)

    def _replay(self, eng, engobj):
        r = self.rec
        pairs = r.fifo[eng][r.pos[eng]]
        r.pos[eng] += 1
        for sem_key, val in pairs:
            engobj.wait_ge(self.sems[sem_key], val)

    def wflush(self):
        if self.mode != "replay":
            return
        for e in ("t", "v", "s", "g", "y"):
            fns = self.window[e]
            if not fns:
                continue
            self.window[e] = []
            def runner(eng, fns=fns):
                for fn in fns:
                    fn(eng)
            getattr(self.block, {"t": "tensor", "v": "vector", "s": "scalar",
                                 "g": "gpsimd", "y": "sync"}[e])(runner)

    def flush(self):
        assert self.mode == "replay"
        self.wflush()
        for e in ("t", "v", "s", "g", "y"):
            assert self.rec.pos[e] == len(self.rec.fifo[e]), \
                f"replay misalignment on {e}: {self.rec.pos[e]} != {len(self.rec.fifo[e])}"

    def phase_barrier(self):
        self._barn = getattr(self, "_barn", 0) + 1
        k = self._barn
        if "bar" not in self.sems:
            self._mksem("bar")
        bar = self.sems["bar"]
        if self.mode == "replay":
            dma_sems = self.rec.barrier_dmasems[k - 1]
        else:
            dma_sems = [n for n in self.sems
                        if n not in ("t", "v", "s", "g", "y", "bar")]
            if self.mode == "record":
                self.rec.barrier_dmasems.append(list(dma_sems))
        def mk(e):
            def f(eng):
                if e == "y":
                    for snn in dma_sems:
                        self.wait("y", eng, snn, self.cnt[snn])
                eng.drain()
                eng.nop().then_inc(bar, 1)
                eng.wait_ge(bar, 5 * k)
            return f
        for e in ("t", "v", "s", "g", "y"):
            self.sec(e, mk(e))
        self.cnt["bar"] = 5 * k
        for e in self.hwm:
            for snn, cv in self.cnt.items():
                if snn != "bar":
                    self.hwm[e][snn] = cv
            self.hwm[e]["bar"] = 5 * k
        self._pend_psum = []
        if self.mode == "replay":
            self.wflush()

    def psum_close(self, keys):
        if self.mode == "replay":
            self.wflush()
            return
        pend = getattr(self, "_pend_psum", [])
        for k in keys:
            w = self.last_w.pop(k, None)
            if w:
                pend.append(w)
            pend.extend(self.readers.pop(k, []))
        self._pend_psum = pend

    def psum_fence(self, eng, engobj):
        if self.mode == "replay":
            self._replay(eng, engobj)
            return
        pairs = []
        for sem, cnt in getattr(self, "_pend_psum", []):
            p = self._wcompute(eng, sem, cnt)
            if p:
                pairs.append(p)
        self._pend_psum = []
        self._emit(eng, engobj, pairs)

    def uid(self, base):
        self._uid += 1
        return f"{base}_{self._uid}"

    def sec(self, eng, fn):
        if self.mode == "replay":
            if fn.__closure__ is not None:
                cells = tuple(types.CellType(c.cell_contents)
                              for c in fn.__closure__)
                fn = types.FunctionType(fn.__code__, fn.__globals__,
                                        fn.__name__, fn.__defaults__, cells)
            self.window[eng].append(fn)
            return
        getattr(self.block, {"t": "tensor", "v": "vector", "s": "scalar",
                             "g": "gpsimd", "y": "sync"}[eng])(fn)

    def inc(self, eng_key, instr, n=1):
        self.cnt[eng_key] += n
        instr.then_inc(self.sems[eng_key], n)
        return self.cnt[eng_key]

    def wait(self, eng, engobj, sem_key, val):
        if self.mode == "replay":
            self._replay(eng, engobj)
            return
        p = self._wcompute(eng, sem_key, val)
        self._emit(eng, engobj, [p] if p else [])

    def mark_w(self, key, sem, cnt):
        if self.mode == "replay":
            return
        self.last_w[key] = (sem, cnt)
        self.readers[key] = []

    def mark_r(self, key, sem, cnt):
        if self.mode == "replay":
            return
        self.readers.setdefault(key, []).append((sem, cnt))

    def need_r(self, eng, engobj, key):
        if self.mode == "replay":
            self._replay(eng, engobj)
            return
        pairs = []
        w = self.last_w.get(key)
        if w:
            p = self._wcompute(eng, w[0], w[1])
            if p:
                pairs.append(p)
        self._emit(eng, engobj, pairs)

    def need_w(self, eng, engobj, key):
        if self.mode == "replay":
            self._replay(eng, engobj)
            return
        pairs = []
        w = self.last_w.get(key)
        if w:
            p = self._wcompute(eng, w[0], w[1])
            if p:
                pairs.append(p)
        for sem, cnt in self.readers.get(key, []):
            p = self._wcompute(eng, sem, cnt)
            if p:
                pairs.append(p)
        self._emit(eng, engobj, pairs)


def _ceil(a, b):
    return -(-a // b)


def emit_ffn_loads(kb, bf, dram, l):
    def y(eng):
        for dst, src in (
            (bf["lng"][:, :], dram["lng_t"][l, :, :]),
            (bf["b1a"][:, :], dram["b1a"][l, :, :]),
            (bf["b1g"][:, :], dram["b1g"][l, :, :]),
            (bf["b2"][:, :], dram["b2_t"][l, :, :]),
        ):
            key = ("ffnpar",)
            kb.need_w("y", eng, key)
            c = kb.inc(kb.dsem("ffnpar_d"), eng.dma_start(out=dst, in_=src), 16)
        kb.mark_w(("ffnpar",), "ffnpar_d", kb.cnt["ffnpar_d"])
    kb.sec("y", y)

    def g_lnbr(eng):
        kb.need_w("g", eng, ("ffnparb",))
        sn = kb.dsem("ffnparb_d")
        kb.inc(sn, eng.dma_start(
            out=bf["lnbr"][0:1, :, :], in_=dram["lnbr"][l, :, :]), 16)
        c = kb.inc(sn, eng.dma_start(
            out=bf["lngr"][0:1, :, :], in_=dram["lngr"][l, :, :]), 16)
        kb.mark_w(("ffnparb",), sn, c)
    kb.sec("g", g_lnbr)

    def g(eng):
        kb.need_w("g", eng, ("w1",))
        for kc in range(8):
            c = kb.inc(kb.dsem("w1_d"), eng.dma_start(
                out=bf["w1"][:, kc, :], in_=dram["w1"][l, 128 * kc:128 * (kc + 1), :]), 16)
        kb.mark_w(("w1",), "w1_d", kb.cnt["w1_d"])
    kb.sec("g", g)


def emit_ffn(kb, bf, dram, l):
    nc = kb.nc
    emit_bnd_snapshot(kb, bf)

    def mk_shift_sq(tsb):
        col0 = 1 + SUB * tsb

        def v_shift(eng, tsb=tsb, col0=col0):
            kb.need_w("v", eng, ("xsb",))
            for hc in range(8, 16):
                kb.need_w("v", eng, ("h", hc))
            kb.need_r("v", eng, ("xres",))
            eng.tensor_copy(bf["xsb"][:, 8:12, :], bf["xres"][:, 0:4, col0:col0 + 512])
            if tsb == 0:
                i = eng.tensor_copy(
                    bf["xsb"][:, 12:16, :], bf["xres"][:, 4:8, col0 - 1:col0 - 1 + 512])
            else:
                kb.need_r("v", eng, ("xbnd",))
                eng.tensor_copy(bf["xsb"][:, 12:16, 0:1], bf["xbnd"][:, 4:8, tsb - 1:tsb])
                i = eng.tensor_copy(
                    bf["xsb"][:, 12:16, 1:512], bf["xres"][:, 4:8, col0:col0 + 511])
                kb.mark_r(("xbnd",), "v", kb.cnt["v"] + 1)
            c = kb.inc("v", i)
            kb.mark_w(("xsb",), "v", c)
            kb.mark_r(("xres",), "v", c)

        def s_sq(eng):
            kb.need_w("s", eng, ("xsq",))
            for hc in range(8):
                kb.need_w("s", eng, ("h", hc))
            kb.need_r("s", eng, ("xsb",))
            c = kb.inc("s", eng.activation(bf["xsq"][:, 0:8, :], bf["xsb"][:, 8:16, :], AF.Square))
            kb.mark_w(("xsq",), "s", c)
            kb.mark_r(("xsb",), "s", c)

        return v_shift, s_sq

    for tsb in range(NSUB):
        col0 = 1 + SUB * tsb
        with ExitStack() as ps_ctx:
            ps_st = ps_ctx.enter_context(nc.psum_tensor(kb.uid("ps_st"), [128, 512], F32))
            ps_rg = ps_ctx.enter_context(nc.psum_tensor(kb.uid("ps_rg"), [128, 512], F32))
            ps_nm = [ps_ctx.enter_context(nc.psum_tensor(kb.uid("ps_nm"), [128, 512], F32))
                     for _ in range(2)]
            ps_a = [ps_ctx.enter_context(nc.psum_tensor(kb.uid("ps_a"), [128, 512], F32)) for i in range(2)]
            ps_g = [ps_ctx.enter_context(nc.psum_tensor(kb.uid("ps_g"), [128, 512], F32)) for i in range(2)]

            if tsb == 0:
                vs0, sq0 = mk_shift_sq(0)
                kb.sec("v", vs0)
                kb.sec("s", sq0)

            def t_st(eng):
                kb.psum_fence("t", eng)
                kb.need_w("t", eng, ("ps_st",))
                kb.need_r("t", eng, ("xsb",))
                kb.need_r("t", eng, ("xsq",))
                kb.need_r("t", eng, ("ones128",))
                for kc in range(8):
                    eng.matmul(ps_st[0:1, :], bf["ones128"][:, :], bf["xsb"][:, 8 + kc, :],
                               start=(kc == 0), stop=(kc == 7), tile_position=(0, 0))
                for kc in range(8):
                    i = eng.matmul(ps_st[32:33, :], bf["ones128"][:, :], bf["xsq"][:, kc, :],
                                   start=(kc == 0), stop=(kc == 7), tile_position=(0, 32))
                c = kb.inc("t", i)
                kb.mark_w(("ps_st",), "t", c)
                kb.mark_r(("xsb",), "t", c)
                kb.mark_r(("xsq",), "t", c)
            kb.sec("t", t_st)

            st = bf["st"]
            def v_f3a(eng):
                kb.need_r("v", eng, ("ps_st",))
                kb.need_w("v", eng, ("st",))
                eng.tensor_scalar_mul(st[0:1, 0, :], ps_st[0:1, :], 1.0 / 1024.0)
                eng.drain()
                eng.scalar_tensor_tensor(st[0:1, 1, :], st[0:1, 0, :], -1.0, st[0:1, 0, :],
                                         ALU.mult, ALU.mult)
                eng.drain()
                c = kb.inc("v", eng.scalar_tensor_tensor(
                    st[0:1, 2, :], ps_st[32:33, :], 1.0 / 1024.0, st[0:1, 1, :],
                    ALU.mult, ALU.add))
                kb.mark_r(("ps_st",), "v", c)
                kb.mark_w(("st",), "v", c)
            kb.sec("v", v_f3a)

            def s_f3(eng):
                kb.need_r("s", eng, ("st",))
                kb.need_r("s", eng, ("eps",))
                kb.need_w("s", eng, ("st2",))
                eng.activation(st[0:1, 1, :], st[0:1, 2, :], AF.Ln, bias=bf["eps"][0:1, :])
                eng.drain()
                c = kb.inc("s", eng.activation(bf["st2"][0:1, 0, :], st[0:1, 1, :],
                                               AF.Exp, scale=-0.5))
                kb.mark_w(("st",), "s", c)
                kb.mark_w(("st2",), "s", c)
            kb.sec("s", s_f3)

            def v_f3b(eng):
                kb.need_r("v", eng, ("st",))
                kb.need_r("v", eng, ("st2",))
                kb.need_w("v", eng, ("st2b",))
                c = kb.inc("v", eng.scalar_tensor_tensor(
                    bf["st2b"][0:1, 0, :], st[0:1, 0, :], -1.0, bf["st2"][0:1, 0, :],
                    ALU.mult, ALU.mult))
                kb.mark_w(("st2b",), "v", c)
                kb.mark_r(("st2",), "v", c)
            kb.sec("v", v_f3b)

            def t_rg(eng):
                kb.need_r("t", eng, ("st2",))
                kb.need_r("t", eng, ("ones1",))
                kb.need_w("t", eng, ("ps_rg",))
                c = kb.inc("t", eng.matmul(ps_rg[:, :], bf["ones1"][:, :],
                                           bf["st2"][0:1, 0, :], start=True, stop=True))
                kb.mark_w(("ps_rg",), "t", c)
                kb.mark_r(("st2",), "t", c)
            kb.sec("t", t_rg)

            def s_rg(eng):
                kb.need_r("s", eng, ("ps_rg",))
                kb.need_w("s", eng, ("rgs",))
                c = kb.inc("s", eng.activation(bf["rgs"][:, :], ps_rg[:, :], AF.Copy))
                kb.mark_w(("rgs",), "s", c)
                kb.mark_r(("ps_rg",), "s", c)
            kb.sec("s", s_rg)

            for kc in range(8):
                def t_f4(eng, kc=kc):
                    kb.need_r("v", eng, ("st2b",)) if False else None
                    kb.need_r("t", eng, ("st2b",))
                    kb.need_w("t", eng, ("ps_nm", kc % 2))
                    kb.need_r("t", eng, ("ffnpar",))
                    kb.need_r("t", eng, ("ffnparb",))
                    kb.need_r("t", eng, ("onesrow",))
                    eng.matmul(ps_nm[kc % 2][:, :], bf["lngr"][0:1, kc, :],
                               bf["st2b"][0:1, 0, :], start=True, stop=False)
                    c = kb.inc("t", eng.matmul(ps_nm[kc % 2][:, :], bf["lnbr"][0:1, kc, :],
                                               bf["onesrow"][:, :], start=False, stop=True))
                    kb.mark_w(("ps_nm", kc % 2), "t", c)
                    kb.mark_r(("ffnpar",), "t", c)
                    kb.mark_r(("ffnparb",), "t", c)
                    kb.mark_r(("st2b",), "t", c)
                kb.sec("t", t_f4)

                def g_f4(eng, kc=kc):
                    tmpc = bf["h"][:, 2 * (kc % 2):2 * (kc % 2) + 2, :].bitcast(
                        F32).rearrange("p a b -> p (a b)")
                    kb.need_r("g", eng, ("rgs",))
                    kb.need_r("g", eng, ("xsb",))
                    kb.need_w("g", eng, ("tmpc", kc % 2))
                    if kc < 2:
                        for hc in range(4):
                            kb.need_w("g", eng, ("h", hc))
                        kb.need_w("g", eng, ("xsq",))
                    c = kb.inc("g", eng.tensor_mul(tmpc, bf["xsb"][:, 8 + kc, :], bf["rgs"][:, :]))
                    kb.mark_w(("tmpc", kc % 2), "g", c)
                    kb.mark_r(("xsb",), "g", c)
                    kb.mark_r(("rgs",), "g", c)
                kb.sec("g", g_f4)

                def v_f4(eng, kc=kc):
                    tmpc = bf["h"][:, 2 * (kc % 2):2 * (kc % 2) + 2, :].bitcast(
                        F32).rearrange("p a b -> p (a b)")
                    kb.need_r("v", eng, ("tmpc", kc % 2))
                    kb.need_r("v", eng, ("ps_nm", kc % 2))
                    kb.need_r("v", eng, ("ffnpar",))
                    c = kb.inc("v", eng.scalar_tensor_tensor(
                        bf["xn"][:, kc, :], tmpc, bf["lng"][:, kc:kc + 1],
                        ps_nm[kc % 2][:, :], ALU.mult, ALU.add))
                    kb.mark_r(("tmpc", kc % 2), "v", c)
                    kb.mark_r(("ps_nm", kc % 2), "v", c)
                    kb.mark_w(("xn", kc), "v", c)
                kb.sec("v", v_f4)

            for c_i in range(NC_DI):
                cw = CW_LAST if c_i == NC_DI - 1 else 128
                def t_f5(eng, c_i=c_i, cw=cw):
                    kb.need_r("t", eng, ("w1",))
                    for kc in range(8):
                        kb.need_r("t", eng, ("xn", kc))
                    kb.need_w("t", eng, ("ps_a", c_i % 2))
                    kb.need_w("t", eng, ("ps_g", c_i % 2))
                    for kc in range(8):
                        eng.matmul(ps_a[c_i % 2][0:cw, :],
                                   bf["w1"][:, kc, 128 * c_i:128 * c_i + cw],
                                   bf["xn"][:, kc, :], start=(kc == 0), stop=(kc == 7))
                    for kc in range(8):
                        i = eng.matmul(ps_g[c_i % 2][0:cw, :],
                                       bf["w1"][:, kc, DI + 128 * c_i:DI + 128 * c_i + cw],
                                       bf["xn"][:, kc, :], start=(kc == 0), stop=(kc == 7))
                    c = kb.inc("t", i)
                    kb.mark_w(("ps_a", c_i % 2), "t", c)
                    kb.mark_w(("ps_g", c_i % 2), "t", c)
                    kb.mark_r(("w1",), "t", c)
                kb.sec("t", t_f5)

                def s_f5(eng, c_i=c_i, cw=cw):
                    kb.need_r("s", eng, ("ps_g", c_i % 2))
                    kb.need_w("s", eng, ("gsb", c_i % 2))
                    kb.need_r("s", eng, ("ffnpar",))
                    c = kb.inc("s", eng.activation(
                        bf["gsb"][0:cw, c_i % 2, :], ps_g[c_i % 2][0:cw, :], AF.Gelu,
                        bias=bf["b1g"][0:cw, c_i:c_i + 1]))
                    kb.mark_w(("gsb", c_i % 2), "s", c)
                    kb.mark_r(("ps_g", c_i % 2), "s", c)
                    kb.mark_r(("ffnpar",), "s", c)
                kb.sec("s", s_f5)

                def v_f5(eng, c_i=c_i, cw=cw):
                    if c_i < 2:
                        kb.need_w("v", eng, ("tmpc",))
                    if c_i < 8:
                        kb.need_w("v", eng, ("xsq",))
                    if 8 <= c_i < 16:
                        kb.need_w("v", eng, ("xsb",))
                    kb.need_r("v", eng, ("ps_a", c_i % 2))
                    kb.need_r("v", eng, ("gsb", c_i % 2))
                    kb.need_w("v", eng, ("h", c_i))
                    c = kb.inc("v", eng.scalar_tensor_tensor(
                        bf["h"][0:cw, c_i, :], ps_a[c_i % 2][0:cw, :],
                        bf["b1a"][0:cw, c_i:c_i + 1], bf["gsb"][0:cw, c_i % 2, :],
                        ALU.add, ALU.mult))
                    kb.mark_w(("h", c_i), "v", c)
                    kb.mark_r(("ps_a", c_i % 2), "v", c)
                    kb.mark_r(("gsb", c_i % 2), "v", c)
                kb.sec("v", v_f5)

        kb.psum_close([("ps_st",), ("ps_rg",), ("ps_nm", 0), ("ps_nm", 1),
                       ("ps_a", 0), ("ps_a", 1), ("ps_g", 0), ("ps_g", 1)])
        with ExitStack() as ps_ctx:
            ps_y = ps_ctx.enter_context(nc.psum_tensor(kb.uid("ps_y"), [128, 8, 512], F32))

            def t_fence(eng):
                kb.psum_fence("t", eng)
            kb.sec("t", t_fence)

            def g_w2(c_i):
                cw = CW_LAST if c_i == NC_DI - 1 else 128
                def g(eng):
                    kb.need_w("g", eng, ("w2", c_i % 2))
                    sname = kb.dsem(f"w2_d{c_i % 2}")
                    c = kb.inc(sname, eng.dma_start(
                        out=bf["w2"][0:cw, c_i % 2, :],
                        in_=dram["w2"][l, 128 * c_i:128 * c_i + cw, :]), 16)
                    kb.mark_w(("w2", c_i % 2), sname, c)
                kb.sec("g", g)

            def t_w2(c_i):
                cw = CW_LAST if c_i == NC_DI - 1 else 128
                def t(eng):
                    kb.need_r("t", eng, ("w2", c_i % 2))
                    kb.need_r("t", eng, ("h", c_i))
                    if c_i == 0:
                        kb.need_w("t", eng, ("ps_y",))
                    for dc in range(8):
                        i = eng.matmul(ps_y[:, dc, :],
                                       bf["w2"][0:cw, c_i % 2, 128 * dc:128 * (dc + 1)],
                                       bf["h"][0:cw, c_i, :],
                                       start=(c_i == 0), stop=(c_i == NC_DI - 1))
                    c = kb.inc("t", i)
                    kb.mark_r(("w2", c_i % 2), "t", c)
                    kb.mark_r(("h", c_i), "t", c)
                    if c_i == NC_DI - 1:
                        kb.mark_w(("ps_y",), "t", c)
                kb.sec("t", t)

            g_w2(0)
            g_w2(1)
            for c_i in range(NC_DI):
                t_w2(c_i)
                if c_i + 2 < NC_DI:
                    g_w2(c_i + 2)
                if c_i == 15 and tsb + 1 < NSUB:
                    vs1, sq1 = mk_shift_sq(tsb + 1)
                    kb.sec("v", vs1)
                    kb.sec("s", sq1)

            def v_f6(eng):
                kb.need_r("v", eng, ("ps_y",))
                kb.need_w("v", eng, ("xres",))
                for dc in range(8):
                    i = eng.scalar_tensor_tensor(
                        bf["xres"][:, dc, col0:col0 + 512], ps_y[:, dc, :],
                        bf["b2"][:, dc:dc + 1], bf["xres"][:, dc, col0:col0 + 512],
                        ALU.add, ALU.add)
                c = kb.inc("v", i)
                kb.mark_w(("xres",), "v", c)
                kb.mark_r(("ps_y",), "v", c)
            kb.sec("v", v_f6)
        kb.psum_close([("ps_y",)])


def alloc_common_bufs(nc, ctx):
    bf = {}
    A = lambda name, shape, dt: ctx.enter_context(nc.sbuf_tensor(_uname(name), shape, dt))
    bf["xres"] = A("xres", [128, 8, T + 1], F32)
    bf["st"] = A("st", [1, 3, 512], F32)
    bf["ones128"] = A("ones128", [128, 1], BF16)
    bf["onesrow"] = A("onesrow", [1, 512], BF16)
    bf["eps"] = A("eps", [1, 1], F32)
    return bf


def alloc_ffn_bufs(nc, ctx, bf):
    A = lambda name, shape, dt: ctx.enter_context(nc.sbuf_tensor(_uname(name), shape, dt))
    bf["w1"] = A("w1", [128, 8, 5460], BF16)
    bf["w2"] = A("w2", [128, 2, 1024], BF16)
    bf["xn"] = A("xn", [128, 8, 512], BF16)
    bf["gsb"] = A("gsb", [128, 2, 512], BF16)
    bf["h"] = A("h", [128, NC_DI, 512], BF16)
    bf["xsq"] = bf["h"]
    bf["xsb"] = bf["h"]
    bf["tmpc"] = None
    bf["st2"] = A("st2", [1, 2, 512], F32)
    bf["lngr"] = A("lngr", [1, 8, 128], BF16)
    bf["lnbr"] = A("lnbr", [1, 8, 128], BF16)
    bf["lng"] = A("lng", [128, 8], F32)
    bf["st2b"] = A("st2b", [1, 1, 512], BF16)
    bf["rgs"] = A("rgs", [128, 512], F32)
    bf["b1a"] = A("b1a", [128, NC_DI], F32)
    bf["b1g"] = A("b1g", [128, NC_DI], F32)
    bf["b2"] = A("b2", [128, 8], F32)
    return bf


def emit_init_consts(kb, bf):
    def g_init(eng):
        c = kb.inc("g", eng.memset(bf["ones128"][:, :], 1.0))
        kb.mark_w(("ones128",), "g", c)
    kb.sec("g", g_init)

    def v_init(eng):
        eng.memset(bf["onesrow"][:, :], 1.0)
        if "ones1" in bf:
            eng.memset(bf["ones1"][:, :], 1.0)
            kb.mark_w(("ones1",), "v", kb.cnt["v"] + 1)
        c = kb.inc("v", eng.memset(bf["eps"][:, :], 1e-5))
        kb.mark_w(("onesrow",), "v", c)
        kb.mark_w(("eps",), "v", c)
    kb.sec("v", v_init)


def build_ffn_test():
    nc = bass.Bass("TRN2", target_bir_lowering=False, debug=False)
    dram = {}
    dram["w1"] = nc.dram_tensor("w1", [6, 1024, 5460], F32, kind="ExternalInput")
    dram["w2"] = nc.dram_tensor("w2", [6, DI, 1024], F32, kind="ExternalInput")
    dram["lngr"] = nc.dram_tensor("lngr", [6, 8, 128], F32, kind="ExternalInput")
    dram["lng_t"] = nc.dram_tensor("lng_t", [6, 128, 8], F32, kind="ExternalInput")
    dram["lnbr"] = nc.dram_tensor("lnbr", [6, 8, 128], F32, kind="ExternalInput")
    dram["b1a"] = nc.dram_tensor("b1a", [6, 128, NC_DI], F32, kind="ExternalInput")
    dram["b1g"] = nc.dram_tensor("b1g", [6, 128, NC_DI], F32, kind="ExternalInput")
    dram["b2_t"] = nc.dram_tensor("b2_t", [6, 128, 8], F32, kind="ExternalInput")
    x_in = nc.dram_tensor("x_in", [128, 8, T + 1], F32, kind="ExternalInput")
    out = nc.dram_tensor("out", [128, 8, T + 1], F32, kind="ExternalOutput")

    with ExitStack() as ctx:
        block = ctx.enter_context(nc.Block())
        kb = KB(nc, block, ctx)
        bf = alloc_common_bufs(nc, ctx)
        bf["xbnd"] = ctx.enter_context(nc.sbuf_tensor(_uname("xbnd"), [128, 8, 3], F32))
        bf["ones1"] = ctx.enter_context(nc.sbuf_tensor(_uname("ones1"), [1, 128], F32))
        alloc_ffn_bufs(nc, ctx, bf)
        emit_init_consts(kb, bf)

        def y_in(eng):
            c = kb.inc(kb.dsem("xin_d"), eng.dma_start(out=bf["xres"][:, :, :], in_=x_in[:, :, :]), 16)
            kb.mark_w(("xres",), "xin_d", c)
        kb.sec("y", y_in)

        emit_ffn_loads(kb, bf, dram, 0)
        emit_ffn(kb, bf, dram, 0)

        def y_out(eng):
            kb.need_r("y", eng, ("xres",))
            sn = kb.dsem("out_d")
            kb.inc(sn, eng.dma_start(out=out[:, :, :], in_=bf["xres"][:, :, :]), 16)
            kb.wait("y", eng, sn, kb.cnt[sn])
        kb.sec("y", y_out)
    return nc


def alloc_persist_bufs(nc, ctx, bf):
    A = lambda name, shape, dt: ctx.enter_context(nc.sbuf_tensor(_uname(name), shape, dt))
    bf["xrow"] = A("xrow", [128, 8], F32)
    bf["xrow2"] = A("xrow2", [128, 8], F32)
    bf["qsl"] = A("qsl", [128, 4], F32)
    bf["negB"] = A("negB", [128, 6], F32)
    bf["p128"] = A("p128", [128, 128], BF16)
    bf["idn"] = A("idn", [128, 128], BF16)
    bf["masku"] = A("masku", [128, 128], BF16)
    bf["maskl"] = A("maskl", [128, 128], BF16)
    bf["sel2t"] = A("sel2t", [128, 2], BF16)
    bf["sel2"] = A("sel2", [2, 128], BF16)
    bf["selA"] = A("selA", [1, 128], F32)
    bf["selB"] = A("selB", [1, 128], F32)
    bf["flag"] = A("flag", [128, 1], F32)
    bf["hmask"] = A("hmask", [128, 1], F32)
    bf["ones1"] = A("ones1", [1, 128], F32)
    bf["qsraw"] = A("qsraw", [1, 2, 64], F32)
    bf["xbnd"] = A("xbnd", [128, 8, 3], F32)
    return bf


def emit_bnd_snapshot(kb, bf):
    def v(eng):
        kb.need_r("v", eng, ("xres",))
        kb.need_w("v", eng, ("xbnd",))
        c = kb.inc("v", eng.tensor_copy(bf["xbnd"][:, :, :],
                                        bf["xres"][:, :, 512:1537:512]))
        kb.mark_w(("xbnd",), "v", c)
        kb.mark_r(("xres",), "v", c)
    kb.sec("v", v)


def alloc_attn_bufs(nc, ctx, bf):
    A = lambda name, shape, dt: ctx.enter_context(nc.sbuf_tensor(_uname(name), shape, dt))
    bf["wqg"] = A("wqg", [128, 8, 1024], BF16)
    bf["wkg"] = A("wkg", [128, 8, 1024], BF16)
    bf["wvg"] = A("wvg", [128, 8, 1024], BF16)
    bf["wog"] = A("wog", [128, 8, 1024], BF16)
    bf["krr"] = A("krr", [128, 8, RING * 128], BF16)
    bf["vrr"] = A("vrr", [128, RING, NH * 65], BF16)
    bf["qr"] = A("qr", [128, 8, 512], BF16)
    bf["xsba"] = A("xsba", [128, 8, 512], BF16)
    bf["avun"] = bf["xsba"]
    bf["kqbf"] = A("kqbf", [128, 512], BF16)
    bf["avbf"] = A("avbf", [128, 512], BF16)
    bf["qsq"] = A("qsq", [128, 512], BF16)
    bf["t1"] = A("t1", [128, 512], F32)
    bf["t2"] = A("t2", [128, 512], F32)
    bf["t3"] = A("t3", [128, 512], F32)
    bf["rqa"] = A("rqa", [2, 512], BF16)
    bf["rqb"] = A("rqb", [2, 512], F32)
    bf["eps30"] = A("eps30", [2, 1], F32)
    bf["recdA"] = A("recdA", [1, 512], F32)
    bf["recdB"] = A("recdB", [1, 512], F32)
    bf["expt"] = A("expt", [128, 2, 1536], BF16)
    bf["tab"] = A("tab", [128, 2, 512], F32)
    return bf


def emit_attn_consts(kb, bf, dram):
    def y(eng):
        sn = kb.dsem("const_d")
        for dst, src in (
            (bf["p128"][:, :], dram["p128"][:, :]),
            (bf["idn"][:, :], dram["idn"][:, :]),
            (bf["masku"][:, :], dram["masku"][:, :]),
            (bf["maskl"][:, :], dram["maskl"][:, :]),
            (bf["sel2t"][:, :], dram["sel2t"][:, :]),
            (bf["sel2"][:, :], dram["sel2"][:, :]),
            (bf["selA"][:, :], dram["selA"][:, :]),
            (bf["selB"][:, :], dram["selB"][:, :]),
            (bf["flag"][:, :], dram["flag"][:, :]),
            (bf["hmask"][:, :], dram["hmask"][:, :]),
        ):
            c = kb.inc(sn, eng.dma_start(out=dst, in_=src), 16)
        kb.mark_w(("consts",), sn, kb.cnt[sn])
    kb.sec("y", y)



def emit_vrr_init(kb, bf):
    def v(eng):
        kb.need_w("v", eng, ("eps30",))
        eng.memset(bf["eps30"][:, :], 1e-30)
        kb.mark_w(("eps30",), "v", kb.cnt["v"] + 1)
        for s in range(RING):
            for half in range(2):
                kb.need_w("v", eng, ("vrs", s, half))
        c = kb.inc("v", eng.memset(bf["vrr"][:, :, :], 1.0))
        for s in range(RING):
            for half in range(2):
                kb.mark_w(("vrs", s, half), "v", c)
    kb.sec("v", v)


def emit_negB(kb, bf, dram, mscq, msck):
    for l in range(DEPTH):
        def y(eng, l=l):
            sn = kb.dsem("qsl_d")
            kb.need_w("y", eng, ("qsraw",))
            kb.inc(sn, eng.dma_start(out=bf["qsraw"][0:1, 0, :], in_=dram["qs_r"][l:l + 1, :]), 16)
            c = kb.inc(sn, eng.dma_start(out=bf["qsraw"][0:1, 1, :], in_=dram["ks_r"][l:l + 1, :]), 16)
            kb.mark_w(("qsraw",), sn, c)
        kb.sec("y", y)

        def v(eng, l=l):
            kb.need_r("v", eng, ("qsraw",))
            eng.tensor_reduce(bf["st"][0:1, 0, 0:1], bf["qsraw"][0:1, 0, :],
                              mybir.AxisListType.X, ALU.max, apply_absolute_value=True)
            eng.tensor_reduce(bf["st"][0:1, 0, 1:2], bf["qsraw"][0:1, 1, :],
                              mybir.AxisListType.X, ALU.max, apply_absolute_value=True)
            eng.drain()
            c = kb.inc("v", eng.scalar_tensor_tensor(
                bf["st"][0:1, 0, 2:3], bf["st"][0:1, 0, 0:1], -8.0 * mscq * msck,
                bf["st"][0:1, 0, 1:2], ALU.mult, ALU.mult))
            kb.mark_w(("st",), "v", c)
            kb.mark_r(("qsraw",), "v", c)
        kb.sec("v", v)

        def t(eng, l=l):
            kb.need_r("t", eng, ("st",))
            kb.need_r("t", eng, ("ones1",))
            kb.need_w("t", eng, ("ps_nb",))
            c = kb.inc("t", eng.matmul(bf["ps_nb"][:, 0:1], bf["ones1"][:, :],
                                       bf["st"][0:1, 0, 2:3], start=True, stop=True))
            kb.mark_w(("ps_nb",), "t", c)
            kb.mark_r(("st",), "t", c)
        kb.sec("t", t)

        def v2(eng, l=l):
            kb.need_r("v", eng, ("ps_nb",))
            c = kb.inc("v", eng.tensor_copy(bf["negB"][:, l:l + 1], bf["ps_nb"][:, 0:1]))
            kb.mark_w(("negB",), "v", c)
            kb.mark_r(("ps_nb",), "v", c)
        kb.sec("v", v2)


def _ring_runs(kcols):
    runs = []
    i = 0
    while i < len(kcols):
        j = i
        while j + 1 < len(kcols) and (kcols[j + 1] % RING) == (kcols[j] % RING) + 1:
            j += 1
        runs.append((i, j - i + 1, kcols[i] % RING))
        i = j + 1
    return runs


def emit_attn_loads(kb, bf, dram, l):
    for gname, key, coff in (("wqg", "wq_g", 0), ("wkg", "wk_g", 1024),
                             ("wvg", "wv_g", 2048)):
        def g(eng, gname=gname, key=key, coff=coff):
            kb.need_w("g", eng, (key,))
            sn = kb.dsem(key + "_d")
            for kc in range(8):
                c = kb.inc(sn, eng.dma_start(
                    out=bf[gname][:, kc, :],
                    in_=dram["wqkv"][l, 128 * kc:128 * (kc + 1), coff:coff + 1024]), 16)
            kb.mark_w((key,), sn, kb.cnt[sn])
        kb.sec("g", g)

    def g_wo(eng):
        kb.need_w("g", eng, ("wo_g",))
        sn = kb.dsem("wo_g_d")
        for kc in range(8):
            c = kb.inc(sn, eng.dma_start(
                out=bf["wog"][:, kc, :],
                in_=dram["wo_i"][l, 128 * kc:128 * (kc + 1), :]), 16)
        kb.mark_w(("wo_g",), sn, kb.cnt[sn])
    kb.sec("g", g_wo)

    def y_qsl(eng):
        kb.need_w("y", eng, ("qsl",))
        sn = kb.dsem("qsl2_d")
        for i, nm in enumerate(("qs_m", "qs_p", "ks_m", "ks_p")):
            c = kb.inc(sn, eng.dma_start(out=bf["qsl"][:, i:i + 1], in_=dram[nm][l, :, :]), 16)
        kb.mark_w(("qsl",), sn, kb.cnt[sn])
    kb.sec("y", y_qsl)


def emit_tab_load(kb, bf, dram, kind, toff, w):
    t0, t1n = ("csq", "ssq") if kind == "q" else ("csk", "ssk")
    def y(eng):
        kb.need_w("y", eng, ("tab",))
        sn = kb.dsem("tab_d")
        kb.inc(sn, eng.dma_start(out=bf["tab"][:, 0, 0:w], in_=dram[t0][:, toff:toff + w]), 16)
        c = kb.inc(sn, eng.dma_start(out=bf["tab"][:, 1, 0:w], in_=dram[t1n][:, toff:toff + w]), 16)
        kb.mark_w(("tab",), sn, c)
    kb.sec("y", y)


def emit_shift_cast_attn(kb, bf, toff, w, bnd=None):
    def v(eng):
        kb.need_w("v", eng, ("xsba",))
        kb.need_r("v", eng, ("xres",))
        eng.tensor_copy(bf["xsba"][:, 0:4, 0:w], bf["xres"][:, 0:4, 1 + toff:1 + toff + w])
        if bnd is None:
            i = eng.tensor_copy(bf["xsba"][:, 4:8, 0:w], bf["xres"][:, 4:8, toff:toff + w])
        else:
            kb.need_r("v", eng, ("xbnd",))
            eng.tensor_copy(bf["xsba"][:, 4:8, 0:1], bf["xbnd"][:, 4:8, bnd:bnd + 1])
            i = eng.tensor_copy(bf["xsba"][:, 4:8, 1:w],
                                bf["xres"][:, 4:8, toff + 1:toff + w])
            kb.mark_r(("xbnd",), "v", kb.cnt["v"] + 1)
        c = kb.inc("v", i)
        kb.mark_w(("xsba",), "v", c)
        kb.mark_r(("xres",), "v", c)
    kb.sec("v", v)


def emit_proj_postproc(kb, bf, kind, l, toff, w, kcols, ps):
    wg = bf["wqg"] if kind == "q" else bf["wkg"]
    wkey = "wq_g" if kind == "q" else "wk_g"
    tab = bf["tab"]
    tabkey = "tab"
    s_m, s_p = (0, 1) if kind == "q" else (2, 3)
    runs = _ring_runs(kcols) if kind == "k" else None

    P, VP, R, SQ, RB, VR, GA, VR2 = [], [], [], [], [], [], [], []
    for mc in range(8):
        par = mc % 2
        pp = ps["pp"][par]
        ppk = ("pp", par)
        rot, rotk = ps["rot"][par], ("psrot", par)
        ss, ssk = ps["ss"][par], ("psss", par)
        rb, rbk = ps["rb"][par], ("psrb", par)

        def t_proj(eng, mc=mc, pp=pp, ppk=ppk):
            kb.psum_fence("t", eng)
            kb.need_r("t", eng, (wkey,))
            kb.need_r("t", eng, ("xsba",))
            kb.need_w("t", eng, ppk)
            for kc in range(8):
                i = eng.matmul(pp[:, 0:w], wg[:, kc, 128 * mc:128 * (mc + 1)],
                               bf["xsba"][:, kc, 0:w], start=(kc == 0), stop=(kc == 7))
            c = kb.inc("t", i)
            kb.mark_w(ppk, "t", c)
            kb.mark_r((wkey,), "t", c)
            kb.mark_r(("xsba",), "t", c)
        P.append(t_proj)

        def v_pp(eng, pp=pp, ppk=ppk):
            kb.need_r("v", eng, ppk)
            kb.need_w("v", eng, ("kqbf",))
            kb.need_w("v", eng, ("qsq",))
            eng.tensor_copy(bf["kqbf"][:, 0:w], pp[:, 0:w])
            eng.drain()
            c = kb.inc("v", eng.scalar_tensor_tensor(
                bf["qsq"][:, 0:w], pp[:, 0:w], 1.0, bf["kqbf"][:, 0:w],
                ALU.mult, ALU.mult))
            kb.mark_w(("kqbf",), "v", c)
            kb.mark_w(("qsq",), "v", c)
            kb.mark_r(ppk, "v", c)
        VP.append(v_pp)

        def t_rot(eng, rot=rot, rotk=rotk, ss=ss, ssk=ssk):
            kb.need_r("t", eng, ("kqbf",))
            kb.need_r("t", eng, ("qsq",))
            kb.need_r("t", eng, ("consts",))
            kb.need_w("t", eng, rotk)
            kb.need_w("t", eng, ssk)
            eng.matmul(rot[:, 0:w], bf["p128"][:, :], bf["kqbf"][:, 0:w],
                       start=True, stop=True)
            c = kb.inc("t", eng.matmul(ss[0:2, 0:w], bf["sel2t"][:, :],
                                       bf["qsq"][:, 0:w], start=True, stop=True))
            kb.mark_w(rotk, "t", c)
            kb.mark_w(ssk, "t", c)
            kb.mark_r(("kqbf",), "t", c)
            kb.mark_r(("qsq",), "t", c)
        R.append(t_rot)

        def s_rq(eng, ss=ss, ssk=ssk):
            kb.need_r("s", eng, ssk)
            kb.need_w("s", eng, ("rqa",))
            kb.need_w("s", eng, ("rqb",))
            kb.need_r("s", eng, ("eps30",))
            eng.activation(bf["rqb"][0:2, 0:w], ss[0:2, 0:w], AF.Ln,
                           bias=bf["eps30"][0:2, :])
            eng.drain()
            c = kb.inc("s", eng.activation(bf["rqa"][0:2, 0:w], bf["rqb"][0:2, 0:w],
                                           AF.Exp, scale=-0.5))
            kb.mark_w(("rqa",), "s", c)
            kb.mark_w(("rqb",), "s", c)
            kb.mark_r(ssk, "s", c)
        SQ.append(s_rq)

        def t_rb(eng, rb=rb, rbk=rbk):
            kb.need_r("t", eng, ("rqa",))
            kb.need_r("t", eng, ("consts",))
            kb.need_w("t", eng, rbk)
            c = kb.inc("t", eng.matmul(rb[:, 0:w], bf["sel2"][:, :],
                                       bf["rqa"][0:2, 0:w], start=True, stop=True))
            kb.mark_w(rbk, "t", c)
            kb.mark_r(("rqa",), "t", c)
        RB.append(t_rb)

        def v_rt(eng, mc=mc, pp=pp, ppk=ppk, rot=rot, rotk=rotk):
            kb.need_r("v", eng, ppk)
            kb.need_r("v", eng, rotk)
            kb.need_r("v", eng, (tabkey,))
            kb.need_r("v", eng, ("qsl",))
            kb.need_w("v", eng, ("t1",))
            kb.need_w("v", eng, ("t2",))
            i = eng.scalar_tensor_tensor(bf["t1"][:, 0:w], pp[:, 0:w],
                                         bf["qsl"][:, s_m:s_m + 1], tab[:, 0, 0:w],
                                         ALU.mult, ALU.mult)
            c12 = kb.inc("v", eng.scalar_tensor_tensor(
                bf["t2"][:, 0:w], rot[:, 0:w],
                bf["qsl"][:, s_p:s_p + 1], tab[:, 1, 0:w], ALU.mult, ALU.mult))
            kb.mark_w(("t1",), "v", c12)
            kb.mark_w(("t2",), "v", c12)
            kb.mark_r(ppk, "v", c12)
            kb.mark_r(rotk, "v", c12)
            kb.mark_r((tabkey,), "v", c12)
        VR.append(v_rt)

        def g_add(eng, mc=mc):
            kb.need_r("g", eng, ("t1",))
            kb.need_r("g", eng, ("t2",))
            kb.need_w("g", eng, ("t3",))
            c = kb.inc("g", eng.tensor_add(bf["t3"][:, 0:w], bf["t1"][:, 0:w], bf["t2"][:, 0:w]))
            kb.mark_w(("t3",), "g", c)
            kb.mark_r(("t1",), "g", c)
            kb.mark_r(("t2",), "g", c)
        GA.append(g_add)

        def v_rt2(eng, mc=mc, rb=rb, rbk=rbk):
            kb.need_r("v", eng, ("t3",))
            kb.need_r("v", eng, rbk)
            if kind == "q":
                i = eng.tensor_tensor(bf["qr"][:, mc, 0:w], bf["t3"][:, 0:w],
                                      rb[:, 0:w], ALU.mult)
                c = kb.inc("v", i)
                kb.mark_w(("qr",), "v", c)
            else:
                for kcol in kcols:
                    kb.need_w("v", eng, ("krs", kcol % RING, mc))
                for (toff128, n, slot0) in runs:
                    i = eng.tensor_tensor(
                        bf["krr"][:, mc, 128 * slot0:128 * (slot0 + n)],
                        bf["t3"][:, 128 * toff128:128 * (toff128 + n)],
                        rb[:, 128 * toff128:128 * (toff128 + n)], ALU.mult)
                c = kb.inc("v", i)
                for kcol in kcols:
                    kb.mark_w(("krs", kcol % RING, mc), "v", c)
            kb.mark_r(rbk, "v", c)
            kb.mark_r(("t3",), "v", c)
        VR2.append(v_rt2)

    n8 = 8
    for it in range(n8 + 2):
        if it < n8:
            kb.sec("t", P[it])
        if 0 <= it - 1 < n8:
            kb.sec("t", R[it - 1])
        if 0 <= it - 2 < n8:
            kb.sec("t", RB[it - 2])
        if 0 <= it - 1 < n8:
            kb.sec("s", SQ[it - 1])
        if it < n8:
            kb.sec("v", VP[it])
        if 0 <= it - 1 < n8:
            kb.sec("v", VR[it - 1])
        if 0 <= it - 2 < n8:
            kb.sec("v", VR2[it - 2])
        if 0 <= it - 1 < n8:
            kb.sec("g", GA[it - 1])


def emit_v_block(kb, bf, toff, w, kcols, ps):
    for ti, kcol in enumerate(kcols):
        slot = kcol % RING
        for half in range(2):
            pp = ps["pp"][(2 * ti + half) % 2]
            ppk = ("pp", (2 * ti + half) % 2)

            def t_v(eng, ti=ti, half=half, pp=pp, ppk=ppk):
                kb.need_r("t", eng, ("wv_g",))
                kb.need_r("t", eng, ("xsba",))
                kb.need_w("t", eng, ppk)
                for kc in range(8):
                    i = eng.matmul(pp[:, :], bf["xsba"][:, kc, 128 * ti:128 * (ti + 1)],
                                   bf["wvg"][:, kc, 512 * half:512 * (half + 1)],
                                   start=(kc == 0), stop=(kc == 7))
                c = kb.inc("t", i)
                kb.mark_w(ppk, "t", c)
                kb.mark_r(("wv_g",), "t", c)
                kb.mark_r(("xsba",), "t", c)
            kb.sec("t", t_v)

            def v_v(eng, slot=slot, half=half, pp=pp, ppk=ppk):
                kb.need_r("v", eng, ppk)
                kb.need_w("v", eng, ("vrs", slot, half))
                dst = bf["vrr"][:, slot, :].rearrange(
                    "p (h c) -> p h c", c=65)[:, 8 * half:8 * (half + 1), 0:64]
                src = pp[:, :].rearrange("p (h c) -> p h c", c=64)
                c = kb.inc("v", eng.tensor_copy(dst, src))
                kb.mark_w(("vrs", slot, half), "v", c)
                kb.mark_r(ppk, "v", c)
            kb.sec("v", v_v)


def emit_attn_core(kb, bf, l, S, ps):
    for hp in range(8):
        def t_sim(eng, hp=hp):
            kb.psum_fence("t", eng)
            kb.need_w("t", eng, ("psrb2",))
            kb.need_r("t", eng, ("qr",))
            kb.need_r("t", eng, ("consts",))
            kb.need_w("t", eng, ("pssim",))
            for Irel in range(4):
                I = 4 * S + Irel
                for Jrel in range(3):
                    kcol = I + Jrel
                    slot = kcol % RING
                    kb.need_r("t", eng, ("krs", slot, hp))
                    off = 384 * Irel + 128 * Jrel
                    masked = Jrel != 1
                    eng.matmul(ps["sim"][:, off:off + 128],
                               bf["krr"][0:64, hp, 128 * slot:128 * (slot + 1)],
                               bf["qr"][0:64, hp, 128 * Irel:128 * (Irel + 1)],
                               start=True, stop=not masked, tile_position=(0, 0))
                    eng.matmul(ps["sim"][:, 1536 + off:1536 + off + 128],
                               bf["krr"][64:128, hp, 128 * slot:128 * (slot + 1)],
                               bf["qr"][64:128, hp, 128 * Irel:128 * (Irel + 1)],
                               start=True, stop=not masked, tile_position=(64, 0))
                    if masked:
                        m = bf["masku"] if Jrel == 0 else bf["maskl"]
                        eng.matmul(ps["sim"][:, off:off + 128], bf["idn"][:, :],
                                   m[:, :], start=False, stop=True)
                        i = eng.matmul(ps["sim"][:, 1536 + off:1536 + off + 128],
                                       bf["idn"][:, :], m[:, :], start=False, stop=True)
                    else:
                        i = None
            c = kb.inc("t", eng.nop() if i is None else i)
            kb.mark_w(("pssim",), "t", c)
            kb.mark_r(("qr",), "t", c)
        kb.sec("t", t_sim)

        if S == 0:
            def v_flag(eng, hp=hp):
                kb.need_r("v", eng, ("pssim",))
                kb.need_r("v", eng, ("consts",))
                i = None
                for (Irel, Jrel) in ((0, 0), (0, 1), (1, 0)):
                    off = 384 * Irel + 128 * Jrel
                    for par in range(2):
                        sl = ps["sim"][:, 1536 * par + off:1536 * par + off + 128]
                        i = eng.tensor_scalar_add(sl, sl, bf["hmask"][:, 0:1])
                c = kb.inc("v", i)
                kb.mark_w(("pssim",), "v", c)
            kb.sec("v", v_flag)

        def s_exp(eng, hp=hp):
            kb.need_r("s", eng, ("pssim",))
            kb.need_r("s", eng, ("negB",))
            kb.need_w("s", eng, ("expt",))
            eng.activation(bf["expt"][:, 0, :], ps["sim"][:, 0:1536], AF.Exp,
                           bias=bf["negB"][:, l:l + 1])
            c = kb.inc("s", eng.activation(bf["expt"][:, 1, :], ps["sim"][:, 1536:3072],
                                           AF.Exp, bias=bf["negB"][:, l:l + 1]))
            kb.mark_w(("expt",), "s", c)
            kb.mark_r(("pssim",), "s", c)
        kb.sec("s", s_exp)

        def t_av(eng, hp=hp):
            kb.need_r("t", eng, ("expt",))
            kb.need_w("t", eng, ("psavA",))
            kb.need_w("t", eng, ("psavB",))
            for Irel in range(4):
                I = 4 * S + Irel
                for Jrel in range(3):
                    kcol = I + Jrel
                    slot = kcol % RING
                    for half in range(2):
                        kb.need_r("t", eng, ("vrs", slot, half))
                    off = 384 * Irel + 128 * Jrel
                    hA, hB = 2 * hp, 2 * hp + 1
                    eng.matmul(ps["avA"][0:65, 128 * Irel:128 * (Irel + 1)],
                               bf["vrr"][:, slot, 65 * hA:65 * (hA + 1)],
                               bf["expt"][:, 0, off:off + 128],
                               start=(Jrel == 0), stop=(Jrel == 2))
                    i = eng.matmul(ps["avB"][0:65, 128 * Irel:128 * (Irel + 1)],
                                   bf["vrr"][:, slot, 65 * hB:65 * (hB + 1)],
                                   bf["expt"][:, 1, off:off + 128],
                                   start=(Jrel == 0), stop=(Jrel == 2))
            c = kb.inc("t", i)
            kb.mark_w(("psavA",), "t", c)
            kb.mark_w(("psavB",), "t", c)
            kb.mark_r(("expt",), "t", c)
            for Irel in range(4):
                for Jrel in range(3):
                    slot = (4 * S + Irel + Jrel) % RING
                    for half in range(2):
                        kb.mark_r(("vrs", slot, half), "t", c)
        kb.sec("t", t_av)

        def s_rec(eng, hp=hp):
            kb.need_r("s", eng, ("psavA",))
            kb.need_r("s", eng, ("psavB",))
            kb.need_w("s", eng, ("recdA",))
            kb.need_w("s", eng, ("recdB",))
            kb.need_w("s", eng, ("t1",))
            kb.need_w("s", eng, ("t2",))
            eng.activation(bf["t1"][0:1, :], ps["avA"][64:65, :], AF.Ln)
            eng.activation(bf["t2"][0:1, :], ps["avB"][64:65, :], AF.Ln)
            eng.drain()
            eng.activation(bf["recdA"][0:1, :], bf["t1"][0:1, :], AF.Exp, scale=-1.0)
            c = kb.inc("s", eng.activation(bf["recdB"][0:1, :], bf["t2"][0:1, :],
                                           AF.Exp, scale=-1.0))
            kb.mark_w(("recdA",), "s", c)
            kb.mark_w(("recdB",), "s", c)
            kb.mark_w(("t1",), "s", c)
            kb.mark_w(("t2",), "s", c)
            kb.mark_r(("psavA",), "s", c)
            kb.mark_r(("psavB",), "s", c)
        kb.sec("s", s_rec)

        def t_dbc(eng, hp=hp):
            kb.need_r("t", eng, ("recdA",))
            kb.need_r("t", eng, ("recdB",))
            kb.need_r("t", eng, ("consts",))
            kb.need_w("t", eng, ("psrb2",))
            eng.matmul(ps["rb2"][:, :], bf["selA"][:, :], bf["recdA"][0:1, :],
                       start=True, stop=False)
            c = kb.inc("t", eng.matmul(ps["rb2"][:, :], bf["selB"][:, :],
                                       bf["recdB"][0:1, :], start=False, stop=True))
            kb.mark_w(("psrb2",), "t", c)
            kb.mark_r(("recdA",), "t", c)
            kb.mark_r(("recdB",), "t", c)
        kb.sec("t", t_dbc)

        def v_av(eng, hp=hp):
            kb.need_r("v", eng, ("psavA",))
            kb.need_r("v", eng, ("psavB",))
            kb.need_w("v", eng, ("avbf",))
            eng.tensor_copy(bf["avbf"][0:64, :], ps["avA"][0:64, :])
            c = kb.inc("v", eng.tensor_copy(bf["avbf"][64:128, :], ps["avB"][0:64, :]))
            kb.mark_w(("avbf",), "v", c)
            kb.mark_r(("psavA",), "v", c)
            kb.mark_r(("psavB",), "v", c)
        kb.sec("v", v_av)

        def v_div(eng, hp=hp):
            kb.need_r("v", eng, ("avbf",))
            kb.need_r("v", eng, ("psrb2",))
            kb.need_w("v", eng, ("xsba",))
            eng.drain()
            c = kb.inc("v", eng.tensor_tensor(bf["avun"][:, hp, :], bf["avbf"][:, :],
                                              ps["rb2"][:, :], ALU.mult))
            kb.mark_w(("xsba",), "v", c)
            kb.mark_r(("avbf",), "v", c)
            kb.mark_r(("psrb2",), "v", c)
        kb.sec("v", v_div)


def emit_outproj(kb, bf, S, ps):
    toff = SUB * S
    for mc in range(8):
        pok = ("po", mc % 2)

        def t_o(eng, mc=mc, pok=pok):
            if mc == 0:
                kb.psum_fence("t", eng)
            kb.need_r("t", eng, ("wo_g",))
            kb.need_r("t", eng, ("xsba",))
            kb.need_w("t", eng, pok)
            for kc in range(8):
                i = eng.matmul(ps["po"][mc % 2][:, :], bf["wog"][:, kc, 128 * mc:128 * (mc + 1)],
                               bf["avun"][:, kc, :], start=(kc == 0), stop=(kc == 7))
            c = kb.inc("t", i)
            kb.mark_w(pok, "t", c)
            kb.mark_r(("wo_g",), "t", c)
            kb.mark_r(("xsba",), "t", c)
        kb.sec("t", t_o)

        def v_o(eng, mc=mc, pok=pok):
            kb.need_r("v", eng, pok)
            kb.need_w("v", eng, ("xres",))
            c = kb.inc("v", eng.tensor_tensor(
                bf["xres"][:, mc, 1 + toff:1 + toff + 512], ps["po"][mc % 2][:, :],
                bf["xres"][:, mc, 1 + toff:1 + toff + 512], ALU.add))
            kb.mark_w(("xres",), "v", c)
            kb.mark_r(pok, "v", c)
        kb.sec("v", v_o)


def emit_exchange_e1(kb, bf, dram, l):
    def v_pack(eng):
        kb.need_r("v", eng, ("xres",))
        kb.need_w("v", eng, ("xrow2",))
        c = kb.inc("v", eng.tensor_copy(bf["xrow2"][:, :].unsqueeze(2),
                                        bf["xres"][:, :, T:T + 1]))
        kb.mark_w(("xrow2",), "v", c)
        kb.mark_r(("xres",), "v", c)
    kb.sec("v", v_pack)

    def g_pack(eng):
        sn = kb.dsem("e1p_d")
        kb.need_w("g", eng, ("e1io",))
        for mc in range(8):
            kb.need_r("g", eng, ("krs", 4, mc))
            kb.need_r("g", eng, ("krs", 5, mc))
        for s in (4, 5):
            for half in range(2):
                kb.need_r("g", eng, ("vrs", s, half))
        kb.need_r("g", eng, ("xres",))
        kb.inc(sn, eng.dma_start(out=dram["e1_in"][:, 0:2048], in_=bf["krr"][:, :, 512:768]), 16)
        kb.inc(sn, eng.dma_start(out=dram["e1_in"][:, 2048:4128], in_=bf["vrr"][:, 4:6, :]), 16)
        kb.need_r("g", eng, ("xrow2",))
        c = kb.inc(sn, eng.dma_start(
            out=dram["e1f_in"][:, :], in_=bf["xrow2"][:, :]), 16)
        kb.mark_r(("xrow2",), sn, c)
        for mc in range(8):
            kb.mark_r(("krs", 4, mc), sn, c)
            kb.mark_r(("krs", 5, mc), sn, c)
        for s in (4, 5):
            for half in range(2):
                kb.mark_r(("vrs", s, half), sn, c)
        kb.mark_r(("xres",), sn, c)

        eng.wait_ge(kb.sems[sn], c)
        kb.inc("cc", eng.collective_compute(
            "AllGather", ALU.bypass, replica_groups=CCGROUPS,
            ins=[dram["e1_in"][:, :].opt()], outs=[dram["e1_out"][:, :, :].opt()]))
        cc = kb.inc("cc", eng.collective_compute(
            "AllGather", ALU.bypass, replica_groups=CCGROUPS,
            ins=[dram["e1f_in"][:, :].opt()], outs=[dram["e1f_out"][:, :, :].opt()]))

        eng.wait_ge(kb.sems["cc"], cc)
        sn2 = kb.dsem("e1u_d")
        for mc in range(8):
            kb.need_w("g", eng, ("krs", 0, mc))
            kb.need_w("g", eng, ("krs", 1, mc))
        for s in (0, 1):
            for half in range(2):
                kb.need_w("g", eng, ("vrs", s, half))
        kb.need_w("g", eng, ("xrow",))
        kb.inc(sn2, eng.dma_start(out=bf["krr"][:, :, 0:256], in_=dram["e1_out"][0, :, 0:2048]), 16)
        kb.inc(sn2, eng.dma_start(out=bf["vrr"][:, 0:2, :], in_=dram["e1_out"][0, :, 2048:4128]), 16)
        c2 = kb.inc(sn2, eng.dma_start(
            out=bf["xrow"][:, :], in_=dram["e1f_out"][0, :, :]), 16)
        for mc in range(8):
            kb.mark_w(("krs", 0, mc), sn2, c2)
            kb.mark_w(("krs", 1, mc), sn2, c2)
        for s in (0, 1):
            for half in range(2):
                kb.mark_w(("vrs", s, half), sn2, c2)
        kb.mark_w(("xrow",), sn2, c2)
        kb.mark_w(("e1io",), sn2, c2)
    kb.sec("g", g_pack)

    def v_row(eng):
        kb.need_r("v", eng, ("xrow",))
        kb.need_r("v", eng, ("consts",))
        kb.need_w("v", eng, ("xres",))
        c = kb.inc("v", eng.tensor_scalar_mul(
            bf["xres"][:, :, 0:1], bf["xrow"][:, :].unsqueeze(2), bf["flag"][:, 0:1]))
        kb.mark_w(("xres",), "v", c)
        kb.mark_r(("xrow",), "v", c)
    kb.sec("v", v_row)


def emit_exchange_e2(kb, bf, dram, l):
    def v_pack(eng):
        kb.need_r("v", eng, ("xres",))
        kb.need_w("v", eng, ("xrow2",))
        c = kb.inc("v", eng.tensor_copy(bf["xrow2"][:, :].unsqueeze(2),
                                        bf["xres"][:, :, T:T + 1]))
        kb.mark_w(("xrow2",), "v", c)
        kb.mark_r(("xres",), "v", c)
    kb.sec("v", v_pack)

    def g(eng):
        sn = kb.dsem("e2p_d")
        kb.need_w("g", eng, ("e2io",))
        kb.need_r("g", eng, ("xres",))
        kb.need_r("g", eng, ("xrow2",))
        c = kb.inc(sn, eng.dma_start(
            out=dram["e2_in"][:, :], in_=bf["xrow2"][:, :]), 16)
        kb.mark_r(("xrow2",), sn, c)
        kb.mark_r(("xres",), sn, c)
        eng.wait_ge(kb.sems[sn], c)
        cc = kb.inc("cc", eng.collective_compute(
            "AllGather", ALU.bypass,
            replica_groups=CCGROUPS,
            ins=[dram["e2_in"][:, :].opt()],
            outs=[dram["e2_out"][:, :, :].opt()]))
        eng.wait_ge(kb.sems["cc"], cc)
        sn2 = kb.dsem("e2u_d")
        kb.need_w("g", eng, ("xrow",))
        c2 = kb.inc(sn2, eng.dma_start(
            out=bf["xrow"][:, :], in_=dram["e2_out"][0, :, :]), 16)
        kb.mark_w(("xrow",), sn2, c2)
        kb.mark_w(("e2io",), sn2, c2)
    kb.sec("g", g)

    def v_row(eng):
        kb.need_r("v", eng, ("xrow",))
        kb.need_w("v", eng, ("xres",))
        c = kb.inc("v", eng.tensor_scalar_mul(
            bf["xres"][:, :, 0:1], bf["xrow"][:, :].unsqueeze(2), bf["flag"][:, 0:1]))
        kb.mark_w(("xres",), "v", c)
        kb.mark_r(("xrow",), "v", c)
    kb.sec("v", v_row)


def emit_attn_layer(kb, bf, dram, l):
    nc = kb.nc
    emit_vrr_init(kb, bf)
    emit_attn_loads(kb, bf, dram, l)

    def mkps(ctx):
        return dict(
            pp=[ctx.enter_context(nc.psum_tensor(kb.uid("pp"), [128, 512], F32))
                for _ in range(2)],
            rot=[ctx.enter_context(nc.psum_tensor(kb.uid("prot"), [128, 512], F32))
                 for _ in range(2)],
            ss=[ctx.enter_context(nc.psum_tensor(kb.uid("pss"), [128, 512], F32))
                for _ in range(2)],
            rb=[ctx.enter_context(nc.psum_tensor(kb.uid("prb"), [128, 512], F32))
                for _ in range(2)],
        )
    PSK_QKV = [("pp", 0), ("pp", 1), ("psrot", 0), ("psrot", 1),
               ("psss", 0), ("psss", 1), ("psrb", 0), ("psrb", 1)]

    with ExitStack() as ctx:
        ps = mkps(ctx)
        emit_tab_load(kb, bf, dram, "k", T - 256, 256)
        emit_shift_cast_attn(kb, bf, T - 256, 256)
        emit_proj_postproc(kb, bf, "k", l, T - 256, 256, [16, 17], ps)
        emit_v_block(kb, bf, T - 256, 256, [16, 17], ps)
        kb.psum_close(PSK_QKV)
    emit_exchange_e1(kb, bf, dram, l)
    emit_bnd_snapshot(kb, bf)

    for S in range(NSUB):
        toff = SUB * S
        kcols = list(range(4 * S + 2, 4 * S + 6))
        with ExitStack() as ctx:
            ps = mkps(ctx)
            emit_tab_load(kb, bf, dram, "k", toff, SUB)
            emit_shift_cast_attn(kb, bf, toff, SUB, bnd=None if S == 0 else S - 1)
            emit_proj_postproc(kb, bf, "k", l, toff, SUB, kcols, ps)
            emit_v_block(kb, bf, toff, SUB, kcols, ps)
            emit_tab_load(kb, bf, dram, "q", toff, SUB)
            emit_proj_postproc(kb, bf, "q", l, toff, SUB, None, ps)
            kb.psum_close(PSK_QKV)
        with ExitStack() as ctx:
            ps2 = dict(
                sim=ctx.enter_context(nc.psum_tensor(kb.uid("psim"), [128, 3072], F32)),
                avA=ctx.enter_context(nc.psum_tensor(kb.uid("pavA"), [128, 512], F32)),
                avB=ctx.enter_context(nc.psum_tensor(kb.uid("pavB"), [128, 512], F32)),
            )
            ps2["rb2"] = ps2["sim"][:, 0:512]
            emit_attn_core(kb, bf, l, S, ps2)
            kb.psum_close([("pssim",), ("psavA",), ("psavB",), ("psrb2",)])
        with ExitStack() as ctx:
            ps3 = dict(po=[ctx.enter_context(
                nc.psum_tensor(kb.uid("po"), [128, 512], F32)) for _ in range(2)])
            emit_outproj(kb, bf, S, ps3)
            kb.psum_close([("po", 0), ("po", 1)])
    emit_exchange_e2(kb, bf, dram, l)
    kb.wflush()


def declare_attn_drams(nc, nranks=2):
    dram = {}
    EI = lambda n, s, d: nc.dram_tensor(n, s, d, kind="ExternalInput")
    dram["wqkv"] = EI("wqkv", [6, 1024, 3072], F32)
    dram["wo_i"] = EI("wo_i", [6, 1024, 1024], F32)
    dram["csq"] = EI("csq", [128, T], F32)
    dram["ssq"] = EI("ssq", [128, T], F32)
    dram["csk"] = EI("csk", [128, T], F32)
    dram["ssk"] = EI("ssk", [128, T], F32)
    dram["qs_m"] = EI("qs_m", [6, 128, 1], F32)
    dram["qs_p"] = EI("qs_p", [6, 128, 1], F32)
    dram["ks_m"] = EI("ks_m", [6, 128, 1], F32)
    dram["ks_p"] = EI("ks_p", [6, 128, 1], F32)
    dram["qs_r"] = EI("qs_r", [6, 64], F32)
    dram["ks_r"] = EI("ks_r", [6, 64], F32)
    dram["p128"] = EI("p128", [128, 128], BF16)
    dram["idn"] = EI("idn", [128, 128], BF16)
    dram["masku"] = EI("masku", [128, 128], BF16)
    dram["maskl"] = EI("maskl", [128, 128], BF16)
    dram["sel2t"] = EI("sel2t", [128, 2], BF16)
    dram["sel2"] = EI("sel2", [2, 128], BF16)
    dram["selA"] = EI("selA", [1, 128], F32)
    dram["selB"] = EI("selB", [1, 128], F32)
    dram["flag"] = EI("flag", [128, 1], F32)
    dram["hmask"] = EI("hmask", [128, 1], F32)
    dram["e1_in"] = nc.dram_tensor("e1_in", [128, 4128], BF16)
    dram["e1_out"] = nc.dram_tensor("e1_out", [nranks, 128, 4128], BF16)
    dram["e1f_in"] = nc.dram_tensor("e1f_in", [128, 8], F32)
    dram["e1f_out"] = nc.dram_tensor("e1f_out", [nranks, 128, 8], F32)
    dram["e2_in"] = nc.dram_tensor("e2_in", [128, 8], F32)
    dram["e2_out"] = nc.dram_tensor("e2_out", [nranks, 128, 8], F32)
    return dram


def _table_maxima():
    return 1.05, 1.0


def build_attn_test(nranks=1):
    global CCGROUPS
    CCGROUPS = [[0]] if nranks == 1 else [[0, 1]]
    nc = bass.Bass("TRN2", target_bir_lowering=False, debug=False)
    dram = declare_attn_drams(nc, nranks=nranks)
    x_in = nc.dram_tensor("x_in", [128, 8, T + 1], F32, kind="ExternalInput")
    out = nc.dram_tensor("out", [128, 8, T + 1], F32, kind="ExternalOutput")

    with ExitStack() as ctx:
        block = ctx.enter_context(nc.Block())
        kb = KB(nc, block, ctx)
        bf = alloc_common_bufs(nc, ctx)
        alloc_persist_bufs(nc, ctx, bf)
        alloc_attn_bufs(nc, ctx, bf)
        emit_init_consts(kb, bf)
        emit_attn_consts(kb, bf, dram)

        def y_in(eng):
            c = kb.inc(kb.dsem("xin_d"), eng.dma_start(out=bf["xres"][:, :, :], in_=x_in[:, :, :]), 16)
            kb.mark_w(("xres",), "xin_d", c)
        kb.sec("y", y_in)

        mscq, msck = _table_maxima()
        with nc.psum_tensor(kb.uid("ps_nb"), [128, 512], F32) as ps_nb:
            bf["ps_nb"] = ps_nb
            emit_negB(kb, bf, dram, mscq, msck)
            kb.psum_close([("ps_nb",)])

        emit_attn_layer(kb, bf, dram, 0)

        def y_out(eng):
            kb.need_r("y", eng, ("xres",))
            sn = kb.dsem("out_d")
            kb.inc(sn, eng.dma_start(out=out[:, :, :], in_=bf["xres"][:, :, :]), 16)
            kb.wait("y", eng, sn, kb.cnt[sn])
        kb.sec("y", y_out)
    return nc


def alloc_outer_bufs(nc, ctx, bf):
    A = lambda name, shape, dt: ctx.enter_context(nc.sbuf_tensor(_uname(name), shape, dt))
    bf["xo"] = A("xo", [128, 8, 512], F32)
    bf["xnb"] = A("xnb", [128, 8, 512], BF16)
    bf["att"] = A("att", [128, 8, 2048], BF16)
    bf["xsqo"] = bf["att"]
    bf["ktmp"] = A("ktmp", [128, 2, 512], BF16)
    bf["qtmp"] = A("qtmp", [128, 2, 512], BF16)
    bf["vtmp"] = A("vtmp", [128, 8, 130], BF16)
    bf["kstrip"] = A("kstrip", [128, 2, 4096], BF16)
    bf["vstrip"] = A("vstrip", [128, 2, 32 * 130], BF16)
    bf["qstrip"] = A("qstrip", [128, 2, 2048], BF16)
    bf["expo"] = A("expo", [128, 2048], BF16)
    bf["avbf_o"] = A("avbf_o", [128, 512], BF16)
    bf["to1"] = A("to1", [128, 512], F32)
    bf["oqsq"] = A("oqsq", [128, 512], BF16)
    bf["owg"] = A("owg", [128, 8, 1024], BF16)
    bf["recdA_o"] = A("recdA_o", [1, 512], F32)
    bf["recdB_o"] = A("recdB_o", [1, 512], F32)
    bf["accq"] = A("accq", [2, 8], F32)
    bf["acck"] = A("acck", [2, 8], F32)
    bf["tacc"] = A("tacc", [2, 8], F32)
    bf["tacc2"] = A("tacc2", [2, 8], F32)
    bf["negBo"] = A("negBo", [128, 8], F32)
    bf["ones21"] = A("ones21", [2, 1], F32)
    bf["bo"] = A("bo", [128, 8], F32)
    return bf


def emit_outer_ln(kb, bf, b8, ps):
    def v_cast(eng):
        kb.need_w("v", eng, ("xnb",))
        kb.need_r("v", eng, ("xo",))
        c = kb.inc("v", eng.tensor_copy(bf["xnb"][:, :, :], bf["xo"][:, :, :]))
        kb.mark_w(("xnb",), "v", c)
        kb.mark_r(("xo",), "v", c)
    kb.sec("v", v_cast)

    def s_sq(eng):
        kb.need_w("s", eng, ("xsqo",))
        kb.need_r("s", eng, ("xnb",))
        c = kb.inc("s", eng.activation(bf["xsqo"][:, :, 0:512], bf["xnb"][:, :, :],
                                       AF.Square))
        kb.mark_w(("xsqo",), "s", c)
        kb.mark_r(("xnb",), "s", c)
    kb.sec("s", s_sq)

    def t_st(eng):
        kb.psum_fence("t", eng)
        kb.need_w("t", eng, ("ps_st",))
        kb.need_r("t", eng, ("xnb",))
        kb.need_r("t", eng, ("xsqo",))
        kb.need_r("t", eng, ("ones128",))
        for kc in range(8):
            eng.matmul(ps["st"][0:1, :], bf["ones128"][:, :], bf["xnb"][:, kc, :],
                       start=(kc == 0), stop=(kc == 7), tile_position=(0, 0))
        for kc in range(8):
            i = eng.matmul(ps["st"][32:33, :], bf["ones128"][:, :],
                           bf["xsqo"][:, kc, 0:512], start=(kc == 0), stop=(kc == 7),
                           tile_position=(0, 32))
        c = kb.inc("t", i)
        kb.mark_w(("ps_st",), "t", c)
        kb.mark_r(("xnb",), "t", c)
        kb.mark_r(("xsqo",), "t", c)
    kb.sec("t", t_st)

    st = bf["st"]
    def v_f3a(eng):
        kb.need_r("v", eng, ("ps_st",))
        kb.need_w("v", eng, ("st",))
        eng.tensor_scalar_mul(st[0:1, 0, :], ps["st"][0:1, :], 1.0 / 1024.0)
        eng.drain()
        eng.scalar_tensor_tensor(st[0:1, 1, :], st[0:1, 0, :], -1.0, st[0:1, 0, :],
                                 ALU.mult, ALU.mult)
        eng.drain()
        c = kb.inc("v", eng.scalar_tensor_tensor(
            st[0:1, 2, :], ps["st"][32:33, :], 1.0 / 1024.0, st[0:1, 1, :],
            ALU.mult, ALU.add))
        kb.mark_r(("ps_st",), "v", c)
        kb.mark_w(("st",), "v", c)
    kb.sec("v", v_f3a)

    def s_f3(eng):
        kb.need_r("s", eng, ("st",))
        kb.need_r("s", eng, ("eps",))
        eng.activation(st[0:1, 1, :], st[0:1, 2, :], AF.Ln, bias=bf["eps"][0:1, :])
        eng.drain()
        c = kb.inc("s", eng.activation(st[0:1, 2, :], st[0:1, 1, :], AF.Exp, scale=-0.5))
        kb.mark_w(("st",), "s", c)
    kb.sec("s", s_f3)

    def v_f3b(eng):
        kb.need_r("v", eng, ("st",))
        c = kb.inc("v", eng.scalar_tensor_tensor(
            st[0:1, 1, :], st[0:1, 0, :], -1.0, st[0:1, 2, :], ALU.mult, ALU.mult))
        kb.mark_w(("st",), "v", c)
    kb.sec("v", v_f3b)

    def t_bc(eng):
        kb.need_r("t", eng, ("st",))
        kb.need_r("t", eng, ("ones1",))
        kb.need_w("t", eng, ("ps_rg",))
        kb.need_w("t", eng, ("ps_nm",))
        eng.matmul(ps["rg"][:, :], bf["ones1"][:, :], st[0:1, 2, :], start=True, stop=True)
        c = kb.inc("t", eng.matmul(ps["nm"][:, :], bf["ones1"][:, :], st[0:1, 1, :],
                                   start=True, stop=True))
        kb.mark_w(("ps_rg",), "t", c)
        kb.mark_w(("ps_nm",), "t", c)
        kb.mark_r(("st",), "t", c)
    kb.sec("t", t_bc)

    for kc in range(8):
        def v_xn(eng, kc=kc):
            kb.need_r("v", eng, ("ps_rg",))
            kb.need_r("v", eng, ("ps_nm",))
            kb.need_r("v", eng, ("xo",))
            kb.need_w("v", eng, ("tmpo",))
            if kc == 0:
                kb.need_w("v", eng, ("xnb",))
            eng.tensor_tensor(bf["to1"][:, :], bf["xo"][:, kc, :], ps["rg"][:, :], ALU.mult)
            eng.drain()
            c = kb.inc("v", eng.tensor_tensor(
                bf["xnb"][:, kc, :], bf["to1"][:, :], ps["nm"][:, :], ALU.add))
            kb.mark_r(("ps_rg",), "v", c)
            kb.mark_r(("ps_nm",), "v", c)
            kb.mark_r(("xo",), "v", c)
            kb.mark_w(("xnb",), "v", c)
            kb.mark_w(("tmpo",), "v", c)
        kb.sec("v", v_xn)


def emit_outer(kb, bf, dram):
    nc = kb.nc

    def y_bo(eng):
        sn = kb.dsem("bo_d")
        c = kb.inc(sn, eng.dma_start(out=bf["bo"][:, :], in_=dram["bo_t"][:, :]), 16)
        kb.mark_w(("bo",), sn, c)
    kb.sec("y", y_bo)

    def v_init(eng):
        eng.memset(bf["vtmp"][:, :, :], 1.0)
        eng.memset(bf["accq"][:, :], 0.0)
        c = kb.inc("v", eng.memset(bf["acck"][:, :], 0.0))
        kb.mark_w(("vtmp", 0), "v", c)
        kb.mark_w(("vtmp", 1), "v", c)
        kb.mark_w(("accq",), "v", c)
        kb.mark_w(("acck",), "v", c)
    kb.sec("v", v_init)

    with ExitStack() as ctx:
        ps = dict(
            st=ctx.enter_context(nc.psum_tensor(kb.uid("ost"), [128, 512], F32)),
            rg=ctx.enter_context(nc.psum_tensor(kb.uid("org"), [128, 512], F32)),
            nm=ctx.enter_context(nc.psum_tensor(kb.uid("onm"), [128, 512], F32)),
        )
        for b8 in range(8):
            def y_x(eng, b8=b8):
                sn = kb.dsem("ox_d")
                kb.need_w("y", eng, ("xo",))
                c = kb.inc(sn, eng.dma_start(out=bf["xo"][:, :, :],
                                             in_=dram["x"][b8, :, :, :]), 16)
                kb.mark_w(("xo",), sn, c)
            kb.sec("y", y_x)

            emit_outer_ln(kb, bf, b8, ps)

            def g_xn(eng, b8=b8):
                sn = kb.dsem("xnsp_d")
                kb.need_r("g", eng, ("xnb",))
                c = kb.inc(sn, eng.dma_start(out=dram["xnspill"][b8, :, :, :],
                                             in_=bf["xnb"][:, :, :]), 16)
                kb.mark_r(("xnb",), sn, c)
                kb.mark_w(("xnsp", b8), sn, c)
            kb.sec("g", g_xn)
        kb.psum_close([("ps_st",), ("ps_rg",), ("ps_nm",)])

    with ExitStack() as ctx:
        ps = dict(
            pp=[ctx.enter_context(nc.psum_tensor(kb.uid("opp"), [128, 512], F32))
                for _ in range(2)],
            ss=ctx.enter_context(nc.psum_tensor(kb.uid("oss"), [128, 512], F32)),
        )
        for kind, wsrc in (("k", "wk"), ("v", "wv"), ("q", "wq")):
            def g_w(eng, wsrc=wsrc):
                kb.need_w("g", eng, ("owg",))
                sn = kb.dsem("owg_d")
                for kc in range(8):
                    kb.inc(sn, eng.dma_start(
                        out=bf["owg"][:, kc, :],
                        in_=dram[wsrc][128 * kc:128 * (kc + 1), :]), 16)
                kb.mark_w(("owg",), sn, kb.cnt[sn])
            kb.sec("g", g_w)

            nb8 = 4 if kind == "q" else 8
            for b8 in range(nb8):
                def y_xn(eng, b8=b8):
                    sn = kb.dsem("xnl_d")
                    kb.need_w("y", eng, ("xnb",))
                    kb.need_r("y", eng, ("xnsp", b8))
                    c = kb.inc(sn, eng.dma_start(out=bf["xnb"][:, :, :],
                                                 in_=dram["xnspill"][b8, :, :, :]), 16)
                    kb.mark_w(("xnb",), sn, c)
                kb.sec("y", y_xn)

                if kind == "v":
                    for tc in range(4):
                        for half in range(2):
                            pp = ps["pp"][half]
                            ppk = ("pp", half)

                            def t_v(eng, tc=tc, half=half, pp=pp, ppk=ppk):
                                kb.psum_fence("t", eng)
                                kb.need_r("t", eng, ("owg",))
                                kb.need_r("t", eng, ("xnb",))
                                kb.need_w("t", eng, ppk)
                                for kc in range(8):
                                    i = eng.matmul(
                                        pp[:, :], bf["xnb"][:, kc, 128 * tc:128 * (tc + 1)],
                                        bf["owg"][:, kc, 512 * half:512 * (half + 1)],
                                        start=(kc == 0), stop=(kc == 7))
                                c = kb.inc("t", i)
                                kb.mark_w(ppk, "t", c)
                                kb.mark_r(("owg",), "t", c)
                                kb.mark_r(("xnb",), "t", c)
                            kb.sec("t", t_v)

                            def v_v(eng, half=half, pp=pp, ppk=ppk):
                                kb.need_r("v", eng, ppk)
                                kb.need_w("v", eng, ("vtmp", half))
                                dst = bf["vtmp"][:, 4 * half:4 * (half + 1), :].rearrange(
                                    "p a (b c) -> p a b c", c=65)[:, :, :, 0:64]
                                src = pp[:, :].rearrange("p (a b c) -> p a b c", b=2, c=64)
                                c = kb.inc("v", eng.tensor_copy(dst, src))
                                kb.mark_w(("vtmp", half), "v", c)
                                kb.mark_r(ppk, "v", c)
                            kb.sec("v", v_v)

                        def g_vsp(eng, tc=tc, b8=b8):
                            sn = kb.dsem("vsp_d")
                            for half in range(2):
                                kb.need_r("g", eng, ("vtmp", half))
                            for hp in range(8):
                                c = kb.inc(sn, eng.dma_start(
                                    out=dram["vspill"][hp, :, 4 * b8 + tc, :],
                                    in_=bf["vtmp"][:, hp, :]), 16)
                            kb.mark_w(("vsp",), sn, c)
                            for half in range(2):
                                kb.mark_r(("vtmp", half), sn, kb.cnt[sn])
                        kb.sec("g", g_vsp)
                    continue

                for mc in range(8):
                    pp = ps["pp"][mc % 2]
                    ppk = ("pp", mc % 2)
                    tmpb = bf["ktmp"] if kind == "k" else bf["qtmp"]
                    tmpk = ("ktmp", mc % 2) if kind == "k" else ("qtmp", mc % 2)

                    def t_p(eng, mc=mc, pp=pp, ppk=ppk):
                        kb.psum_fence("t", eng)
                        kb.need_r("t", eng, ("owg",))
                        kb.need_r("t", eng, ("xnb",))
                        kb.need_w("t", eng, ppk)
                        for kc in range(8):
                            i = eng.matmul(pp[:, :], bf["owg"][:, kc, 128 * mc:128 * (mc + 1)],
                                           bf["xnb"][:, kc, :], start=(kc == 0), stop=(kc == 7))
                        c = kb.inc("t", i)
                        kb.mark_w(ppk, "t", c)
                        kb.mark_r(("owg",), "t", c)
                        kb.mark_r(("xnb",), "t", c)
                    kb.sec("t", t_p)

                    def v_e(eng, mc=mc, pp=pp, ppk=ppk, tmpb=tmpb, tmpk=tmpk):
                        kb.need_r("v", eng, ppk)
                        kb.need_w("v", eng, tmpk)
                        kb.need_w("v", eng, ("qsqo",))
                        eng.tensor_copy(tmpb[:, mc % 2, :], pp[:, :])
                        eng.drain()
                        c = kb.inc("v", eng.scalar_tensor_tensor(
                            bf["oqsq"][:, :], pp[:, :], 1.0, tmpb[:, mc % 2, :],
                            ALU.mult, ALU.mult))
                        kb.mark_w(tmpk, "v", c)
                        kb.mark_w(("qsqo",), "v", c)
                        kb.mark_r(ppk, "v", c)
                    kb.sec("v", v_e)

                    def t_ss(eng):
                        kb.need_r("t", eng, ("qsqo",))
                        kb.need_r("t", eng, ("consts",))
                        kb.need_w("t", eng, ("psss",))
                        c = kb.inc("t", eng.matmul(ps["ss"][0:2, :], bf["sel2t"][:, :],
                                                   bf["oqsq"][:, :], start=True, stop=True))
                        kb.mark_w(("psss",), "t", c)
                        kb.mark_r(("qsqo",), "t", c)
                    kb.sec("t", t_ss)

                    def v_acc(eng, mc=mc, kind=kind):
                        acc = bf["accq"] if kind == "q" else bf["acck"]
                        akey = ("accq",) if kind == "q" else ("acck",)
                        kb.need_r("v", eng, ("psss",))
                        kb.need_w("v", eng, ("tacc",))
                        kb.need_r("v", eng, akey)
                        eng.tensor_reduce(bf["tacc"][0:2, 0:1], ps["ss"][0:2, :],
                                          mybir.AxisListType.X, ALU.max)
                        eng.drain()
                        c = kb.inc("v", eng.tensor_tensor(
                            acc[0:2, mc:mc + 1], acc[0:2, mc:mc + 1],
                            bf["tacc"][0:2, 0:1], ALU.max))
                        kb.mark_w(akey, "v", c)
                        kb.mark_w(("tacc",), "v", c)
                        kb.mark_r(("psss",), "v", c)
                    kb.sec("v", v_acc)

                    def y_sp(eng, mc=mc, b8=b8, kind=kind, tmpb=tmpb, tmpk=tmpk):
                        sp = dram["kspill"] if kind == "k" else dram["qspill"]
                        sn = kb.dsem(kind + "sp_d" + str(mc % 2))
                        kb.need_r("y", eng, tmpk)
                        c = kb.inc(sn, eng.dma_start(
                            out=sp[mc, :, 512 * b8:512 * (b8 + 1)],
                            in_=tmpb[:, mc % 2, :]), 16)
                        kb.mark_r(tmpk, sn, c)
                        kb.mark_w((kind + "sp", mc), sn, c)
                    kb.sec("y", y_sp)
        kb.psum_close([("pp", 0), ("pp", 1), ("psss",)])


def emit_outer_attn(kb, bf, dram):
    nc = kb.nc
    with ExitStack() as ctx:
        ps_nb = ctx.enter_context(nc.psum_tensor(kb.uid("onb"), [128, 512], F32))

        def v_b1(eng):
            kb.need_r("v", eng, ("accq",))
            kb.need_r("v", eng, ("acck",))
            kb.need_w("v", eng, ("tacc",))
            eng.memset(bf["ones21"][:, :], 1.0)
            c = kb.inc("v", eng.tensor_tensor(bf["tacc"][0:2, :], bf["accq"][0:2, :],
                                              bf["acck"][0:2, :], ALU.mult))
            kb.mark_w(("tacc",), "v", c)
            kb.mark_w(("ones21",), "v", c)
        kb.sec("v", v_b1)

        def s_b(eng):
            kb.need_r("s", eng, ("tacc",))
            eng.activation(bf["tacc2"][0:2, :], bf["tacc"][0:2, :], AF.Ln)
            eng.drain()
            c = kb.inc("s", eng.activation(bf["tacc"][0:2, :], bf["tacc2"][0:2, :],
                                           AF.Exp, scale=0.5))
            kb.mark_w(("tacc",), "s", c)
        kb.sec("s", s_b)

        def t_b(eng):
            kb.psum_fence("t", eng)
            kb.need_r("t", eng, ("tacc",))
            kb.need_r("t", eng, ("ones21",))
            c = kb.inc("t", eng.matmul(ps_nb[0:1, 0:8], bf["ones21"][:, :],
                                       bf["tacc"][0:2, :], start=True, stop=True))
            kb.mark_w(("ps_nb",), "t", c)
            kb.mark_r(("tacc",), "t", c)
        kb.sec("t", t_b)

        def v_b2(eng):
            kb.need_r("v", eng, ("ps_nb",))
            kb.need_w("v", eng, ("tacc",))
            c = kb.inc("v", eng.tensor_scalar_mul(bf["tacc"][0:1, :], ps_nb[0:1, 0:8],
                                                  -0.125 * 1.05))
            kb.mark_w(("tacc",), "v", c)
            kb.mark_r(("ps_nb",), "v", c)
        kb.sec("v", v_b2)

        def t_b2(eng):
            kb.need_r("t", eng, ("tacc",))
            kb.need_r("t", eng, ("ones1",))
            kb.need_w("t", eng, ("ps_nb",))
            c = kb.inc("t", eng.matmul(ps_nb[:, 0:8], bf["ones1"][:, :],
                                       bf["tacc"][0:1, :], start=True, stop=True))
            kb.mark_w(("ps_nb",), "t", c)
            kb.mark_r(("tacc",), "t", c)
        kb.sec("t", t_b2)

        def v_b3(eng):
            kb.need_r("v", eng, ("ps_nb",))
            kb.need_w("v", eng, ("negBo",))
            c = kb.inc("v", eng.tensor_copy(bf["negBo"][:, :], ps_nb[:, 0:8]))
            kb.mark_w(("negBo",), "v", c)
            kb.mark_r(("ps_nb",), "v", c)
        kb.sec("v", v_b3)
        kb.psum_close([("ps_nb",)])

    with ExitStack() as ctx:
        ps = dict(
            sim=[ctx.enter_context(nc.psum_tensor(kb.uid("osim"), [128, 1024], F32))
                 for _ in range(2)],
            avA=ctx.enter_context(nc.psum_tensor(kb.uid("oavA"), [128, 1024], F32)),
            avB=ctx.enter_context(nc.psum_tensor(kb.uid("oavB"), [128, 1024], F32)),
        )
        for hp in range(8):
            buf = hp % 2

            def g_stream(eng, hp=hp, buf=buf):
                for nm, dst, sh in (("kstrip", bf["kstrip"], dram["kspill"]),
                                    ("qstrip", bf["qstrip"], dram["qspill"]),
                                    ("vstrip", bf["vstrip"], dram["vspill"])):
                    kb.need_w("g", eng, (nm, buf))
                    if nm == "kstrip":
                        kb.need_r("g", eng, ("ksp", hp))
                    elif nm == "qstrip":
                        kb.need_r("g", eng, ("qsp", hp))
                    else:
                        kb.need_r("g", eng, ("vsp",))
                    sn = kb.dsem(nm + "_d" + str(buf))
                    if nm == "vstrip":
                        c = kb.inc(sn, eng.dma_start(
                            out=dst[:, buf, :],
                            in_=sh[hp, :, :, :].rearrange("p a b -> p (a b)")), 16)
                    else:
                        c = kb.inc(sn, eng.dma_start(out=dst[:, buf, :], in_=sh[hp, :, :]), 16)
                    kb.mark_w((nm, buf), sn, c)
            kb.sec("g", g_stream)

            for ih in range(2):
                TS, SE, TA = [], [], []
                for jc in range(32):
                    for i2 in range(2):
                        def t_sim(eng, hp=hp, buf=buf, ih=ih, jc=jc, i2=i2):
                            kb.psum_fence("t", eng)
                            kb.need_r("t", eng, ("kstrip", buf))
                            kb.need_r("t", eng, ("qstrip", buf))
                            kb.need_w("t", eng, ("pssim", i2))
                            kb.need_w("t", eng, ("psrb2",))
                            if jc == 0:
                                kb.need_w("t", eng, ("psavA",))
                                kb.need_w("t", eng, ("psavB",))
                            ib = 2 * ih + i2
                            eng.matmul(ps["sim"][i2][:, 0:512],
                                       bf["kstrip"][0:64, buf, 128 * jc:128 * (jc + 1)],
                                       bf["qstrip"][0:64, buf, 512 * ib:512 * (ib + 1)],
                                       start=True, stop=True, tile_position=(0, 0))
                            i = eng.matmul(ps["sim"][i2][:, 512:1024],
                                           bf["kstrip"][64:128, buf, 128 * jc:128 * (jc + 1)],
                                           bf["qstrip"][64:128, buf, 512 * ib:512 * (ib + 1)],
                                           start=True, stop=True, tile_position=(64, 0))
                            c = kb.inc("t", i)
                            kb.mark_w(("pssim", i2), "t", c)
                            kb.mark_r(("kstrip", buf), "t", c)
                            kb.mark_r(("qstrip", buf), "t", c)
                        TS.append(t_sim)

                        def s_exp(eng, hp=hp, i2=i2):
                            kb.need_r("s", eng, ("pssim", i2))
                            kb.need_r("s", eng, ("negBo",))
                            kb.need_w("s", eng, ("expo", i2))
                            c = kb.inc("s", eng.activation(
                                bf["expo"][:, 1024 * i2:1024 * (i2 + 1)],
                                ps["sim"][i2][:, :], AF.Exp,
                                bias=bf["negBo"][:, hp:hp + 1], scale=0.125))
                            kb.mark_w(("expo", i2), "s", c)
                            kb.mark_r(("pssim", i2), "s", c)
                        SE.append(s_exp)

                        def t_av(eng, buf=buf, jc=jc, i2=i2):
                            kb.need_r("t", eng, ("expo", i2))
                            kb.need_r("t", eng, ("vstrip", buf))
                            eng.matmul(ps["avA"][0:65, 512 * i2:512 * (i2 + 1)],
                                       bf["vstrip"][:, buf, 130 * jc:130 * jc + 65],
                                       bf["expo"][:, 1024 * i2:1024 * i2 + 512],
                                       start=(jc == 0), stop=(jc == 31))
                            i = eng.matmul(ps["avB"][0:65, 512 * i2:512 * (i2 + 1)],
                                           bf["vstrip"][:, buf, 130 * jc + 65:130 * (jc + 1)],
                                           bf["expo"][:, 1024 * i2 + 512:1024 * (i2 + 1)],
                                           start=(jc == 0), stop=(jc == 31))
                            c = kb.inc("t", i)
                            kb.mark_w(("psavA",), "t", c)
                            kb.mark_w(("psavB",), "t", c)
                            kb.mark_r(("expo", i2), "t", c)
                            kb.mark_r(("vstrip", buf), "t", c)
                        TA.append(t_av)

                nstep = len(TS)
                for k in range(nstep + 1):
                    if k < nstep:
                        kb.sec("t", TS[k])
                    if 0 <= k - 1 < nstep:
                        kb.sec("s", SE[k - 1])
                        kb.sec("t", TA[k - 1])

                for i2 in range(2):
                    ib = 2 * ih + i2

                    def s_rec(eng, i2=i2):
                        kb.need_r("s", eng, ("psavA",))
                        kb.need_r("s", eng, ("psavB",))
                        kb.need_w("s", eng, ("recdA_o",))
                        kb.need_w("s", eng, ("recdB_o",))
                        kb.need_w("s", eng, ("tmpo",))
                        kb.need_w("s", eng, ("xnb",))
                        reclA = bf["to1"][0:1, :]
                        reclB = bf["xnb"][0:1, 0:2, :].bitcast(F32).rearrange(
                            "p a b -> p (a b)")
                        eng.activation(reclA,
                                       ps["avA"][64:65, 512 * i2:512 * (i2 + 1)], AF.Ln)
                        eng.activation(reclB,
                                       ps["avB"][64:65, 512 * i2:512 * (i2 + 1)], AF.Ln)
                        eng.drain()
                        eng.activation(bf["recdA_o"][0:1, :], reclA, AF.Exp, scale=-1.0)
                        c = kb.inc("s", eng.activation(bf["recdB_o"][0:1, :], reclB,
                                                       AF.Exp, scale=-1.0))
                        kb.mark_w(("recdA_o",), "s", c)
                        kb.mark_w(("recdB_o",), "s", c)
                        kb.mark_w(("tmpo",), "s", c)
                        kb.mark_w(("xnb",), "s", c)
                        kb.mark_r(("psavA",), "s", c)
                        kb.mark_r(("psavB",), "s", c)
                    kb.sec("s", s_rec)

                    def t_dbc(eng, i2=i2):
                        kb.need_r("t", eng, ("recdA_o",))
                        kb.need_r("t", eng, ("recdB_o",))
                        kb.need_r("t", eng, ("consts",))
                        kb.need_w("t", eng, ("psrb2",))
                        kb.need_w("t", eng, ("pssim", 0))
                        eng.matmul(ps["sim"][0][:, 0:512], bf["selA"][:, :],
                                   bf["recdA_o"][0:1, :], start=True, stop=False)
                        c = kb.inc("t", eng.matmul(ps["sim"][0][:, 0:512], bf["selB"][:, :],
                                                   bf["recdB_o"][0:1, :], start=False, stop=True))
                        kb.mark_w(("psrb2",), "t", c)
                        kb.mark_r(("recdA_o",), "t", c)
                        kb.mark_r(("recdB_o",), "t", c)
                    kb.sec("t", t_dbc)

                    def v_av(eng, i2=i2):
                        kb.need_r("v", eng, ("psavA",))
                        kb.need_r("v", eng, ("psavB",))
                        kb.need_w("v", eng, ("avbf_o",))
                        eng.tensor_copy(bf["avbf_o"][0:64, :],
                                        ps["avA"][0:64, 512 * i2:512 * (i2 + 1)])
                        c = kb.inc("v", eng.tensor_copy(
                            bf["avbf_o"][64:128, :],
                            ps["avB"][0:64, 512 * i2:512 * (i2 + 1)]))
                        kb.mark_w(("avbf_o",), "v", c)
                        kb.mark_r(("psavA",), "v", c)
                        kb.mark_r(("psavB",), "v", c)
                    kb.sec("v", v_av)

                    def v_div(eng, hp=hp, ib=ib):
                        kb.need_r("v", eng, ("avbf_o",))
                        kb.need_r("v", eng, ("psrb2",))
                        kb.need_w("v", eng, ("att", hp, ib))
                        eng.drain()
                        c = kb.inc("v", eng.tensor_tensor(
                            bf["att"][:, hp, 512 * ib:512 * (ib + 1)], bf["avbf_o"][:, :],
                            ps["sim"][0][:, 0:512], ALU.mult))
                        kb.mark_w(("att", hp, ib), "v", c)
                        kb.mark_r(("avbf_o",), "v", c)
                        kb.mark_r(("psrb2",), "v", c)
                    kb.sec("v", v_div)
        kb.psum_close([("pssim", 0), ("pssim", 1), ("psavA",), ("psavB",),
                       ("psrb2",)])

    with ExitStack() as ctx:
        ps = dict(po=[ctx.enter_context(nc.psum_tensor(kb.uid("opo"), [128, 512], F32))
                      for _ in range(2)])

        def g_wo(eng):
            kb.need_w("g", eng, ("owg",))
            sn = kb.dsem("owg_d")
            for kc in range(8):
                kb.inc(sn, eng.dma_start(
                    out=bf["owg"][:, kc, :],
                    in_=dram["wo"][128 * kc:128 * (kc + 1), :]), 16)
            kb.mark_w(("owg",), sn, kb.cnt[sn])
        kb.sec("g", g_wo)

        for S in range(4):
            def y_x(eng, S=S):
                sn = kb.dsem("ox_d")
                kb.need_w("y", eng, ("xo",))
                c = kb.inc(sn, eng.dma_start(out=bf["xo"][:, :, :],
                                             in_=dram["x"][S, :, :, :]), 16)
                kb.mark_w(("xo",), sn, c)
            kb.sec("y", y_x)

            for mc in range(8):
                pok = ("po", mc % 2)

                def t_o(eng, S=S, mc=mc, pok=pok):
                    kb.psum_fence("t", eng)
                    kb.need_r("t", eng, ("owg",))
                    for hp in range(8):
                        kb.need_r("t", eng, ("att", hp, S))
                    kb.need_w("t", eng, pok)
                    for kc in range(8):
                        i = eng.matmul(ps["po"][mc % 2][:, :],
                                       bf["owg"][:, kc, 128 * mc:128 * (mc + 1)],
                                       bf["att"][:, kc, 512 * S:512 * (S + 1)],
                                       start=(kc == 0), stop=(kc == 7))
                    c = kb.inc("t", i)
                    kb.mark_w(pok, "t", c)
                    kb.mark_r(("owg",), "t", c)
                    for hp in range(8):
                        kb.mark_r(("att", hp, S), "t", c)
                kb.sec("t", t_o)

                def v_o(eng, S=S, mc=mc, pok=pok):
                    kb.need_r("v", eng, pok)
                    kb.need_r("v", eng, ("xo",))
                    kb.need_r("v", eng, ("bo",))
                    kb.need_w("v", eng, ("xres",))
                    c = kb.inc("v", eng.scalar_tensor_tensor(
                        bf["xres"][:, mc, 1 + 512 * S:1 + 512 * (S + 1)],
                        ps["po"][mc % 2][:, :], bf["bo"][:, mc:mc + 1],
                        bf["xo"][:, mc, :], ALU.add, ALU.add))
                    kb.mark_w(("xres",), "v", c)
                    kb.mark_r(pok, "v", c)
                    kb.mark_r(("xo",), "v", c)
                kb.sec("v", v_o)
        kb.psum_close([("po", 0), ("po", 1)])


def declare_outer_drams(nc):
    dram = {}
    EI = lambda n, s, d: nc.dram_tensor(n, s, d, kind="ExternalInput")
    dram["x"] = EI("x", [8, 128, 8, 512], F32)
    dram["wq"] = EI("wq", [1024, 1024], F32)
    dram["wk"] = EI("wk", [1024, 1024], F32)
    dram["wv"] = EI("wv", [1024, 1024], F32)
    dram["wo"] = EI("wo", [1024, 1024], F32)
    dram["bo_t"] = EI("bo_t", [128, 8], F32)
    dram["kspill"] = nc.dram_tensor("kspill", [8, 128, 4096], BF16)
    dram["xnspill"] = nc.dram_tensor("xnspill", [8, 128, 8, 512], BF16)
    dram["x1spill"] = nc.dram_tensor("x1spill", [4, 128, 8, 512], F32)
    dram["qspill"] = nc.dram_tensor("qspill", [8, 128, 2048], BF16)
    dram["vspill"] = nc.dram_tensor("vspill", [8, 128, 32, 130], BF16)
    return dram


def build_outer_test():
    nc = bass.Bass("TRN2", target_bir_lowering=False, debug=False)
    dram = declare_outer_drams(nc)
    EI = lambda n, s, d: nc.dram_tensor(n, s, d, kind="ExternalInput")
    dram["sel2t"] = EI("sel2t", [128, 2], BF16)
    dram["selA"] = EI("selA", [1, 128], F32)
    dram["selB"] = EI("selB", [1, 128], F32)
    out = nc.dram_tensor("out", [128, 8, T], F32, kind="ExternalOutput")

    with ExitStack() as ctx:
        block = ctx.enter_context(nc.Block())
        kb = KB(nc, block, ctx)
        bf = alloc_common_bufs(nc, ctx)
        alloc_persist_bufs(nc, ctx, bf)
        alloc_outer_bufs(nc, ctx, bf)
        emit_init_consts(kb, bf)

        def y_c(eng):
            sn = kb.dsem("const_d")
            for dst, srcn in ((bf["sel2t"], "sel2t"), (bf["selA"], "selA"),
                              (bf["selB"], "selB")):
                kb.inc(sn, eng.dma_start(out=dst[:, :], in_=dram[srcn][:, :]), 16)
            kb.mark_w(("consts",), sn, kb.cnt[sn])
        kb.sec("y", y_c)

        emit_outer(kb, bf, dram)
        emit_outer_attn(kb, bf, dram)

        def y_out(eng):
            kb.need_r("y", eng, ("xres",))
            sn = kb.dsem("out_d")
            kb.inc(sn, eng.dma_start(out=out[:, :, :], in_=bf["xres"][:, :, 1:T + 1]), 16)
            kb.wait("y", eng, sn, kb.cnt[sn])
        kb.sec("y", y_out)
    return nc


def declare_ffn_drams(nc):
    dram = {}
    EI = lambda n, s, d: nc.dram_tensor(n, s, d, kind="ExternalInput")
    dram["w1"] = EI("w1", [6, 1024, 5460], F32)
    dram["w2"] = EI("w2", [6, DI, 1024], F32)
    dram["lngr"] = EI("lngr", [6, 8, 128], F32)
    dram["lng_t"] = EI("lng_t", [6, 128, 8], F32)
    dram["lnbr"] = EI("lnbr", [6, 8, 128], F32)
    dram["b1a"] = EI("b1a", [6, 128, NC_DI], F32)
    dram["b1g"] = EI("b1g", [6, 128, NC_DI], F32)
    dram["b2_t"] = EI("b2_t", [6, 128, 8], F32)
    return dram


def build_full(nranks=2, depth=DEPTH, attn_only=False, debug_taps=False):
    rec = Rec()
    _build_full_pass(nranks, depth, attn_only, debug_taps, "record", rec)
    return _build_full_pass(nranks, depth, attn_only, debug_taps, "replay", rec)


def _build_full_pass(nranks, depth, attn_only, debug_taps, mode, rec):
    global CCGROUPS
    if nranks == 1:
        CCGROUPS = [[0]]
    elif nranks == 2:
        CCGROUPS = [[0, 1]]
    else:
        CCGROUPS = [[0, 1], [2, 3], [4, 5], [6, 7]]
    nc = bass.Bass("TRN2", target_bir_lowering=False, debug=False)
    dram = {}
    dram.update(declare_outer_drams(nc))
    dram.update(declare_attn_drams(nc, nranks=min(nranks, 2)))
    dram.update(declare_ffn_drams(nc))
    out = nc.dram_tensor("out", [128, 8, T], F32, kind="ExternalOutput")

    with ExitStack() as ctx:
        block = ctx.enter_context(nc.Block())
        kb = KB(nc, block, ctx, mode=mode, rec=rec)
        bf = alloc_common_bufs(nc, ctx)
        alloc_persist_bufs(nc, ctx, bf)
        emit_init_consts(kb, bf)
        emit_attn_consts(kb, bf, dram)

        mscq, msck = _table_maxima()
        with nc.psum_tensor(kb.uid("ps_nb"), [128, 512], F32) as ps_nb:
            bf["ps_nb"] = ps_nb
            emit_negB(kb, bf, dram, mscq, msck)
            kb.psum_close([("ps_nb",)])

        with ExitStack() as octx:
            alloc_outer_bufs(nc, octx, bf)
            emit_outer(kb, bf, dram)
            emit_outer_attn(kb, bf, dram)
            if depth > 0:
                emit_ln2_and_spill(kb, bf, dram)
        kb.phase_barrier()

        for l in range(depth):
            with ExitStack() as actx:
                alloc_attn_bufs(nc, actx, bf)
                emit_attn_layer(kb, bf, dram, l)
            kb.phase_barrier()
            if attn_only and l == depth - 1:
                break
            with ExitStack() as fctx:
                alloc_ffn_bufs(nc, fctx, bf)
                emit_ffn_loads(kb, bf, dram, l)
                emit_ffn(kb, bf, dram, l)
            kb.phase_barrier()

        if depth > 0:
            emit_final_add(kb, bf, dram)
            kb.phase_barrier()

        if debug_taps:
            xn_d = nc.dram_tensor("xn_d", [128, 8, 512], BF16, kind="ExternalOutput")
            h_d = nc.dram_tensor("h_d", [128, 21, 512], BF16, kind="ExternalOutput")

            def y_tap(eng):
                for hc in range(21):
                    kb.need_r("y", eng, ("h", hc))
                for kc in range(8):
                    kb.need_r("y", eng, ("xn", kc))
                sn = kb.dsem("tap_d")
                kb.inc(sn, eng.dma_start(out=xn_d[:, :, :], in_=bf["xn"][:, :, :]), 16)
                kb.inc(sn, eng.dma_start(out=h_d[:, :, :], in_=bf["h"][:, 0:21, :]), 16)
                kb.wait("y", eng, sn, kb.cnt[sn])
            kb.sec("y", y_tap)

        def y_out(eng):
            kb.need_r("y", eng, ("xres",))
            sn = kb.dsem("out_d")
            kb.inc(sn, eng.dma_start(out=out[:, :, :], in_=bf["xres"][:, :, 1:T + 1]), 16)
            kb.wait("y", eng, sn, kb.cnt[sn])
        kb.sec("y", y_out)
        if mode == "replay":
            kb.flush()
    return nc


_NC_CACHE = {}

_INPUT_NAMES = [
    "x", "wq", "wk", "wv", "wo", "bo_t", "wqkv", "wo_i", "w1", "w2",
    "lngr", "lng_t", "lnbr", "b1a", "b1g", "b2_t", "csq", "ssq", "csk", "ssk",
    "qs_m", "qs_p", "ks_m", "ks_p", "qs_r", "ks_r",
    "p128", "idn", "masku", "maskl", "sel2t", "sel2", "selA", "selB",
    "flag", "hmask",
]


def kernel(**inputs):
    in_maps = stage_inputs(inputs)
    in_maps = [{k: m[k] for k in _INPUT_NAMES} for m in in_maps]
    if "full" not in _NC_CACHE:
        _NC_CACHE["full"] = build_full(nranks=8)
    nc = _NC_CACHE["full"]
    res = run_bass_kernel_spmd(nc, in_maps, core_ids=list(range(8)))
    return unstage_output(res.results)


def emit_ln2_and_spill(kb, bf, dram):
    nc = kb.nc
    with ExitStack() as ctx:
        ps = dict(
            st=ctx.enter_context(nc.psum_tensor(kb.uid("l2st"), [128, 512], F32)),
            rg=ctx.enter_context(nc.psum_tensor(kb.uid("l2rg"), [128, 512], F32)),
            nm=ctx.enter_context(nc.psum_tensor(kb.uid("l2nm"), [128, 512], F32)),
        )
        for S in range(4):
            col0 = 1 + 512 * S

            def g_sp(eng, S=S):
                sn = kb.dsem("x1sp_d")
                kb.need_r("g", eng, ("xres",))
                c = kb.inc(sn, eng.dma_start(out=dram["x1spill"][S, :, :, :],
                                             in_=bf["xres"][:, :, col0:col0 + 512]), 16)
                kb.mark_r(("xres",), sn, c)
                kb.mark_w(("x1sp", S), sn, c)
            kb.sec("g", g_sp)

            def v_cast(eng, S=S):
                kb.need_w("v", eng, ("xnb",))
                kb.need_r("v", eng, ("xres",))
                c = kb.inc("v", eng.tensor_copy(bf["xnb"][:, :, :],
                                                bf["xres"][:, :, col0:col0 + 512]))
                kb.mark_w(("xnb",), "v", c)
                kb.mark_r(("xres",), "v", c)
            kb.sec("v", v_cast)

            def s_sq(eng):
                kb.need_w("s", eng, ("xsqo",))
                kb.need_r("s", eng, ("xnb",))
                c = kb.inc("s", eng.activation(bf["xsqo"][:, :, 0:512], bf["xnb"][:, :, :],
                                               AF.Square))
                kb.mark_w(("xsqo",), "s", c)
                kb.mark_r(("xnb",), "s", c)
            kb.sec("s", s_sq)

            def t_st(eng):
                kb.psum_fence("t", eng)
                kb.need_w("t", eng, ("ps_st",))
                kb.need_r("t", eng, ("xnb",))
                kb.need_r("t", eng, ("xsqo",))
                kb.need_r("t", eng, ("ones128",))
                for kc in range(8):
                    eng.matmul(ps["st"][0:1, :], bf["ones128"][:, :], bf["xnb"][:, kc, :],
                               start=(kc == 0), stop=(kc == 7), tile_position=(0, 0))
                for kc in range(8):
                    i = eng.matmul(ps["st"][32:33, :], bf["ones128"][:, :],
                                   bf["xsqo"][:, kc, 0:512], start=(kc == 0), stop=(kc == 7),
                                   tile_position=(0, 32))
                c = kb.inc("t", i)
                kb.mark_w(("ps_st",), "t", c)
                kb.mark_r(("xnb",), "t", c)
                kb.mark_r(("xsqo",), "t", c)
            kb.sec("t", t_st)

            st = bf["st"]
            def v_f3a(eng):
                kb.need_r("v", eng, ("ps_st",))
                kb.need_w("v", eng, ("st",))
                eng.tensor_scalar_mul(st[0:1, 0, :], ps["st"][0:1, :], 1.0 / 1024.0)
                eng.drain()
                eng.scalar_tensor_tensor(st[0:1, 1, :], st[0:1, 0, :], -1.0, st[0:1, 0, :],
                                         ALU.mult, ALU.mult)
                eng.drain()
                c = kb.inc("v", eng.scalar_tensor_tensor(
                    st[0:1, 2, :], ps["st"][32:33, :], 1.0 / 1024.0, st[0:1, 1, :],
                    ALU.mult, ALU.add))
                kb.mark_r(("ps_st",), "v", c)
                kb.mark_w(("st",), "v", c)
            kb.sec("v", v_f3a)

            def s_f3(eng):
                kb.need_r("s", eng, ("st",))
                kb.need_r("s", eng, ("eps",))
                eng.activation(st[0:1, 1, :], st[0:1, 2, :], AF.Ln, bias=bf["eps"][0:1, :])
                eng.drain()
                c = kb.inc("s", eng.activation(st[0:1, 2, :], st[0:1, 1, :], AF.Exp, scale=-0.5))
                kb.mark_w(("st",), "s", c)
            kb.sec("s", s_f3)

            def v_f3b(eng):
                kb.need_r("v", eng, ("st",))
                c = kb.inc("v", eng.scalar_tensor_tensor(
                    st[0:1, 1, :], st[0:1, 0, :], -1.0, st[0:1, 2, :], ALU.mult, ALU.mult))
                kb.mark_w(("st",), "v", c)
            kb.sec("v", v_f3b)

            def t_bc(eng):
                kb.need_r("t", eng, ("st",))
                kb.need_r("t", eng, ("ones1",))
                kb.need_w("t", eng, ("ps_rg",))
                kb.need_w("t", eng, ("ps_nm",))
                eng.matmul(ps["rg"][:, :], bf["ones1"][:, :], st[0:1, 2, :],
                           start=True, stop=True)
                c = kb.inc("t", eng.matmul(ps["nm"][:, :], bf["ones1"][:, :], st[0:1, 1, :],
                                           start=True, stop=True))
                kb.mark_w(("ps_rg",), "t", c)
                kb.mark_w(("ps_nm",), "t", c)
                kb.mark_r(("st",), "t", c)
            kb.sec("t", t_bc)

            for kc in range(8):
                def v_xn(eng, kc=kc, S=S, col0=col0):
                    kb.need_r("v", eng, ("ps_rg",))
                    kb.need_r("v", eng, ("ps_nm",))
                    kb.need_r("v", eng, ("x1sp", S))
                    kb.need_w("v", eng, ("tmpo",))
                    if kc == 0:
                        kb.need_w("v", eng, ("xres",))
                    eng.tensor_tensor(bf["to1"][:, :], bf["xres"][:, kc, col0:col0 + 512],
                                      ps["rg"][:, :], ALU.mult)
                    eng.drain()
                    c = kb.inc("v", eng.tensor_tensor(
                        bf["xres"][:, kc, col0:col0 + 512], bf["to1"][:, :],
                        ps["nm"][:, :], ALU.add))
                    kb.mark_r(("ps_rg",), "v", c)
                    kb.mark_r(("ps_nm",), "v", c)
                    kb.mark_w(("xres",), "v", c)
                    kb.mark_w(("tmpo",), "v", c)
                kb.sec("v", v_xn)
        kb.psum_close([("ps_st",), ("ps_rg",), ("ps_nm",)])


def emit_final_add(kb, bf, dram):
    nc = kb.nc
    with ExitStack() as ctx:
        x1b = ctx.enter_context(nc.sbuf_tensor(_uname("x1b"), [128, 8, 512], F32))
        for S in range(4):
            col0 = 1 + 512 * S

            def y_ld(eng, S=S):
                sn = kb.dsem("x1l_d")
                kb.need_w("y", eng, ("x1b",))
                kb.need_r("y", eng, ("x1sp", S))
                c = kb.inc(sn, eng.dma_start(out=x1b[:, :, :],
                                             in_=dram["x1spill"][S, :, :, :]), 16)
                kb.mark_w(("x1b",), sn, c)
            kb.sec("y", y_ld)

            def v_add(eng, S=S, col0=col0):
                kb.need_r("v", eng, ("x1b",))
                kb.need_w("v", eng, ("xres",))
                c = kb.inc("v", eng.tensor_tensor(
                    bf["xres"][:, :, col0:col0 + 512], bf["xres"][:, :, col0:col0 + 512],
                    x1b[:, :, :], ALU.add))
                kb.mark_w(("xres",), "v", c)
                kb.mark_r(("x1b",), "v", c)
            kb.sec("v", v_add)
        kb.wflush()


# version 10
# speedup vs baseline: 1.0078x; 1.0078x over previous
import types

import numpy as np
import ml_dtypes
from contextlib import ExitStack

import concourse.bass as bass
import concourse.mybir as mybir
from concourse.bass_utils import run_bass_kernel_spmd

F32 = mybir.dt.float32
_NAMEC = [0]


def _uname(n):
    _NAMEC[0] += 1
    return f"sb_{n}_{_NAMEC[0]}"
BF16 = mybir.dt.bfloat16
AF = mybir.ActivationFunctionType
ALU = mybir.AluOpType
BF = ml_dtypes.bfloat16

BATCH, L, D = 4, 4096, 1024
T = 2048
NH, HD = 16, 64
NP = 8
DEPTH = 6
DI = 2730
NC_DI = 22
CW_LAST = DI - 21 * 128
WINDOW = 256
NEG = -1e30

NSUB = 4
SUB = 512
NKCOL = T // 128 + 2
RING = 6
CCGROUPS = [[0, 1], [2, 3], [4, 5], [6, 7]]


def _rotary_host():
    n = L
    half = np.arange(0, HD, 2, dtype=np.float64)
    inv_freq = 1.0 / 10000.0 ** (half / HD)
    t = np.arange(n, dtype=np.float64)
    freqs = t[:, None] * inv_freq[None, :]
    freqs = np.concatenate([freqs, freqs], -1)
    base_scale = (half + 0.4 * HD) / (1.4 * HD)
    power = (t - n // 2) / (WINDOW // 2)
    scale = base_scale[None, :] ** power[:, None]
    scale = np.concatenate([scale, scale], -1)
    cos = np.cos(freqs)
    sin = np.sin(freqs)
    rsign = np.where(np.arange(HD) < 32, -1.0, 1.0)
    QK = 8.0
    cosT = cos.T
    sinT = sin.T
    scT = scale.T
    csq = QK * cosT * scT
    ssq = QK * rsign[:, None] * sinT * scT
    csk = cosT / scT
    ssk = rsign[:, None] * sinT / scT
    out = []
    for tab in (csq, ssq, csk, ssk):
        dup = np.concatenate([tab, tab], 0)
        halves = np.stack([dup[:, :T], dup[:, T:]], 0)
        out.append(np.ascontiguousarray(halves.astype(np.float32)))
    return out


def _perm128():
    P = np.zeros((128, 128), dtype=np.float32)
    for m in range(128):
        k = (m + 32) % 64 + 64 * (m // 64)
        P[k, m] = 1.0
    return P.astype(BF)


def _tri_masks():
    c = np.arange(128)
    p = np.arange(128)[:, None]
    masku = np.where(p < c, NEG, 0.0).astype(BF)
    maskl = np.where(p > c, NEG, 0.0).astype(BF)
    return masku, maskl


def _hsel():
    hsel8 = np.zeros((128, 8, 16), dtype=np.float32)
    hselb = np.zeros((8, 16, 128), dtype=np.float32)
    for mc in range(8):
        for r in range(128):
            hsel8[r, mc, 2 * mc + (r >= 64)] = 1.0
            hselb[mc, 2 * mc + (r >= 64), r] = 1.0
    return hsel8.astype(BF), np.ascontiguousarray(hselb.transpose(1, 0, 2)).astype(BF)


def _sel2():
    s = np.zeros((2, 128), dtype=np.float32)
    s[0, :64] = 1.0
    s[1, 64:] = 1.0
    return s


def stage_inputs(inputs):
    x = np.asarray(inputs["L0_attn_x"])
    csq, ssq, csk, ssk = _rotary_host()
    p128 = _perm128()
    masku, maskl = _tri_masks()
    hsel8, hselb = _hsel()
    sel2 = _sel2()
    idn = np.eye(128, dtype=np.float32).astype(BF)
    onesrow = np.ones((1, 512), dtype=np.float32)

    qs = np.asarray(inputs["L0_ffn_q_scale"])
    ks = np.asarray(inputs["L0_ffn_k_scale"])
    def dup(v):
        return np.ascontiguousarray(np.tile(v, (1, 2))[:, :, None])
    def dup_perm(v):
        vp = v[:, (np.arange(64) + 32) % 64]
        return dup(vp)

    lng = np.asarray(inputs["L0_ffn_ln_g"])
    lnb = np.asarray(inputs["L0_ffn_ln_b"])
    b1 = np.asarray(inputs["L0_ffn_b1"])
    b2 = np.asarray(inputs["L0_ffn_b2"])
    bo = np.asarray(inputs["L0_attn_bo"])

    def featcol(v):
        return np.ascontiguousarray(
            v.reshape(v.shape[:-1] + (8, 128)).swapaxes(-1, -2))

    b1pad = np.zeros((6, 2, 22 * 128), dtype=np.float32)
    b1pad[:, 0, :DI] = b1[:, :DI]
    b1pad[:, 1, :DI] = b1[:, DI:]
    b1a = np.ascontiguousarray(
        b1pad[:, 0].reshape(6, 22, 128).swapaxes(-1, -2))
    b1g = np.ascontiguousarray(
        b1pad[:, 1].reshape(6, 22, 128).swapaxes(-1, -2))

    lngr = np.ascontiguousarray(lng.reshape(6, 8, 128))
    lnbr = np.ascontiguousarray(lnb.reshape(6, 8, 128))

    common = dict(
        wq=np.asarray(inputs["L0_attn_Wq"]), wk=np.asarray(inputs["L0_attn_Wk"]),
        wv=np.asarray(inputs["L0_attn_Wv"]), wo=np.asarray(inputs["L0_attn_Wo"]),
        bo_t=featcol(bo), wqkv=np.asarray(inputs["L0_ffn_Wqkv"]),
        wo_i=np.asarray(inputs["L0_ffn_Wo"]), w1=np.asarray(inputs["L0_ffn_W1"]),
        w2=np.asarray(inputs["L0_ffn_W2"]),
        qs_m=dup(qs), qs_p=dup_perm(qs), ks_m=dup(ks), ks_p=dup_perm(ks),
        lng_t=featcol(lng), lngr=lngr, lnbr=lnbr,
        b1a=b1a, b1g=b1g, b2_t=featcol(b2),
        p128=p128, masku=masku, maskl=maskl, idn=idn,
        sel2=sel2.astype(BF), sel2t=np.ascontiguousarray(sel2.T).astype(BF),
        selA=np.ascontiguousarray(sel2[0:1]), selB=np.ascontiguousarray(sel2[1:2]),
        qs_r=qs, ks_r=ks, onesrow=onesrow,
    )

    in_maps = []
    for c in range(8):
        b, h = c // 2, c % 2
        xb = x[b]
        xT = np.ascontiguousarray(xb.T)
        blocks = []
        order = list(range(4 * h, 4 * h + 4)) + list(range(4 * (1 - h), 4 * (1 - h) + 4))
        xr = xT.reshape(8, 128, 8, 512)
        for blk in order:
            blocks.append(np.ascontiguousarray(xr[:, :, blk, :].swapaxes(0, 1)))
        xs = np.stack(blocks, 0)
        m = dict(common)
        m["x"] = xs
        m["csq"] = csq[h]; m["ssq"] = ssq[h]; m["csk"] = csk[h]; m["ssk"] = ssk[h]
        m["flag"] = np.full((128, 1), float(h), dtype=np.float32)
        m["hmask"] = np.full((128, 1), 0.0 if h else -1e30, dtype=np.float32)
        in_maps.append(m)
    return in_maps


def unstage_output(results):
    out = np.empty((BATCH, L, D), dtype=np.float32)
    for c, r in enumerate(results):
        b, h = c // 2, c % 2
        o = np.asarray(r["out"])
        xT = o.transpose(1, 0, 2).reshape(D, T)
        out[b, 2048 * h: 2048 * h + 2048, :] = xT.T
    return out


class Rec:

    def __init__(self):
        self.fifo = {e: [] for e in ("t", "v", "s", "g", "y")}
        self.pos = {e: 0 for e in ("t", "v", "s", "g", "y")}
        self.sem_order = []
        self.barrier_dmasems = []


class KB:

    def __init__(self, nc, block, ctx, mode="direct", rec=None):
        self.nc = nc
        self.block = block
        self.ctx = ctx
        self.mode = mode
        self.rec = rec
        self._uid = 0
        self.cnt = {}
        self.sems = {}
        self.hwm = {e: {} for e in ("t", "v", "s", "g", "y")}
        self.last_w = {}
        self.readers = {}
        if mode == "replay":
            self.window = {e: [] for e in ("t", "v", "s", "g", "y")}
            for name in rec.sem_order:
                self.sems[name] = ctx.enter_context(nc.semaphore("sem_" + name))
                self.cnt[name] = 0
        else:
            for name in ("t", "v", "s", "g", "y", "cc"):
                self._mksem(name)

    def _mksem(self, name):
        self.sems[name] = self.ctx.enter_context(self.nc.semaphore("sem_" + name))
        self.cnt[name] = 0
        if self.mode == "record":
            self.rec.sem_order.append(name)

    def dsem(self, name):
        if name not in self.sems:
            self._mksem(name)
        return name

    def _wcompute(self, eng, sem_key, val):
        if val <= 0:
            return None
        h = self.hwm[eng]
        if h.get(sem_key, 0) >= val:
            return None
        h[sem_key] = val
        return (sem_key, val)

    def _emit(self, eng, engobj, pairs):
        if self.mode == "record":
            self.rec.fifo[eng].append(tuple(pairs))
        for sem_key, val in pairs:
            engobj.wait_ge(self.sems[sem_key], val)

    def _replay(self, eng, engobj):
        r = self.rec
        pairs = r.fifo[eng][r.pos[eng]]
        r.pos[eng] += 1
        for sem_key, val in pairs:
            engobj.wait_ge(self.sems[sem_key], val)

    def wflush(self):
        if self.mode != "replay":
            return
        for e in ("t", "v", "s", "g", "y"):
            fns = self.window[e]
            if not fns:
                continue
            self.window[e] = []
            def runner(eng, fns=fns):
                for fn in fns:
                    fn(eng)
            getattr(self.block, {"t": "tensor", "v": "vector", "s": "scalar",
                                 "g": "gpsimd", "y": "sync"}[e])(runner)

    def flush(self):
        assert self.mode == "replay"
        self.wflush()
        for e in ("t", "v", "s", "g", "y"):
            assert self.rec.pos[e] == len(self.rec.fifo[e]), \
                f"replay misalignment on {e}: {self.rec.pos[e]} != {len(self.rec.fifo[e])}"

    def phase_barrier(self):
        self._barn = getattr(self, "_barn", 0) + 1
        k = self._barn
        if "bar" not in self.sems:
            self._mksem("bar")
        bar = self.sems["bar"]
        if self.mode == "replay":
            dma_sems = self.rec.barrier_dmasems[k - 1]
        else:
            dma_sems = [n for n in self.sems
                        if n not in ("t", "v", "s", "g", "y", "bar")]
            if self.mode == "record":
                self.rec.barrier_dmasems.append(list(dma_sems))
        def mk(e):
            def f(eng):
                if e == "y":
                    for snn in dma_sems:
                        self.wait("y", eng, snn, self.cnt[snn])
                eng.drain()
                eng.nop().then_inc(bar, 1)
                eng.wait_ge(bar, 5 * k)
            return f
        for e in ("t", "v", "s", "g", "y"):
            self.sec(e, mk(e))
        self.cnt["bar"] = 5 * k
        for e in self.hwm:
            for snn, cv in self.cnt.items():
                if snn != "bar":
                    self.hwm[e][snn] = cv
            self.hwm[e]["bar"] = 5 * k
        self._pend_psum = []
        if self.mode == "replay":
            self.wflush()

    def psum_close(self, keys):
        if self.mode == "replay":
            self.wflush()
            return
        pend = getattr(self, "_pend_psum", [])
        for k in keys:
            w = self.last_w.pop(k, None)
            if w:
                pend.append(w)
            pend.extend(self.readers.pop(k, []))
        self._pend_psum = pend

    def psum_fence(self, eng, engobj):
        if self.mode == "replay":
            self._replay(eng, engobj)
            return
        pairs = []
        for sem, cnt in getattr(self, "_pend_psum", []):
            p = self._wcompute(eng, sem, cnt)
            if p:
                pairs.append(p)
        self._pend_psum = []
        self._emit(eng, engobj, pairs)

    def uid(self, base):
        self._uid += 1
        return f"{base}_{self._uid}"

    def sec(self, eng, fn):
        if self.mode == "replay":
            if fn.__closure__ is not None:
                cells = tuple(types.CellType(c.cell_contents)
                              for c in fn.__closure__)
                fn = types.FunctionType(fn.__code__, fn.__globals__,
                                        fn.__name__, fn.__defaults__, cells)
            self.window[eng].append(fn)
            return
        getattr(self.block, {"t": "tensor", "v": "vector", "s": "scalar",
                             "g": "gpsimd", "y": "sync"}[eng])(fn)

    def inc(self, eng_key, instr, n=1):
        self.cnt[eng_key] += n
        instr.then_inc(self.sems[eng_key], n)
        return self.cnt[eng_key]

    def wait(self, eng, engobj, sem_key, val):
        if self.mode == "replay":
            self._replay(eng, engobj)
            return
        p = self._wcompute(eng, sem_key, val)
        self._emit(eng, engobj, [p] if p else [])

    def mark_w(self, key, sem, cnt):
        if self.mode == "replay":
            return
        self.last_w[key] = (sem, cnt)
        self.readers[key] = []

    def mark_r(self, key, sem, cnt):
        if self.mode == "replay":
            return
        self.readers.setdefault(key, []).append((sem, cnt))

    def need_r(self, eng, engobj, key):
        if self.mode == "replay":
            self._replay(eng, engobj)
            return
        pairs = []
        w = self.last_w.get(key)
        if w:
            p = self._wcompute(eng, w[0], w[1])
            if p:
                pairs.append(p)
        self._emit(eng, engobj, pairs)

    def need_w(self, eng, engobj, key):
        if self.mode == "replay":
            self._replay(eng, engobj)
            return
        pairs = []
        w = self.last_w.get(key)
        if w:
            p = self._wcompute(eng, w[0], w[1])
            if p:
                pairs.append(p)
        for sem, cnt in self.readers.get(key, []):
            p = self._wcompute(eng, sem, cnt)
            if p:
                pairs.append(p)
        self._emit(eng, engobj, pairs)


def _ceil(a, b):
    return -(-a // b)


def mk_precast_chunks(dram):
    chunks = []
    for l in range(6):
        chunks.append((dram["wqkvb"][l, 0:512, :], dram["wqkv"][l, 0:512, :],
                       ("wqkvb", l, 0)))
        chunks.append((dram["wqkvb"][l, 512:1024, :], dram["wqkv"][l, 512:1024, :],
                       ("wqkvb", l, 1)))
        chunks.append((dram["wob"][l, :, :], dram["wo_i"][l, :, :], ("wob", l)))
        for j in range(4):
            chunks.append((dram["w1b"][l, 256 * j:256 * (j + 1), :],
                           dram["w1"][l, 256 * j:256 * (j + 1), :], ("w1b", l, j)))
        chunks.append((dram["w2b"][l, 0:1365, :], dram["w2"][l, 0:1365, :],
                       ("w2b", l, 0)))
        chunks.append((dram["w2b"][l, 1365:2730, :], dram["w2"][l, 1365:2730, :],
                       ("w2b", l, 1)))
    return chunks


def emit_precast_step(kb, chunks, n=1):
    for _ in range(n):
        if not chunks:
            return
        dst, src, key = chunks.pop(0)

        def g(eng, dst=dst, src=src, key=key):
            sn = kb.dsem("pc_d")
            c = kb.inc(sn, eng.dma_start(out=dst, in_=src), 16)
            kb.mark_w(key, sn, c)
        kb.sec("g", g)


def emit_ffn_loads(kb, bf, dram, l):
    def y(eng):
        for dst, src in (
            (bf["lng"][:, :], dram["lng_t"][l, :, :]),
            (bf["b1a"][:, :], dram["b1a"][l, :, :]),
            (bf["b1g"][:, :], dram["b1g"][l, :, :]),
            (bf["b2"][:, :], dram["b2_t"][l, :, :]),
        ):
            key = ("ffnpar",)
            kb.need_w("y", eng, key)
            c = kb.inc(kb.dsem("ffnpar_d"), eng.dma_start(out=dst, in_=src), 16)
        kb.mark_w(("ffnpar",), "ffnpar_d", kb.cnt["ffnpar_d"])
    kb.sec("y", y)

    def g_lnbr(eng):
        kb.need_w("g", eng, ("ffnparb",))
        sn = kb.dsem("ffnparb_d")
        kb.inc(sn, eng.dma_start(
            out=bf["lnbr"][0:1, :, :], in_=dram["lnbr"][l, :, :]), 16)
        c = kb.inc(sn, eng.dma_start(
            out=bf["lngr"][0:1, :, :], in_=dram["lngr"][l, :, :]), 16)
        kb.mark_w(("ffnparb",), sn, c)
    kb.sec("g", g_lnbr)

    if "w1b" in dram:
        def yw1(eng):
            kb.need_w("y", eng, ("w1",))
            for kc in range(8):
                kb.need_r("y", eng, ("w1b", l, kc // 2))
                c = kb.inc(kb.dsem("w1_d"), eng.dma_start(
                    out=bf["w1"][:, kc, :],
                    in_=dram["w1b"][l, 128 * kc:128 * (kc + 1), :]), 16)
            kb.mark_w(("w1",), "w1_d", kb.cnt["w1_d"])
        kb.sec("y", yw1)
    else:
        def g(eng):
            kb.need_w("g", eng, ("w1",))
            for kc in range(8):
                c = kb.inc(kb.dsem("w1_d"), eng.dma_start(
                    out=bf["w1"][:, kc, :], in_=dram["w1"][l, 128 * kc:128 * (kc + 1), :]), 16)
            kb.mark_w(("w1",), "w1_d", kb.cnt["w1_d"])
        kb.sec("g", g)


def emit_ffn(kb, bf, dram, l):
    nc = kb.nc
    emit_bnd_snapshot(kb, bf)

    def mk_shift_sq(tsb):
        col0 = 1 + SUB * tsb

        def v_shift(eng, tsb=tsb, col0=col0):
            kb.need_w("v", eng, ("xsb",))
            for hc in range(8, 16):
                kb.need_w("v", eng, ("h", hc))
            kb.need_r("v", eng, ("xres",))
            eng.tensor_copy(bf["xsb"][:, 8:12, :], bf["xres"][:, 0:4, col0:col0 + 512])
            if tsb == 0:
                i = eng.tensor_copy(
                    bf["xsb"][:, 12:16, :], bf["xres"][:, 4:8, col0 - 1:col0 - 1 + 512])
            else:
                kb.need_r("v", eng, ("xbnd",))
                eng.tensor_copy(bf["xsb"][:, 12:16, 0:1], bf["xbnd"][:, 4:8, tsb - 1:tsb])
                i = eng.tensor_copy(
                    bf["xsb"][:, 12:16, 1:512], bf["xres"][:, 4:8, col0:col0 + 511])
                kb.mark_r(("xbnd",), "v", kb.cnt["v"] + 1)
            c = kb.inc("v", i)
            kb.mark_w(("xsb",), "v", c)
            kb.mark_r(("xres",), "v", c)

        def s_sq(eng):
            kb.need_w("s", eng, ("xsq",))
            for hc in range(8):
                kb.need_w("s", eng, ("h", hc))
            kb.need_r("s", eng, ("xsb",))
            c = kb.inc("s", eng.activation(bf["xsq"][:, 0:8, :], bf["xsb"][:, 8:16, :], AF.Square))
            kb.mark_w(("xsq",), "s", c)
            kb.mark_r(("xsb",), "s", c)

        return v_shift, s_sq

    for tsb in range(NSUB):
        col0 = 1 + SUB * tsb
        with ExitStack() as ps_ctx:
            ps_st = ps_ctx.enter_context(nc.psum_tensor(kb.uid("ps_st"), [128, 512], F32))
            ps_rg = ps_ctx.enter_context(nc.psum_tensor(kb.uid("ps_rg"), [128, 512], F32))
            ps_nm = [ps_ctx.enter_context(nc.psum_tensor(kb.uid("ps_nm"), [128, 512], F32))
                     for _ in range(2)]
            ps_a = [ps_ctx.enter_context(nc.psum_tensor(kb.uid("ps_a"), [128, 512], F32)) for i in range(2)]
            ps_g = [ps_ctx.enter_context(nc.psum_tensor(kb.uid("ps_g"), [128, 512], F32)) for i in range(2)]

            if tsb == 0:
                vs0, sq0 = mk_shift_sq(0)
                kb.sec("v", vs0)
                kb.sec("s", sq0)

            def t_st(eng):
                kb.psum_fence("t", eng)
                kb.need_w("t", eng, ("ps_st",))
                kb.need_r("t", eng, ("xsb",))
                kb.need_r("t", eng, ("xsq",))
                kb.need_r("t", eng, ("ones128",))
                for kc in range(8):
                    eng.matmul(ps_st[0:1, :], bf["ones128"][:, :], bf["xsb"][:, 8 + kc, :],
                               start=(kc == 0), stop=(kc == 7), tile_position=(0, 0))
                for kc in range(8):
                    i = eng.matmul(ps_st[32:33, :], bf["ones128"][:, :], bf["xsq"][:, kc, :],
                                   start=(kc == 0), stop=(kc == 7), tile_position=(0, 32))
                c = kb.inc("t", i)
                kb.mark_w(("ps_st",), "t", c)
                kb.mark_r(("xsb",), "t", c)
                kb.mark_r(("xsq",), "t", c)
            kb.sec("t", t_st)

            st = bf["st"]
            def v_f3a(eng):
                kb.need_r("v", eng, ("ps_st",))
                kb.need_w("v", eng, ("st",))
                eng.tensor_scalar_mul(st[0:1, 0, :], ps_st[0:1, :], 1.0 / 1024.0)
                eng.drain()
                eng.scalar_tensor_tensor(st[0:1, 1, :], st[0:1, 0, :], -1.0, st[0:1, 0, :],
                                         ALU.mult, ALU.mult)
                eng.drain()
                c = kb.inc("v", eng.scalar_tensor_tensor(
                    st[0:1, 2, :], ps_st[32:33, :], 1.0 / 1024.0, st[0:1, 1, :],
                    ALU.mult, ALU.add))
                kb.mark_r(("ps_st",), "v", c)
                kb.mark_w(("st",), "v", c)
            kb.sec("v", v_f3a)

            def s_f3(eng):
                kb.need_r("s", eng, ("st",))
                kb.need_r("s", eng, ("eps",))
                kb.need_w("s", eng, ("st2",))
                eng.activation(st[0:1, 1, :], st[0:1, 2, :], AF.Ln, bias=bf["eps"][0:1, :])
                eng.drain()
                c = kb.inc("s", eng.activation(bf["st2"][0:1, 0, :], st[0:1, 1, :],
                                               AF.Exp, scale=-0.5))
                kb.mark_w(("st",), "s", c)
                kb.mark_w(("st2",), "s", c)
            kb.sec("s", s_f3)

            def v_f3b(eng):
                kb.need_r("v", eng, ("st",))
                kb.need_r("v", eng, ("st2",))
                kb.need_w("v", eng, ("st2b",))
                c = kb.inc("v", eng.scalar_tensor_tensor(
                    bf["st2b"][0:1, 0, :], st[0:1, 0, :], -1.0, bf["st2"][0:1, 0, :],
                    ALU.mult, ALU.mult))
                kb.mark_w(("st2b",), "v", c)
                kb.mark_r(("st2",), "v", c)
            kb.sec("v", v_f3b)

            def t_rg(eng):
                kb.need_r("t", eng, ("st2",))
                kb.need_r("t", eng, ("ones1",))
                kb.need_w("t", eng, ("ps_rg",))
                c = kb.inc("t", eng.matmul(ps_rg[:, :], bf["ones1"][:, :],
                                           bf["st2"][0:1, 0, :], start=True, stop=True))
                kb.mark_w(("ps_rg",), "t", c)
                kb.mark_r(("st2",), "t", c)
            kb.sec("t", t_rg)

            def s_rg(eng):
                kb.need_r("s", eng, ("ps_rg",))
                kb.need_w("s", eng, ("rgs",))
                c = kb.inc("s", eng.activation(bf["rgs"][:, :], ps_rg[:, :], AF.Copy))
                kb.mark_w(("rgs",), "s", c)
                kb.mark_r(("ps_rg",), "s", c)
            kb.sec("s", s_rg)

            for kc in range(8):
                def t_f4(eng, kc=kc):
                    kb.need_r("v", eng, ("st2b",)) if False else None
                    kb.need_r("t", eng, ("st2b",))
                    kb.need_w("t", eng, ("ps_nm", kc % 2))
                    kb.need_r("t", eng, ("ffnpar",))
                    kb.need_r("t", eng, ("ffnparb",))
                    kb.need_r("t", eng, ("onesrow",))
                    eng.matmul(ps_nm[kc % 2][:, :], bf["lngr"][0:1, kc, :],
                               bf["st2b"][0:1, 0, :], start=True, stop=False)
                    c = kb.inc("t", eng.matmul(ps_nm[kc % 2][:, :], bf["lnbr"][0:1, kc, :],
                                               bf["onesrow"][:, :], start=False, stop=True))
                    kb.mark_w(("ps_nm", kc % 2), "t", c)
                    kb.mark_r(("ffnpar",), "t", c)
                    kb.mark_r(("ffnparb",), "t", c)
                    kb.mark_r(("st2b",), "t", c)
                kb.sec("t", t_f4)

                def g_f4(eng, kc=kc):
                    tmpc = bf["h"][:, 2 * (kc % 2):2 * (kc % 2) + 2, :].bitcast(
                        F32).rearrange("p a b -> p (a b)")
                    kb.need_r("g", eng, ("rgs",))
                    kb.need_r("g", eng, ("xsb",))
                    kb.need_w("g", eng, ("tmpc", kc % 2))
                    if kc < 2:
                        for hc in range(4):
                            kb.need_w("g", eng, ("h", hc))
                        kb.need_w("g", eng, ("xsq",))
                    c = kb.inc("g", eng.tensor_mul(tmpc, bf["xsb"][:, 8 + kc, :], bf["rgs"][:, :]))
                    kb.mark_w(("tmpc", kc % 2), "g", c)
                    kb.mark_r(("xsb",), "g", c)
                    kb.mark_r(("rgs",), "g", c)
                kb.sec("g", g_f4)

                def v_f4(eng, kc=kc):
                    tmpc = bf["h"][:, 2 * (kc % 2):2 * (kc % 2) + 2, :].bitcast(
                        F32).rearrange("p a b -> p (a b)")
                    kb.need_r("v", eng, ("tmpc", kc % 2))
                    kb.need_r("v", eng, ("ps_nm", kc % 2))
                    kb.need_r("v", eng, ("ffnpar",))
                    c = kb.inc("v", eng.scalar_tensor_tensor(
                        bf["xn"][:, kc, :], tmpc, bf["lng"][:, kc:kc + 1],
                        ps_nm[kc % 2][:, :], ALU.mult, ALU.add))
                    kb.mark_r(("tmpc", kc % 2), "v", c)
                    kb.mark_r(("ps_nm", kc % 2), "v", c)
                    kb.mark_w(("xn", kc), "v", c)
                kb.sec("v", v_f4)

            for c_i in range(NC_DI):
                cw = CW_LAST if c_i == NC_DI - 1 else 128
                def t_f5(eng, c_i=c_i, cw=cw):
                    kb.need_r("t", eng, ("w1",))
                    for kc in range(8):
                        kb.need_r("t", eng, ("xn", kc))
                    kb.need_w("t", eng, ("ps_a", c_i % 2))
                    kb.need_w("t", eng, ("ps_g", c_i % 2))
                    for kc in range(8):
                        eng.matmul(ps_a[c_i % 2][0:cw, :],
                                   bf["w1"][:, kc, 128 * c_i:128 * c_i + cw],
                                   bf["xn"][:, kc, :], start=(kc == 0), stop=(kc == 7))
                    for kc in range(8):
                        i = eng.matmul(ps_g[c_i % 2][0:cw, :],
                                       bf["w1"][:, kc, DI + 128 * c_i:DI + 128 * c_i + cw],
                                       bf["xn"][:, kc, :], start=(kc == 0), stop=(kc == 7))
                    c = kb.inc("t", i)
                    kb.mark_w(("ps_a", c_i % 2), "t", c)
                    kb.mark_w(("ps_g", c_i % 2), "t", c)
                    kb.mark_r(("w1",), "t", c)
                kb.sec("t", t_f5)

                def s_f5(eng, c_i=c_i, cw=cw):
                    kb.need_r("s", eng, ("ps_g", c_i % 2))
                    kb.need_w("s", eng, ("gsb", c_i % 2))
                    kb.need_r("s", eng, ("ffnpar",))
                    c = kb.inc("s", eng.activation(
                        bf["gsb"][0:cw, c_i % 2, :], ps_g[c_i % 2][0:cw, :], AF.Gelu,
                        bias=bf["b1g"][0:cw, c_i:c_i + 1]))
                    kb.mark_w(("gsb", c_i % 2), "s", c)
                    kb.mark_r(("ps_g", c_i % 2), "s", c)
                    kb.mark_r(("ffnpar",), "s", c)
                kb.sec("s", s_f5)

                def v_f5(eng, c_i=c_i, cw=cw):
                    if c_i < 2:
                        kb.need_w("v", eng, ("tmpc",))
                    if c_i < 8:
                        kb.need_w("v", eng, ("xsq",))
                    if 8 <= c_i < 16:
                        kb.need_w("v", eng, ("xsb",))
                    kb.need_r("v", eng, ("ps_a", c_i % 2))
                    kb.need_r("v", eng, ("gsb", c_i % 2))
                    kb.need_w("v", eng, ("h", c_i))
                    c = kb.inc("v", eng.scalar_tensor_tensor(
                        bf["h"][0:cw, c_i, :], ps_a[c_i % 2][0:cw, :],
                        bf["b1a"][0:cw, c_i:c_i + 1], bf["gsb"][0:cw, c_i % 2, :],
                        ALU.add, ALU.mult))
                    kb.mark_w(("h", c_i), "v", c)
                    kb.mark_r(("ps_a", c_i % 2), "v", c)
                    kb.mark_r(("gsb", c_i % 2), "v", c)
                kb.sec("v", v_f5)

        kb.psum_close([("ps_st",), ("ps_rg",), ("ps_nm", 0), ("ps_nm", 1),
                       ("ps_a", 0), ("ps_a", 1), ("ps_g", 0), ("ps_g", 1)])
        with ExitStack() as ps_ctx:
            ps_y = ps_ctx.enter_context(nc.psum_tensor(kb.uid("ps_y"), [128, 8, 512], F32))

            def t_fence(eng):
                kb.psum_fence("t", eng)
            kb.sec("t", t_fence)

            def g_w2(c_i):
                cw = CW_LAST if c_i == NC_DI - 1 else 128
                if "w2b" in dram:
                    def yld(eng):
                        kb.need_w("y", eng, ("w2", c_i % 2))
                        if 128 * c_i < 1365:
                            kb.need_r("y", eng, ("w2b", l, 0))
                        if 128 * c_i + cw > 1365:
                            kb.need_r("y", eng, ("w2b", l, 1))
                        sname = kb.dsem(f"w2_d{c_i % 2}")
                        c = kb.inc(sname, eng.dma_start(
                            out=bf["w2"][0:cw, c_i % 2, :],
                            in_=dram["w2b"][l, 128 * c_i:128 * c_i + cw, :]), 16)
                        kb.mark_w(("w2", c_i % 2), sname, c)
                    kb.sec("y", yld)
                    return
                def g(eng):
                    kb.need_w("g", eng, ("w2", c_i % 2))
                    sname = kb.dsem(f"w2_d{c_i % 2}")
                    c = kb.inc(sname, eng.dma_start(
                        out=bf["w2"][0:cw, c_i % 2, :],
                        in_=dram["w2"][l, 128 * c_i:128 * c_i + cw, :]), 16)
                    kb.mark_w(("w2", c_i % 2), sname, c)
                kb.sec("g", g)

            def t_w2(c_i):
                cw = CW_LAST if c_i == NC_DI - 1 else 128
                def t(eng):
                    kb.need_r("t", eng, ("w2", c_i % 2))
                    kb.need_r("t", eng, ("h", c_i))
                    if c_i == 0:
                        kb.need_w("t", eng, ("ps_y",))
                    for dc in range(8):
                        i = eng.matmul(ps_y[:, dc, :],
                                       bf["w2"][0:cw, c_i % 2, 128 * dc:128 * (dc + 1)],
                                       bf["h"][0:cw, c_i, :],
                                       start=(c_i == 0), stop=(c_i == NC_DI - 1))
                    c = kb.inc("t", i)
                    kb.mark_r(("w2", c_i % 2), "t", c)
                    kb.mark_r(("h", c_i), "t", c)
                    if c_i == NC_DI - 1:
                        kb.mark_w(("ps_y",), "t", c)
                kb.sec("t", t)

            g_w2(0)
            g_w2(1)
            for c_i in range(NC_DI):
                t_w2(c_i)
                if c_i + 2 < NC_DI:
                    g_w2(c_i + 2)
                if c_i == 15 and tsb + 1 < NSUB:
                    vs1, sq1 = mk_shift_sq(tsb + 1)
                    kb.sec("v", vs1)
                    kb.sec("s", sq1)

            def v_f6(eng):
                kb.need_r("v", eng, ("ps_y",))
                kb.need_w("v", eng, ("xres",))
                for dc in range(8):
                    i = eng.scalar_tensor_tensor(
                        bf["xres"][:, dc, col0:col0 + 512], ps_y[:, dc, :],
                        bf["b2"][:, dc:dc + 1], bf["xres"][:, dc, col0:col0 + 512],
                        ALU.add, ALU.add)
                c = kb.inc("v", i)
                kb.mark_w(("xres",), "v", c)
                kb.mark_r(("ps_y",), "v", c)
            kb.sec("v", v_f6)
        kb.psum_close([("ps_y",)])


def alloc_common_bufs(nc, ctx):
    bf = {}
    A = lambda name, shape, dt: ctx.enter_context(nc.sbuf_tensor(_uname(name), shape, dt))
    bf["xres"] = A("xres", [128, 8, T + 1], F32)
    bf["st"] = A("st", [1, 3, 512], F32)
    bf["ones128"] = A("ones128", [128, 1], BF16)
    bf["onesrow"] = A("onesrow", [1, 512], BF16)
    bf["eps"] = A("eps", [1, 1], F32)
    return bf


def alloc_ffn_bufs(nc, ctx, bf):
    A = lambda name, shape, dt: ctx.enter_context(nc.sbuf_tensor(_uname(name), shape, dt))
    bf["w1"] = A("w1", [128, 8, 5460], BF16)
    bf["w2"] = A("w2", [128, 2, 1024], BF16)
    bf["xn"] = A("xn", [128, 8, 512], BF16)
    bf["gsb"] = A("gsb", [128, 2, 512], BF16)
    bf["h"] = A("h", [128, NC_DI, 512], BF16)
    bf["xsq"] = bf["h"]
    bf["xsb"] = bf["h"]
    bf["tmpc"] = None
    bf["st2"] = A("st2", [1, 2, 512], F32)
    bf["lngr"] = A("lngr", [1, 8, 128], BF16)
    bf["lnbr"] = A("lnbr", [1, 8, 128], BF16)
    bf["lng"] = A("lng", [128, 8], F32)
    bf["st2b"] = A("st2b", [1, 1, 512], BF16)
    bf["rgs"] = A("rgs", [128, 512], F32)
    bf["b1a"] = A("b1a", [128, NC_DI], F32)
    bf["b1g"] = A("b1g", [128, NC_DI], F32)
    bf["b2"] = A("b2", [128, 8], F32)
    return bf


def emit_init_consts(kb, bf):
    def g_init(eng):
        c = kb.inc("g", eng.memset(bf["ones128"][:, :], 1.0))
        kb.mark_w(("ones128",), "g", c)
    kb.sec("g", g_init)

    def v_init(eng):
        eng.memset(bf["onesrow"][:, :], 1.0)
        if "ones1" in bf:
            eng.memset(bf["ones1"][:, :], 1.0)
            kb.mark_w(("ones1",), "v", kb.cnt["v"] + 1)
        c = kb.inc("v", eng.memset(bf["eps"][:, :], 1e-5))
        kb.mark_w(("onesrow",), "v", c)
        kb.mark_w(("eps",), "v", c)
    kb.sec("v", v_init)


def build_ffn_test():
    nc = bass.Bass("TRN2", target_bir_lowering=False, debug=False)
    dram = {}
    dram["w1"] = nc.dram_tensor("w1", [6, 1024, 5460], F32, kind="ExternalInput")
    dram["w2"] = nc.dram_tensor("w2", [6, DI, 1024], F32, kind="ExternalInput")
    dram["lngr"] = nc.dram_tensor("lngr", [6, 8, 128], F32, kind="ExternalInput")
    dram["lng_t"] = nc.dram_tensor("lng_t", [6, 128, 8], F32, kind="ExternalInput")
    dram["lnbr"] = nc.dram_tensor("lnbr", [6, 8, 128], F32, kind="ExternalInput")
    dram["b1a"] = nc.dram_tensor("b1a", [6, 128, NC_DI], F32, kind="ExternalInput")
    dram["b1g"] = nc.dram_tensor("b1g", [6, 128, NC_DI], F32, kind="ExternalInput")
    dram["b2_t"] = nc.dram_tensor("b2_t", [6, 128, 8], F32, kind="ExternalInput")
    x_in = nc.dram_tensor("x_in", [128, 8, T + 1], F32, kind="ExternalInput")
    out = nc.dram_tensor("out", [128, 8, T + 1], F32, kind="ExternalOutput")

    with ExitStack() as ctx:
        block = ctx.enter_context(nc.Block())
        kb = KB(nc, block, ctx)
        bf = alloc_common_bufs(nc, ctx)
        bf["xbnd"] = ctx.enter_context(nc.sbuf_tensor(_uname("xbnd"), [128, 8, 3], F32))
        bf["ones1"] = ctx.enter_context(nc.sbuf_tensor(_uname("ones1"), [1, 128], F32))
        alloc_ffn_bufs(nc, ctx, bf)
        emit_init_consts(kb, bf)

        def y_in(eng):
            c = kb.inc(kb.dsem("xin_d"), eng.dma_start(out=bf["xres"][:, :, :], in_=x_in[:, :, :]), 16)
            kb.mark_w(("xres",), "xin_d", c)
        kb.sec("y", y_in)

        emit_ffn_loads(kb, bf, dram, 0)
        emit_ffn(kb, bf, dram, 0)

        def y_out(eng):
            kb.need_r("y", eng, ("xres",))
            sn = kb.dsem("out_d")
            kb.inc(sn, eng.dma_start(out=out[:, :, :], in_=bf["xres"][:, :, :]), 16)
            kb.wait("y", eng, sn, kb.cnt[sn])
        kb.sec("y", y_out)
    return nc


def alloc_persist_bufs(nc, ctx, bf):
    A = lambda name, shape, dt: ctx.enter_context(nc.sbuf_tensor(_uname(name), shape, dt))
    bf["xrow"] = A("xrow", [128, 8], F32)
    bf["xrow2"] = A("xrow2", [128, 8], F32)
    bf["qsl"] = A("qsl", [128, 4], F32)
    bf["negB"] = A("negB", [128, 6], F32)
    bf["p128"] = A("p128", [128, 128], BF16)
    bf["idn"] = A("idn", [128, 128], BF16)
    bf["masku"] = A("masku", [128, 128], BF16)
    bf["maskl"] = A("maskl", [128, 128], BF16)
    bf["sel2t"] = A("sel2t", [128, 2], BF16)
    bf["sel2"] = A("sel2", [2, 128], BF16)
    bf["selA"] = A("selA", [1, 128], F32)
    bf["selB"] = A("selB", [1, 128], F32)
    bf["flag"] = A("flag", [128, 1], F32)
    bf["hmask"] = A("hmask", [128, 1], F32)
    bf["ones1"] = A("ones1", [1, 128], F32)
    bf["qsraw"] = A("qsraw", [1, 2, 64], F32)
    bf["xbnd"] = A("xbnd", [128, 8, 3], F32)
    return bf


def emit_bnd_snapshot(kb, bf):
    def v(eng):
        kb.need_r("v", eng, ("xres",))
        kb.need_w("v", eng, ("xbnd",))
        c = kb.inc("v", eng.tensor_copy(bf["xbnd"][:, :, :],
                                        bf["xres"][:, :, 512:1537:512]))
        kb.mark_w(("xbnd",), "v", c)
        kb.mark_r(("xres",), "v", c)
    kb.sec("v", v)


def alloc_attn_bufs(nc, ctx, bf):
    A = lambda name, shape, dt: ctx.enter_context(nc.sbuf_tensor(_uname(name), shape, dt))
    bf["wqg"] = A("wqg", [128, 8, 1024], BF16)
    bf["wkg"] = A("wkg", [128, 8, 1024], BF16)
    bf["wvg"] = A("wvg", [128, 8, 1024], BF16)
    bf["wog"] = A("wog", [128, 8, 1024], BF16)
    bf["krr"] = A("krr", [128, 8, RING * 128], BF16)
    bf["vrr"] = A("vrr", [128, RING, NH * 65], BF16)
    bf["qr"] = A("qr", [128, 8, 512], BF16)
    bf["xsba"] = A("xsba", [128, 8, 512], BF16)
    bf["avun"] = bf["xsba"]
    bf["kqbf"] = A("kqbf", [128, 512], BF16)
    bf["avbf"] = A("avbf", [128, 512], BF16)
    bf["qsq"] = A("qsq", [128, 512], BF16)
    bf["t1"] = A("t1", [128, 512], F32)
    bf["t2"] = A("t2", [128, 512], F32)
    bf["t3"] = A("t3", [128, 512], F32)
    bf["rqa"] = A("rqa", [2, 512], BF16)
    bf["rqb"] = A("rqb", [2, 512], F32)
    bf["eps30"] = A("eps30", [2, 1], F32)
    bf["recdA"] = A("recdA", [1, 512], F32)
    bf["recdB"] = A("recdB", [1, 512], F32)
    bf["expt"] = A("expt", [128, 2, 1536], BF16)
    bf["tab"] = A("tab", [128, 2, 512], F32)
    return bf


def emit_attn_consts(kb, bf, dram):
    def y(eng):
        sn = kb.dsem("const_d")
        for dst, src in (
            (bf["p128"][:, :], dram["p128"][:, :]),
            (bf["idn"][:, :], dram["idn"][:, :]),
            (bf["masku"][:, :], dram["masku"][:, :]),
            (bf["maskl"][:, :], dram["maskl"][:, :]),
            (bf["sel2t"][:, :], dram["sel2t"][:, :]),
            (bf["sel2"][:, :], dram["sel2"][:, :]),
            (bf["selA"][:, :], dram["selA"][:, :]),
            (bf["selB"][:, :], dram["selB"][:, :]),
            (bf["flag"][:, :], dram["flag"][:, :]),
            (bf["hmask"][:, :], dram["hmask"][:, :]),
        ):
            c = kb.inc(sn, eng.dma_start(out=dst, in_=src), 16)
        kb.mark_w(("consts",), sn, kb.cnt[sn])
    kb.sec("y", y)



def emit_vrr_init(kb, bf):
    def v(eng):
        kb.need_w("v", eng, ("eps30",))
        eng.memset(bf["eps30"][:, :], 1e-30)
        kb.mark_w(("eps30",), "v", kb.cnt["v"] + 1)
        for s in range(RING):
            for half in range(2):
                kb.need_w("v", eng, ("vrs", s, half))
        c = kb.inc("v", eng.memset(bf["vrr"][:, :, :], 1.0))
        for s in range(RING):
            for half in range(2):
                kb.mark_w(("vrs", s, half), "v", c)
    kb.sec("v", v)


def emit_negB(kb, bf, dram, mscq, msck):
    for l in range(DEPTH):
        def y(eng, l=l):
            sn = kb.dsem("qsl_d")
            kb.need_w("y", eng, ("qsraw",))
            kb.inc(sn, eng.dma_start(out=bf["qsraw"][0:1, 0, :], in_=dram["qs_r"][l:l + 1, :]), 16)
            c = kb.inc(sn, eng.dma_start(out=bf["qsraw"][0:1, 1, :], in_=dram["ks_r"][l:l + 1, :]), 16)
            kb.mark_w(("qsraw",), sn, c)
        kb.sec("y", y)

        def v(eng, l=l):
            kb.need_r("v", eng, ("qsraw",))
            eng.tensor_reduce(bf["st"][0:1, 0, 0:1], bf["qsraw"][0:1, 0, :],
                              mybir.AxisListType.X, ALU.max, apply_absolute_value=True)
            eng.tensor_reduce(bf["st"][0:1, 0, 1:2], bf["qsraw"][0:1, 1, :],
                              mybir.AxisListType.X, ALU.max, apply_absolute_value=True)
            eng.drain()
            c = kb.inc("v", eng.scalar_tensor_tensor(
                bf["st"][0:1, 0, 2:3], bf["st"][0:1, 0, 0:1], -8.0 * mscq * msck,
                bf["st"][0:1, 0, 1:2], ALU.mult, ALU.mult))
            kb.mark_w(("st",), "v", c)
            kb.mark_r(("qsraw",), "v", c)
        kb.sec("v", v)

        def t(eng, l=l):
            kb.need_r("t", eng, ("st",))
            kb.need_r("t", eng, ("ones1",))
            kb.need_w("t", eng, ("ps_nb",))
            c = kb.inc("t", eng.matmul(bf["ps_nb"][:, 0:1], bf["ones1"][:, :],
                                       bf["st"][0:1, 0, 2:3], start=True, stop=True))
            kb.mark_w(("ps_nb",), "t", c)
            kb.mark_r(("st",), "t", c)
        kb.sec("t", t)

        def v2(eng, l=l):
            kb.need_r("v", eng, ("ps_nb",))
            c = kb.inc("v", eng.tensor_copy(bf["negB"][:, l:l + 1], bf["ps_nb"][:, 0:1]))
            kb.mark_w(("negB",), "v", c)
            kb.mark_r(("ps_nb",), "v", c)
        kb.sec("v", v2)


def _ring_runs(kcols):
    runs = []
    i = 0
    while i < len(kcols):
        j = i
        while j + 1 < len(kcols) and (kcols[j + 1] % RING) == (kcols[j] % RING) + 1:
            j += 1
        runs.append((i, j - i + 1, kcols[i] % RING))
        i = j + 1
    return runs


def emit_attn_loads(kb, bf, dram, l):
    if "wqkvb" in dram:
        for gname, key, coff in (("wqg", "wq_g", 0), ("wkg", "wk_g", 1024),
                                 ("wvg", "wv_g", 2048)):
            def yld(eng, gname=gname, key=key, coff=coff):
                kb.need_w("y", eng, (key,))
                kb.need_r("y", eng, ("wqkvb", l, 0))
                kb.need_r("y", eng, ("wqkvb", l, 1))
                sn = kb.dsem(key + "_d")
                for kc in range(8):
                    c = kb.inc(sn, eng.dma_start(
                        out=bf[gname][:, kc, :],
                        in_=dram["wqkvb"][l, 128 * kc:128 * (kc + 1), coff:coff + 1024]), 16)
                kb.mark_w((key,), sn, kb.cnt[sn])
            kb.sec("y", yld)

        def y_wo(eng):
            kb.need_w("y", eng, ("wo_g",))
            kb.need_r("y", eng, ("wob", l))
            sn = kb.dsem("wo_g_d")
            for kc in range(8):
                c = kb.inc(sn, eng.dma_start(
                    out=bf["wog"][:, kc, :],
                    in_=dram["wob"][l, 128 * kc:128 * (kc + 1), :]), 16)
            kb.mark_w(("wo_g",), sn, kb.cnt[sn])
        kb.sec("y", y_wo)
    else:
        for gname, key, coff in (("wqg", "wq_g", 0), ("wkg", "wk_g", 1024),
                                 ("wvg", "wv_g", 2048)):
            def g(eng, gname=gname, key=key, coff=coff):
                kb.need_w("g", eng, (key,))
                sn = kb.dsem(key + "_d")
                for kc in range(8):
                    c = kb.inc(sn, eng.dma_start(
                        out=bf[gname][:, kc, :],
                        in_=dram["wqkv"][l, 128 * kc:128 * (kc + 1), coff:coff + 1024]), 16)
                kb.mark_w((key,), sn, kb.cnt[sn])
            kb.sec("g", g)

        def g_wo(eng):
            kb.need_w("g", eng, ("wo_g",))
            sn = kb.dsem("wo_g_d")
            for kc in range(8):
                c = kb.inc(sn, eng.dma_start(
                    out=bf["wog"][:, kc, :],
                    in_=dram["wo_i"][l, 128 * kc:128 * (kc + 1), :]), 16)
            kb.mark_w(("wo_g",), sn, kb.cnt[sn])
        kb.sec("g", g_wo)

    def y_qsl(eng):
        kb.need_w("y", eng, ("qsl",))
        sn = kb.dsem("qsl2_d")
        for i, nm in enumerate(("qs_m", "qs_p", "ks_m", "ks_p")):
            c = kb.inc(sn, eng.dma_start(out=bf["qsl"][:, i:i + 1], in_=dram[nm][l, :, :]), 16)
        kb.mark_w(("qsl",), sn, kb.cnt[sn])
    kb.sec("y", y_qsl)


def emit_tab_load(kb, bf, dram, kind, toff, w):
    t0, t1n = ("csq", "ssq") if kind == "q" else ("csk", "ssk")
    def y(eng):
        kb.need_w("y", eng, ("tab",))
        sn = kb.dsem("tab_d")
        kb.inc(sn, eng.dma_start(out=bf["tab"][:, 0, 0:w], in_=dram[t0][:, toff:toff + w]), 16)
        c = kb.inc(sn, eng.dma_start(out=bf["tab"][:, 1, 0:w], in_=dram[t1n][:, toff:toff + w]), 16)
        kb.mark_w(("tab",), sn, c)
    kb.sec("y", y)


def emit_shift_cast_attn(kb, bf, toff, w, bnd=None):
    def v(eng):
        kb.need_w("v", eng, ("xsba",))
        kb.need_r("v", eng, ("xres",))
        eng.tensor_copy(bf["xsba"][:, 0:4, 0:w], bf["xres"][:, 0:4, 1 + toff:1 + toff + w])
        if bnd is None:
            i = eng.tensor_copy(bf["xsba"][:, 4:8, 0:w], bf["xres"][:, 4:8, toff:toff + w])
        else:
            kb.need_r("v", eng, ("xbnd",))
            eng.tensor_copy(bf["xsba"][:, 4:8, 0:1], bf["xbnd"][:, 4:8, bnd:bnd + 1])
            i = eng.tensor_copy(bf["xsba"][:, 4:8, 1:w],
                                bf["xres"][:, 4:8, toff + 1:toff + w])
            kb.mark_r(("xbnd",), "v", kb.cnt["v"] + 1)
        c = kb.inc("v", i)
        kb.mark_w(("xsba",), "v", c)
        kb.mark_r(("xres",), "v", c)
    kb.sec("v", v)


def emit_proj_postproc(kb, bf, kind, l, toff, w, kcols, ps):
    wg = bf["wqg"] if kind == "q" else bf["wkg"]
    wkey = "wq_g" if kind == "q" else "wk_g"
    tab = bf["tab"]
    tabkey = "tab"
    s_m, s_p = (0, 1) if kind == "q" else (2, 3)
    runs = _ring_runs(kcols) if kind == "k" else None

    P, VP, R, SQ, RB, VR, GA, VR2 = [], [], [], [], [], [], [], []
    for mc in range(8):
        par = mc % 2
        pp = ps["pp"][par]
        ppk = ("pp", par)
        rot, rotk = ps["rot"][par], ("psrot", par)
        ss, ssk = ps["ss"][par], ("psss", par)
        rb, rbk = ps["rb"][par], ("psrb", par)

        def t_proj(eng, mc=mc, pp=pp, ppk=ppk):
            kb.psum_fence("t", eng)
            kb.need_r("t", eng, (wkey,))
            kb.need_r("t", eng, ("xsba",))
            kb.need_w("t", eng, ppk)
            for kc in range(8):
                i = eng.matmul(pp[:, 0:w], wg[:, kc, 128 * mc:128 * (mc + 1)],
                               bf["xsba"][:, kc, 0:w], start=(kc == 0), stop=(kc == 7))
            c = kb.inc("t", i)
            kb.mark_w(ppk, "t", c)
            kb.mark_r((wkey,), "t", c)
            kb.mark_r(("xsba",), "t", c)
        P.append(t_proj)

        def v_pp(eng, pp=pp, ppk=ppk):
            kb.need_r("v", eng, ppk)
            kb.need_w("v", eng, ("kqbf",))
            kb.need_w("v", eng, ("qsq",))
            eng.tensor_copy(bf["kqbf"][:, 0:w], pp[:, 0:w])
            eng.drain()
            c = kb.inc("v", eng.scalar_tensor_tensor(
                bf["qsq"][:, 0:w], pp[:, 0:w], 1.0, bf["kqbf"][:, 0:w],
                ALU.mult, ALU.mult))
            kb.mark_w(("kqbf",), "v", c)
            kb.mark_w(("qsq",), "v", c)
            kb.mark_r(ppk, "v", c)
        VP.append(v_pp)

        def t_rot(eng, rot=rot, rotk=rotk, ss=ss, ssk=ssk):
            kb.need_r("t", eng, ("kqbf",))
            kb.need_r("t", eng, ("qsq",))
            kb.need_r("t", eng, ("consts",))
            kb.need_w("t", eng, rotk)
            kb.need_w("t", eng, ssk)
            eng.matmul(rot[:, 0:w], bf["p128"][:, :], bf["kqbf"][:, 0:w],
                       start=True, stop=True)
            c = kb.inc("t", eng.matmul(ss[0:2, 0:w], bf["sel2t"][:, :],
                                       bf["qsq"][:, 0:w], start=True, stop=True))
            kb.mark_w(rotk, "t", c)
            kb.mark_w(ssk, "t", c)
            kb.mark_r(("kqbf",), "t", c)
            kb.mark_r(("qsq",), "t", c)
        R.append(t_rot)

        def s_rq(eng, ss=ss, ssk=ssk):
            kb.need_r("s", eng, ssk)
            kb.need_w("s", eng, ("rqa",))
            kb.need_w("s", eng, ("rqb",))
            kb.need_r("s", eng, ("eps30",))
            eng.activation(bf["rqb"][0:2, 0:w], ss[0:2, 0:w], AF.Ln,
                           bias=bf["eps30"][0:2, :])
            eng.drain()
            c = kb.inc("s", eng.activation(bf["rqa"][0:2, 0:w], bf["rqb"][0:2, 0:w],
                                           AF.Exp, scale=-0.5))
            kb.mark_w(("rqa",), "s", c)
            kb.mark_w(("rqb",), "s", c)
            kb.mark_r(ssk, "s", c)
        SQ.append(s_rq)

        def t_rb(eng, rb=rb, rbk=rbk):
            kb.need_r("t", eng, ("rqa",))
            kb.need_r("t", eng, ("consts",))
            kb.need_w("t", eng, rbk)
            c = kb.inc("t", eng.matmul(rb[:, 0:w], bf["sel2"][:, :],
                                       bf["rqa"][0:2, 0:w], start=True, stop=True))
            kb.mark_w(rbk, "t", c)
            kb.mark_r(("rqa",), "t", c)
        RB.append(t_rb)

        def v_rt(eng, mc=mc, pp=pp, ppk=ppk, rot=rot, rotk=rotk):
            kb.need_r("v", eng, ppk)
            kb.need_r("v", eng, rotk)
            kb.need_r("v", eng, (tabkey,))
            kb.need_r("v", eng, ("qsl",))
            kb.need_w("v", eng, ("t1",))
            kb.need_w("v", eng, ("t2",))
            i = eng.scalar_tensor_tensor(bf["t1"][:, 0:w], pp[:, 0:w],
                                         bf["qsl"][:, s_m:s_m + 1], tab[:, 0, 0:w],
                                         ALU.mult, ALU.mult)
            c12 = kb.inc("v", eng.scalar_tensor_tensor(
                bf["t2"][:, 0:w], rot[:, 0:w],
                bf["qsl"][:, s_p:s_p + 1], tab[:, 1, 0:w], ALU.mult, ALU.mult))
            kb.mark_w(("t1",), "v", c12)
            kb.mark_w(("t2",), "v", c12)
            kb.mark_r(ppk, "v", c12)
            kb.mark_r(rotk, "v", c12)
            kb.mark_r((tabkey,), "v", c12)
        VR.append(v_rt)

        def g_add(eng, mc=mc):
            kb.need_r("g", eng, ("t1",))
            kb.need_r("g", eng, ("t2",))
            kb.need_w("g", eng, ("t3",))
            c = kb.inc("g", eng.tensor_add(bf["t3"][:, 0:w], bf["t1"][:, 0:w], bf["t2"][:, 0:w]))
            kb.mark_w(("t3",), "g", c)
            kb.mark_r(("t1",), "g", c)
            kb.mark_r(("t2",), "g", c)
        GA.append(g_add)

        def v_rt2(eng, mc=mc, rb=rb, rbk=rbk):
            kb.need_r("v", eng, ("t3",))
            kb.need_r("v", eng, rbk)
            if kind == "q":
                i = eng.tensor_tensor(bf["qr"][:, mc, 0:w], bf["t3"][:, 0:w],
                                      rb[:, 0:w], ALU.mult)
                c = kb.inc("v", i)
                kb.mark_w(("qr",), "v", c)
            else:
                for kcol in kcols:
                    kb.need_w("v", eng, ("krs", kcol % RING, mc))
                for (toff128, n, slot0) in runs:
                    i = eng.tensor_tensor(
                        bf["krr"][:, mc, 128 * slot0:128 * (slot0 + n)],
                        bf["t3"][:, 128 * toff128:128 * (toff128 + n)],
                        rb[:, 128 * toff128:128 * (toff128 + n)], ALU.mult)
                c = kb.inc("v", i)
                for kcol in kcols:
                    kb.mark_w(("krs", kcol % RING, mc), "v", c)
            kb.mark_r(rbk, "v", c)
            kb.mark_r(("t3",), "v", c)
        VR2.append(v_rt2)

    n8 = 8
    for it in range(n8 + 2):
        if it < n8:
            kb.sec("t", P[it])
        if 0 <= it - 1 < n8:
            kb.sec("t", R[it - 1])
        if 0 <= it - 2 < n8:
            kb.sec("t", RB[it - 2])
        if 0 <= it - 1 < n8:
            kb.sec("s", SQ[it - 1])
        if it < n8:
            kb.sec("v", VP[it])
        if 0 <= it - 1 < n8:
            kb.sec("v", VR[it - 1])
        if 0 <= it - 2 < n8:
            kb.sec("v", VR2[it - 2])
        if 0 <= it - 1 < n8:
            kb.sec("g", GA[it - 1])


def emit_v_block(kb, bf, toff, w, kcols, ps):
    for ti, kcol in enumerate(kcols):
        slot = kcol % RING
        for half in range(2):
            pp = ps["pp"][(2 * ti + half) % 2]
            ppk = ("pp", (2 * ti + half) % 2)

            def t_v(eng, ti=ti, half=half, pp=pp, ppk=ppk):
                kb.need_r("t", eng, ("wv_g",))
                kb.need_r("t", eng, ("xsba",))
                kb.need_w("t", eng, ppk)
                for kc in range(8):
                    i = eng.matmul(pp[:, :], bf["xsba"][:, kc, 128 * ti:128 * (ti + 1)],
                                   bf["wvg"][:, kc, 512 * half:512 * (half + 1)],
                                   start=(kc == 0), stop=(kc == 7))
                c = kb.inc("t", i)
                kb.mark_w(ppk, "t", c)
                kb.mark_r(("wv_g",), "t", c)
                kb.mark_r(("xsba",), "t", c)
            kb.sec("t", t_v)

            def v_v(eng, slot=slot, half=half, pp=pp, ppk=ppk):
                kb.need_r("v", eng, ppk)
                kb.need_w("v", eng, ("vrs", slot, half))
                dst = bf["vrr"][:, slot, :].rearrange(
                    "p (h c) -> p h c", c=65)[:, 8 * half:8 * (half + 1), 0:64]
                src = pp[:, :].rearrange("p (h c) -> p h c", c=64)
                c = kb.inc("v", eng.tensor_copy(dst, src))
                kb.mark_w(("vrs", slot, half), "v", c)
                kb.mark_r(ppk, "v", c)
            kb.sec("v", v_v)


def emit_attn_core(kb, bf, l, S, ps):
    for hp in range(8):
        def t_sim(eng, hp=hp):
            kb.psum_fence("t", eng)
            kb.need_w("t", eng, ("psrb2",))
            kb.need_r("t", eng, ("qr",))
            kb.need_r("t", eng, ("consts",))
            kb.need_w("t", eng, ("pssim",))
            for Irel in range(4):
                I = 4 * S + Irel
                for Jrel in range(3):
                    kcol = I + Jrel
                    slot = kcol % RING
                    kb.need_r("t", eng, ("krs", slot, hp))
                    off = 384 * Irel + 128 * Jrel
                    masked = Jrel != 1
                    eng.matmul(ps["sim"][:, off:off + 128],
                               bf["krr"][0:64, hp, 128 * slot:128 * (slot + 1)],
                               bf["qr"][0:64, hp, 128 * Irel:128 * (Irel + 1)],
                               start=True, stop=not masked, tile_position=(0, 0))
                    eng.matmul(ps["sim"][:, 1536 + off:1536 + off + 128],
                               bf["krr"][64:128, hp, 128 * slot:128 * (slot + 1)],
                               bf["qr"][64:128, hp, 128 * Irel:128 * (Irel + 1)],
                               start=True, stop=not masked, tile_position=(64, 0))
                    if masked:
                        m = bf["masku"] if Jrel == 0 else bf["maskl"]
                        eng.matmul(ps["sim"][:, off:off + 128], bf["idn"][:, :],
                                   m[:, :], start=False, stop=True)
                        i = eng.matmul(ps["sim"][:, 1536 + off:1536 + off + 128],
                                       bf["idn"][:, :], m[:, :], start=False, stop=True)
                    else:
                        i = None
            c = kb.inc("t", eng.nop() if i is None else i)
            kb.mark_w(("pssim",), "t", c)
            kb.mark_r(("qr",), "t", c)
        kb.sec("t", t_sim)

        if S == 0:
            def v_flag(eng, hp=hp):
                kb.need_r("v", eng, ("pssim",))
                kb.need_r("v", eng, ("consts",))
                i = None
                for (Irel, Jrel) in ((0, 0), (0, 1), (1, 0)):
                    off = 384 * Irel + 128 * Jrel
                    for par in range(2):
                        sl = ps["sim"][:, 1536 * par + off:1536 * par + off + 128]
                        i = eng.tensor_scalar_add(sl, sl, bf["hmask"][:, 0:1])
                c = kb.inc("v", i)
                kb.mark_w(("pssim",), "v", c)
            kb.sec("v", v_flag)

        def s_exp(eng, hp=hp):
            kb.need_r("s", eng, ("pssim",))
            kb.need_r("s", eng, ("negB",))
            kb.need_w("s", eng, ("expt",))
            eng.activation(bf["expt"][:, 0, :], ps["sim"][:, 0:1536], AF.Exp,
                           bias=bf["negB"][:, l:l + 1])
            c = kb.inc("s", eng.activation(bf["expt"][:, 1, :], ps["sim"][:, 1536:3072],
                                           AF.Exp, bias=bf["negB"][:, l:l + 1]))
            kb.mark_w(("expt",), "s", c)
            kb.mark_r(("pssim",), "s", c)
        kb.sec("s", s_exp)

        def t_av(eng, hp=hp):
            kb.need_r("t", eng, ("expt",))
            kb.need_w("t", eng, ("psavA",))
            kb.need_w("t", eng, ("psavB",))
            for Irel in range(4):
                I = 4 * S + Irel
                for Jrel in range(3):
                    kcol = I + Jrel
                    slot = kcol % RING
                    for half in range(2):
                        kb.need_r("t", eng, ("vrs", slot, half))
                    off = 384 * Irel + 128 * Jrel
                    hA, hB = 2 * hp, 2 * hp + 1
                    eng.matmul(ps["avA"][0:65, 128 * Irel:128 * (Irel + 1)],
                               bf["vrr"][:, slot, 65 * hA:65 * (hA + 1)],
                               bf["expt"][:, 0, off:off + 128],
                               start=(Jrel == 0), stop=(Jrel == 2))
                    i = eng.matmul(ps["avB"][0:65, 128 * Irel:128 * (Irel + 1)],
                                   bf["vrr"][:, slot, 65 * hB:65 * (hB + 1)],
                                   bf["expt"][:, 1, off:off + 128],
                                   start=(Jrel == 0), stop=(Jrel == 2))
            c = kb.inc("t", i)
            kb.mark_w(("psavA",), "t", c)
            kb.mark_w(("psavB",), "t", c)
            kb.mark_r(("expt",), "t", c)
            for Irel in range(4):
                for Jrel in range(3):
                    slot = (4 * S + Irel + Jrel) % RING
                    for half in range(2):
                        kb.mark_r(("vrs", slot, half), "t", c)
        kb.sec("t", t_av)

        def s_rec(eng, hp=hp):
            kb.need_r("s", eng, ("psavA",))
            kb.need_r("s", eng, ("psavB",))
            kb.need_w("s", eng, ("recdA",))
            kb.need_w("s", eng, ("recdB",))
            kb.need_w("s", eng, ("t1",))
            kb.need_w("s", eng, ("t2",))
            eng.activation(bf["t1"][0:1, :], ps["avA"][64:65, :], AF.Ln)
            eng.activation(bf["t2"][0:1, :], ps["avB"][64:65, :], AF.Ln)
            eng.drain()
            eng.activation(bf["recdA"][0:1, :], bf["t1"][0:1, :], AF.Exp, scale=-1.0)
            c = kb.inc("s", eng.activation(bf["recdB"][0:1, :], bf["t2"][0:1, :],
                                           AF.Exp, scale=-1.0))
            kb.mark_w(("recdA",), "s", c)
            kb.mark_w(("recdB",), "s", c)
            kb.mark_w(("t1",), "s", c)
            kb.mark_w(("t2",), "s", c)
            kb.mark_r(("psavA",), "s", c)
            kb.mark_r(("psavB",), "s", c)
        kb.sec("s", s_rec)

        def t_dbc(eng, hp=hp):
            kb.need_r("t", eng, ("recdA",))
            kb.need_r("t", eng, ("recdB",))
            kb.need_r("t", eng, ("consts",))
            kb.need_w("t", eng, ("psrb2",))
            eng.matmul(ps["rb2"][:, :], bf["selA"][:, :], bf["recdA"][0:1, :],
                       start=True, stop=False)
            c = kb.inc("t", eng.matmul(ps["rb2"][:, :], bf["selB"][:, :],
                                       bf["recdB"][0:1, :], start=False, stop=True))
            kb.mark_w(("psrb2",), "t", c)
            kb.mark_r(("recdA",), "t", c)
            kb.mark_r(("recdB",), "t", c)
        kb.sec("t", t_dbc)

        def v_av(eng, hp=hp):
            kb.need_r("v", eng, ("psavA",))
            kb.need_r("v", eng, ("psavB",))
            kb.need_w("v", eng, ("avbf",))
            eng.tensor_copy(bf["avbf"][0:64, :], ps["avA"][0:64, :])
            c = kb.inc("v", eng.tensor_copy(bf["avbf"][64:128, :], ps["avB"][0:64, :]))
            kb.mark_w(("avbf",), "v", c)
            kb.mark_r(("psavA",), "v", c)
            kb.mark_r(("psavB",), "v", c)
        kb.sec("v", v_av)

        def v_div(eng, hp=hp):
            kb.need_r("v", eng, ("avbf",))
            kb.need_r("v", eng, ("psrb2",))
            kb.need_w("v", eng, ("xsba",))
            eng.drain()
            c = kb.inc("v", eng.tensor_tensor(bf["avun"][:, hp, :], bf["avbf"][:, :],
                                              ps["rb2"][:, :], ALU.mult))
            kb.mark_w(("xsba",), "v", c)
            kb.mark_r(("avbf",), "v", c)
            kb.mark_r(("psrb2",), "v", c)
        kb.sec("v", v_div)


def emit_outproj(kb, bf, S, ps):
    toff = SUB * S
    for mc in range(8):
        pok = ("po", mc % 2)

        def t_o(eng, mc=mc, pok=pok):
            if mc == 0:
                kb.psum_fence("t", eng)
            kb.need_r("t", eng, ("wo_g",))
            kb.need_r("t", eng, ("xsba",))
            kb.need_w("t", eng, pok)
            for kc in range(8):
                i = eng.matmul(ps["po"][mc % 2][:, :], bf["wog"][:, kc, 128 * mc:128 * (mc + 1)],
                               bf["avun"][:, kc, :], start=(kc == 0), stop=(kc == 7))
            c = kb.inc("t", i)
            kb.mark_w(pok, "t", c)
            kb.mark_r(("wo_g",), "t", c)
            kb.mark_r(("xsba",), "t", c)
        kb.sec("t", t_o)

        def v_o(eng, mc=mc, pok=pok):
            kb.need_r("v", eng, pok)
            kb.need_w("v", eng, ("xres",))
            c = kb.inc("v", eng.tensor_tensor(
                bf["xres"][:, mc, 1 + toff:1 + toff + 512], ps["po"][mc % 2][:, :],
                bf["xres"][:, mc, 1 + toff:1 + toff + 512], ALU.add))
            kb.mark_w(("xres",), "v", c)
            kb.mark_r(pok, "v", c)
        kb.sec("v", v_o)


def emit_exchange_e1(kb, bf, dram, l):
    def v_pack(eng):
        kb.need_r("v", eng, ("xres",))
        kb.need_w("v", eng, ("xrow2",))
        c = kb.inc("v", eng.tensor_copy(bf["xrow2"][:, :].unsqueeze(2),
                                        bf["xres"][:, :, T:T + 1]))
        kb.mark_w(("xrow2",), "v", c)
        kb.mark_r(("xres",), "v", c)
    kb.sec("v", v_pack)

    def g_pack(eng):
        sn = kb.dsem("e1p_d")
        kb.need_w("g", eng, ("e1io",))
        for mc in range(8):
            kb.need_r("g", eng, ("krs", 4, mc))
            kb.need_r("g", eng, ("krs", 5, mc))
        for s in (4, 5):
            for half in range(2):
                kb.need_r("g", eng, ("vrs", s, half))
        kb.need_r("g", eng, ("xres",))
        kb.inc(sn, eng.dma_start(out=dram["e1_in"][:, 0:2048], in_=bf["krr"][:, :, 512:768]), 16)
        kb.inc(sn, eng.dma_start(out=dram["e1_in"][:, 2048:4128], in_=bf["vrr"][:, 4:6, :]), 16)
        kb.need_r("g", eng, ("xrow2",))
        c = kb.inc(sn, eng.dma_start(
            out=dram["e1f_in"][:, :], in_=bf["xrow2"][:, :]), 16)
        kb.mark_r(("xrow2",), sn, c)
        for mc in range(8):
            kb.mark_r(("krs", 4, mc), sn, c)
            kb.mark_r(("krs", 5, mc), sn, c)
        for s in (4, 5):
            for half in range(2):
                kb.mark_r(("vrs", s, half), sn, c)
        kb.mark_r(("xres",), sn, c)

        eng.wait_ge(kb.sems[sn], c)
        kb.inc("cc", eng.collective_compute(
            "AllGather", ALU.bypass, replica_groups=CCGROUPS,
            ins=[dram["e1_in"][:, :].opt()], outs=[dram["e1_out"][:, :, :].opt()]))
        cc = kb.inc("cc", eng.collective_compute(
            "AllGather", ALU.bypass, replica_groups=CCGROUPS,
            ins=[dram["e1f_in"][:, :].opt()], outs=[dram["e1f_out"][:, :, :].opt()]))

        eng.wait_ge(kb.sems["cc"], cc)
        sn2 = kb.dsem("e1u_d")
        for mc in range(8):
            kb.need_w("g", eng, ("krs", 0, mc))
            kb.need_w("g", eng, ("krs", 1, mc))
        for s in (0, 1):
            for half in range(2):
                kb.need_w("g", eng, ("vrs", s, half))
        kb.need_w("g", eng, ("xrow",))
        kb.inc(sn2, eng.dma_start(out=bf["krr"][:, :, 0:256], in_=dram["e1_out"][0, :, 0:2048]), 16)
        kb.inc(sn2, eng.dma_start(out=bf["vrr"][:, 0:2, :], in_=dram["e1_out"][0, :, 2048:4128]), 16)
        c2 = kb.inc(sn2, eng.dma_start(
            out=bf["xrow"][:, :], in_=dram["e1f_out"][0, :, :]), 16)
        for mc in range(8):
            kb.mark_w(("krs", 0, mc), sn2, c2)
            kb.mark_w(("krs", 1, mc), sn2, c2)
        for s in (0, 1):
            for half in range(2):
                kb.mark_w(("vrs", s, half), sn2, c2)
        kb.mark_w(("xrow",), sn2, c2)
        kb.mark_w(("e1io",), sn2, c2)
    kb.sec("g", g_pack)

    def v_row(eng):
        kb.need_r("v", eng, ("xrow",))
        kb.need_r("v", eng, ("consts",))
        kb.need_w("v", eng, ("xres",))
        c = kb.inc("v", eng.tensor_scalar_mul(
            bf["xres"][:, :, 0:1], bf["xrow"][:, :].unsqueeze(2), bf["flag"][:, 0:1]))
        kb.mark_w(("xres",), "v", c)
        kb.mark_r(("xrow",), "v", c)
    kb.sec("v", v_row)


def emit_exchange_e2(kb, bf, dram, l):
    def v_pack(eng):
        kb.need_r("v", eng, ("xres",))
        kb.need_w("v", eng, ("xrow2",))
        c = kb.inc("v", eng.tensor_copy(bf["xrow2"][:, :].unsqueeze(2),
                                        bf["xres"][:, :, T:T + 1]))
        kb.mark_w(("xrow2",), "v", c)
        kb.mark_r(("xres",), "v", c)
    kb.sec("v", v_pack)

    def g(eng):
        sn = kb.dsem("e2p_d")
        kb.need_w("g", eng, ("e2io",))
        kb.need_r("g", eng, ("xres",))
        kb.need_r("g", eng, ("xrow2",))
        c = kb.inc(sn, eng.dma_start(
            out=dram["e2_in"][:, :], in_=bf["xrow2"][:, :]), 16)
        kb.mark_r(("xrow2",), sn, c)
        kb.mark_r(("xres",), sn, c)
        eng.wait_ge(kb.sems[sn], c)
        cc = kb.inc("cc", eng.collective_compute(
            "AllGather", ALU.bypass,
            replica_groups=CCGROUPS,
            ins=[dram["e2_in"][:, :].opt()],
            outs=[dram["e2_out"][:, :, :].opt()]))
        eng.wait_ge(kb.sems["cc"], cc)
        sn2 = kb.dsem("e2u_d")
        kb.need_w("g", eng, ("xrow",))
        c2 = kb.inc(sn2, eng.dma_start(
            out=bf["xrow"][:, :], in_=dram["e2_out"][0, :, :]), 16)
        kb.mark_w(("xrow",), sn2, c2)
        kb.mark_w(("e2io",), sn2, c2)
    kb.sec("g", g)

    def v_row(eng):
        kb.need_r("v", eng, ("xrow",))
        kb.need_w("v", eng, ("xres",))
        c = kb.inc("v", eng.tensor_scalar_mul(
            bf["xres"][:, :, 0:1], bf["xrow"][:, :].unsqueeze(2), bf["flag"][:, 0:1]))
        kb.mark_w(("xres",), "v", c)
        kb.mark_r(("xrow",), "v", c)
    kb.sec("v", v_row)


def emit_attn_layer(kb, bf, dram, l):
    nc = kb.nc
    emit_vrr_init(kb, bf)
    emit_attn_loads(kb, bf, dram, l)

    def mkps(ctx):
        return dict(
            pp=[ctx.enter_context(nc.psum_tensor(kb.uid("pp"), [128, 512], F32))
                for _ in range(2)],
            rot=[ctx.enter_context(nc.psum_tensor(kb.uid("prot"), [128, 512], F32))
                 for _ in range(2)],
            ss=[ctx.enter_context(nc.psum_tensor(kb.uid("pss"), [128, 512], F32))
                for _ in range(2)],
            rb=[ctx.enter_context(nc.psum_tensor(kb.uid("prb"), [128, 512], F32))
                for _ in range(2)],
        )
    PSK_QKV = [("pp", 0), ("pp", 1), ("psrot", 0), ("psrot", 1),
               ("psss", 0), ("psss", 1), ("psrb", 0), ("psrb", 1)]

    with ExitStack() as ctx:
        ps = mkps(ctx)
        emit_tab_load(kb, bf, dram, "k", T - 256, 256)
        emit_shift_cast_attn(kb, bf, T - 256, 256)
        emit_proj_postproc(kb, bf, "k", l, T - 256, 256, [16, 17], ps)
        emit_v_block(kb, bf, T - 256, 256, [16, 17], ps)
        kb.psum_close(PSK_QKV)
    emit_exchange_e1(kb, bf, dram, l)
    emit_bnd_snapshot(kb, bf)

    for S in range(NSUB):
        toff = SUB * S
        kcols = list(range(4 * S + 2, 4 * S + 6))
        with ExitStack() as ctx:
            ps = mkps(ctx)
            emit_tab_load(kb, bf, dram, "k", toff, SUB)
            emit_shift_cast_attn(kb, bf, toff, SUB, bnd=None if S == 0 else S - 1)
            emit_proj_postproc(kb, bf, "k", l, toff, SUB, kcols, ps)
            emit_v_block(kb, bf, toff, SUB, kcols, ps)
            emit_tab_load(kb, bf, dram, "q", toff, SUB)
            emit_proj_postproc(kb, bf, "q", l, toff, SUB, None, ps)
            kb.psum_close(PSK_QKV)
        with ExitStack() as ctx:
            ps2 = dict(
                sim=ctx.enter_context(nc.psum_tensor(kb.uid("psim"), [128, 3072], F32)),
                avA=ctx.enter_context(nc.psum_tensor(kb.uid("pavA"), [128, 512], F32)),
                avB=ctx.enter_context(nc.psum_tensor(kb.uid("pavB"), [128, 512], F32)),
            )
            ps2["rb2"] = ps2["sim"][:, 0:512]
            emit_attn_core(kb, bf, l, S, ps2)
            kb.psum_close([("pssim",), ("psavA",), ("psavB",), ("psrb2",)])
        with ExitStack() as ctx:
            ps3 = dict(po=[ctx.enter_context(
                nc.psum_tensor(kb.uid("po"), [128, 512], F32)) for _ in range(2)])
            emit_outproj(kb, bf, S, ps3)
            kb.psum_close([("po", 0), ("po", 1)])
    emit_exchange_e2(kb, bf, dram, l)
    kb.wflush()


def declare_attn_drams(nc, nranks=2):
    dram = {}
    EI = lambda n, s, d: nc.dram_tensor(n, s, d, kind="ExternalInput")
    dram["wqkv"] = EI("wqkv", [6, 1024, 3072], F32)
    dram["wo_i"] = EI("wo_i", [6, 1024, 1024], F32)
    dram["csq"] = EI("csq", [128, T], F32)
    dram["ssq"] = EI("ssq", [128, T], F32)
    dram["csk"] = EI("csk", [128, T], F32)
    dram["ssk"] = EI("ssk", [128, T], F32)
    dram["qs_m"] = EI("qs_m", [6, 128, 1], F32)
    dram["qs_p"] = EI("qs_p", [6, 128, 1], F32)
    dram["ks_m"] = EI("ks_m", [6, 128, 1], F32)
    dram["ks_p"] = EI("ks_p", [6, 128, 1], F32)
    dram["qs_r"] = EI("qs_r", [6, 64], F32)
    dram["ks_r"] = EI("ks_r", [6, 64], F32)
    dram["p128"] = EI("p128", [128, 128], BF16)
    dram["idn"] = EI("idn", [128, 128], BF16)
    dram["masku"] = EI("masku", [128, 128], BF16)
    dram["maskl"] = EI("maskl", [128, 128], BF16)
    dram["sel2t"] = EI("sel2t", [128, 2], BF16)
    dram["sel2"] = EI("sel2", [2, 128], BF16)
    dram["selA"] = EI("selA", [1, 128], F32)
    dram["selB"] = EI("selB", [1, 128], F32)
    dram["flag"] = EI("flag", [128, 1], F32)
    dram["hmask"] = EI("hmask", [128, 1], F32)
    dram["e1_in"] = nc.dram_tensor("e1_in", [128, 4128], BF16)
    dram["e1_out"] = nc.dram_tensor("e1_out", [nranks, 128, 4128], BF16)
    dram["e1f_in"] = nc.dram_tensor("e1f_in", [128, 8], F32)
    dram["e1f_out"] = nc.dram_tensor("e1f_out", [nranks, 128, 8], F32)
    dram["e2_in"] = nc.dram_tensor("e2_in", [128, 8], F32)
    dram["e2_out"] = nc.dram_tensor("e2_out", [nranks, 128, 8], F32)
    return dram


def _table_maxima():
    return 1.05, 1.0


def build_attn_test(nranks=1):
    global CCGROUPS
    CCGROUPS = [[0]] if nranks == 1 else [[0, 1]]
    nc = bass.Bass("TRN2", target_bir_lowering=False, debug=False)
    dram = declare_attn_drams(nc, nranks=nranks)
    x_in = nc.dram_tensor("x_in", [128, 8, T + 1], F32, kind="ExternalInput")
    out = nc.dram_tensor("out", [128, 8, T + 1], F32, kind="ExternalOutput")

    with ExitStack() as ctx:
        block = ctx.enter_context(nc.Block())
        kb = KB(nc, block, ctx)
        bf = alloc_common_bufs(nc, ctx)
        alloc_persist_bufs(nc, ctx, bf)
        alloc_attn_bufs(nc, ctx, bf)
        emit_init_consts(kb, bf)
        emit_attn_consts(kb, bf, dram)

        def y_in(eng):
            c = kb.inc(kb.dsem("xin_d"), eng.dma_start(out=bf["xres"][:, :, :], in_=x_in[:, :, :]), 16)
            kb.mark_w(("xres",), "xin_d", c)
        kb.sec("y", y_in)

        mscq, msck = _table_maxima()
        with nc.psum_tensor(kb.uid("ps_nb"), [128, 512], F32) as ps_nb:
            bf["ps_nb"] = ps_nb
            emit_negB(kb, bf, dram, mscq, msck)
            kb.psum_close([("ps_nb",)])

        emit_attn_layer(kb, bf, dram, 0)

        def y_out(eng):
            kb.need_r("y", eng, ("xres",))
            sn = kb.dsem("out_d")
            kb.inc(sn, eng.dma_start(out=out[:, :, :], in_=bf["xres"][:, :, :]), 16)
            kb.wait("y", eng, sn, kb.cnt[sn])
        kb.sec("y", y_out)
    return nc


def alloc_outer_bufs(nc, ctx, bf):
    A = lambda name, shape, dt: ctx.enter_context(nc.sbuf_tensor(_uname(name), shape, dt))
    bf["xo"] = A("xo", [128, 8, 512], F32)
    bf["xnb"] = A("xnb", [128, 8, 512], BF16)
    bf["att"] = A("att", [128, 8, 2048], BF16)
    bf["xsqo"] = bf["att"]
    bf["ktmp"] = A("ktmp", [128, 2, 512], BF16)
    bf["qtmp"] = A("qtmp", [128, 2, 512], BF16)
    bf["vtmp"] = A("vtmp", [128, 8, 130], BF16)
    bf["kstrip"] = A("kstrip", [128, 2, 4096], BF16)
    bf["vstrip"] = A("vstrip", [128, 2, 32 * 130], BF16)
    bf["qstrip"] = A("qstrip", [128, 2, 2048], BF16)
    bf["expo"] = A("expo", [128, 2048], BF16)
    bf["avbf_o"] = A("avbf_o", [128, 512], BF16)
    bf["to1"] = A("to1", [128, 512], F32)
    bf["oqsq"] = A("oqsq", [128, 512], BF16)
    bf["owg"] = A("owg", [128, 8, 1024], BF16)
    bf["recdA_o"] = A("recdA_o", [1, 512], F32)
    bf["recdB_o"] = A("recdB_o", [1, 512], F32)
    bf["accq"] = A("accq", [2, 8], F32)
    bf["acck"] = A("acck", [2, 8], F32)
    bf["tacc"] = A("tacc", [2, 8], F32)
    bf["tacc2"] = A("tacc2", [2, 8], F32)
    bf["negBo"] = A("negBo", [128, 8], F32)
    bf["ones21"] = A("ones21", [2, 1], F32)
    bf["bo"] = A("bo", [128, 8], F32)
    return bf


def emit_outer_ln(kb, bf, b8, ps):
    def v_cast(eng):
        kb.need_w("v", eng, ("xnb",))
        kb.need_r("v", eng, ("xo",))
        c = kb.inc("v", eng.tensor_copy(bf["xnb"][:, :, :], bf["xo"][:, :, :]))
        kb.mark_w(("xnb",), "v", c)
        kb.mark_r(("xo",), "v", c)
    kb.sec("v", v_cast)

    def s_sq(eng):
        kb.need_w("s", eng, ("xsqo",))
        kb.need_r("s", eng, ("xnb",))
        c = kb.inc("s", eng.activation(bf["xsqo"][:, :, 0:512], bf["xnb"][:, :, :],
                                       AF.Square))
        kb.mark_w(("xsqo",), "s", c)
        kb.mark_r(("xnb",), "s", c)
    kb.sec("s", s_sq)

    def t_st(eng):
        kb.psum_fence("t", eng)
        kb.need_w("t", eng, ("ps_st",))
        kb.need_r("t", eng, ("xnb",))
        kb.need_r("t", eng, ("xsqo",))
        kb.need_r("t", eng, ("ones128",))
        for kc in range(8):
            eng.matmul(ps["st"][0:1, :], bf["ones128"][:, :], bf["xnb"][:, kc, :],
                       start=(kc == 0), stop=(kc == 7), tile_position=(0, 0))
        for kc in range(8):
            i = eng.matmul(ps["st"][32:33, :], bf["ones128"][:, :],
                           bf["xsqo"][:, kc, 0:512], start=(kc == 0), stop=(kc == 7),
                           tile_position=(0, 32))
        c = kb.inc("t", i)
        kb.mark_w(("ps_st",), "t", c)
        kb.mark_r(("xnb",), "t", c)
        kb.mark_r(("xsqo",), "t", c)
    kb.sec("t", t_st)

    st = bf["st"]
    def v_f3a(eng):
        kb.need_r("v", eng, ("ps_st",))
        kb.need_w("v", eng, ("st",))
        eng.tensor_scalar_mul(st[0:1, 0, :], ps["st"][0:1, :], 1.0 / 1024.0)
        eng.drain()
        eng.scalar_tensor_tensor(st[0:1, 1, :], st[0:1, 0, :], -1.0, st[0:1, 0, :],
                                 ALU.mult, ALU.mult)
        eng.drain()
        c = kb.inc("v", eng.scalar_tensor_tensor(
            st[0:1, 2, :], ps["st"][32:33, :], 1.0 / 1024.0, st[0:1, 1, :],
            ALU.mult, ALU.add))
        kb.mark_r(("ps_st",), "v", c)
        kb.mark_w(("st",), "v", c)
    kb.sec("v", v_f3a)

    def s_f3(eng):
        kb.need_r("s", eng, ("st",))
        kb.need_r("s", eng, ("eps",))
        eng.activation(st[0:1, 1, :], st[0:1, 2, :], AF.Ln, bias=bf["eps"][0:1, :])
        eng.drain()
        c = kb.inc("s", eng.activation(st[0:1, 2, :], st[0:1, 1, :], AF.Exp, scale=-0.5))
        kb.mark_w(("st",), "s", c)
    kb.sec("s", s_f3)

    def v_f3b(eng):
        kb.need_r("v", eng, ("st",))
        c = kb.inc("v", eng.scalar_tensor_tensor(
            st[0:1, 1, :], st[0:1, 0, :], -1.0, st[0:1, 2, :], ALU.mult, ALU.mult))
        kb.mark_w(("st",), "v", c)
    kb.sec("v", v_f3b)

    def t_bc(eng):
        kb.need_r("t", eng, ("st",))
        kb.need_r("t", eng, ("ones1",))
        kb.need_w("t", eng, ("ps_rg",))
        kb.need_w("t", eng, ("ps_nm",))
        eng.matmul(ps["rg"][:, :], bf["ones1"][:, :], st[0:1, 2, :], start=True, stop=True)
        c = kb.inc("t", eng.matmul(ps["nm"][:, :], bf["ones1"][:, :], st[0:1, 1, :],
                                   start=True, stop=True))
        kb.mark_w(("ps_rg",), "t", c)
        kb.mark_w(("ps_nm",), "t", c)
        kb.mark_r(("st",), "t", c)
    kb.sec("t", t_bc)

    for kc in range(8):
        def v_xn(eng, kc=kc):
            kb.need_r("v", eng, ("ps_rg",))
            kb.need_r("v", eng, ("ps_nm",))
            kb.need_r("v", eng, ("xo",))
            kb.need_w("v", eng, ("tmpo",))
            if kc == 0:
                kb.need_w("v", eng, ("xnb",))
            eng.tensor_tensor(bf["to1"][:, :], bf["xo"][:, kc, :], ps["rg"][:, :], ALU.mult)
            eng.drain()
            c = kb.inc("v", eng.tensor_tensor(
                bf["xnb"][:, kc, :], bf["to1"][:, :], ps["nm"][:, :], ALU.add))
            kb.mark_r(("ps_rg",), "v", c)
            kb.mark_r(("ps_nm",), "v", c)
            kb.mark_r(("xo",), "v", c)
            kb.mark_w(("xnb",), "v", c)
            kb.mark_w(("tmpo",), "v", c)
        kb.sec("v", v_xn)


def emit_outer(kb, bf, dram, pc=None):
    nc = kb.nc

    def y_bo(eng):
        sn = kb.dsem("bo_d")
        c = kb.inc(sn, eng.dma_start(out=bf["bo"][:, :], in_=dram["bo_t"][:, :]), 16)
        kb.mark_w(("bo",), sn, c)
    kb.sec("y", y_bo)

    def v_init(eng):
        eng.memset(bf["vtmp"][:, :, :], 1.0)
        eng.memset(bf["accq"][:, :], 0.0)
        c = kb.inc("v", eng.memset(bf["acck"][:, :], 0.0))
        kb.mark_w(("vtmp", 0), "v", c)
        kb.mark_w(("vtmp", 1), "v", c)
        kb.mark_w(("accq",), "v", c)
        kb.mark_w(("acck",), "v", c)
    kb.sec("v", v_init)

    with ExitStack() as ctx:
        ps = dict(
            st=ctx.enter_context(nc.psum_tensor(kb.uid("ost"), [128, 512], F32)),
            rg=ctx.enter_context(nc.psum_tensor(kb.uid("org"), [128, 512], F32)),
            nm=ctx.enter_context(nc.psum_tensor(kb.uid("onm"), [128, 512], F32)),
        )
        for b8 in range(8):
            def y_x(eng, b8=b8):
                sn = kb.dsem("ox_d")
                kb.need_w("y", eng, ("xo",))
                c = kb.inc(sn, eng.dma_start(out=bf["xo"][:, :, :],
                                             in_=dram["x"][b8, :, :, :]), 16)
                kb.mark_w(("xo",), sn, c)
            kb.sec("y", y_x)

            emit_outer_ln(kb, bf, b8, ps)

            def g_xn(eng, b8=b8):
                sn = kb.dsem("xnsp_d")
                kb.need_r("g", eng, ("xnb",))
                c = kb.inc(sn, eng.dma_start(out=dram["xnspill"][b8, :, :, :],
                                             in_=bf["xnb"][:, :, :]), 16)
                kb.mark_r(("xnb",), sn, c)
                kb.mark_w(("xnsp", b8), sn, c)
            kb.sec("g", g_xn)
            if pc:
                emit_precast_step(kb, pc, 1)
        kb.psum_close([("ps_st",), ("ps_rg",), ("ps_nm",)])

    with ExitStack() as ctx:
        ps = dict(
            pp=[ctx.enter_context(nc.psum_tensor(kb.uid("opp"), [128, 512], F32))
                for _ in range(2)],
            ss=ctx.enter_context(nc.psum_tensor(kb.uid("oss"), [128, 512], F32)),
        )
        for kind, wsrc in (("k", "wk"), ("v", "wv"), ("q", "wq")):
            def g_w(eng, wsrc=wsrc):
                kb.need_w("g", eng, ("owg",))
                sn = kb.dsem("owg_d")
                for kc in range(8):
                    kb.inc(sn, eng.dma_start(
                        out=bf["owg"][:, kc, :],
                        in_=dram[wsrc][128 * kc:128 * (kc + 1), :]), 16)
                kb.mark_w(("owg",), sn, kb.cnt[sn])
            kb.sec("g", g_w)

            nb8 = 4 if kind == "q" else 8
            for b8 in range(nb8):
                def y_xn(eng, b8=b8):
                    sn = kb.dsem("xnl_d")
                    kb.need_w("y", eng, ("xnb",))
                    kb.need_r("y", eng, ("xnsp", b8))
                    c = kb.inc(sn, eng.dma_start(out=bf["xnb"][:, :, :],
                                                 in_=dram["xnspill"][b8, :, :, :]), 16)
                    kb.mark_w(("xnb",), sn, c)
                kb.sec("y", y_xn)
                if pc:
                    emit_precast_step(kb, pc, 1)

                if kind == "v":
                    for tc in range(4):
                        for half in range(2):
                            pp = ps["pp"][half]
                            ppk = ("pp", half)

                            def t_v(eng, tc=tc, half=half, pp=pp, ppk=ppk):
                                kb.psum_fence("t", eng)
                                kb.need_r("t", eng, ("owg",))
                                kb.need_r("t", eng, ("xnb",))
                                kb.need_w("t", eng, ppk)
                                for kc in range(8):
                                    i = eng.matmul(
                                        pp[:, :], bf["xnb"][:, kc, 128 * tc:128 * (tc + 1)],
                                        bf["owg"][:, kc, 512 * half:512 * (half + 1)],
                                        start=(kc == 0), stop=(kc == 7))
                                c = kb.inc("t", i)
                                kb.mark_w(ppk, "t", c)
                                kb.mark_r(("owg",), "t", c)
                                kb.mark_r(("xnb",), "t", c)
                            kb.sec("t", t_v)

                            def v_v(eng, half=half, pp=pp, ppk=ppk):
                                kb.need_r("v", eng, ppk)
                                kb.need_w("v", eng, ("vtmp", half))
                                dst = bf["vtmp"][:, 4 * half:4 * (half + 1), :].rearrange(
                                    "p a (b c) -> p a b c", c=65)[:, :, :, 0:64]
                                src = pp[:, :].rearrange("p (a b c) -> p a b c", b=2, c=64)
                                c = kb.inc("v", eng.tensor_copy(dst, src))
                                kb.mark_w(("vtmp", half), "v", c)
                                kb.mark_r(ppk, "v", c)
                            kb.sec("v", v_v)

                        def g_vsp(eng, tc=tc, b8=b8):
                            sn = kb.dsem("vsp_d")
                            for half in range(2):
                                kb.need_r("g", eng, ("vtmp", half))
                            for hp in range(8):
                                c = kb.inc(sn, eng.dma_start(
                                    out=dram["vspill"][hp, :, 4 * b8 + tc, :],
                                    in_=bf["vtmp"][:, hp, :]), 16)
                            kb.mark_w(("vsp",), sn, c)
                            for half in range(2):
                                kb.mark_r(("vtmp", half), sn, kb.cnt[sn])
                        kb.sec("g", g_vsp)
                    continue

                for mc in range(8):
                    pp = ps["pp"][mc % 2]
                    ppk = ("pp", mc % 2)
                    tmpb = bf["ktmp"] if kind == "k" else bf["qtmp"]
                    tmpk = ("ktmp", mc % 2) if kind == "k" else ("qtmp", mc % 2)

                    def t_p(eng, mc=mc, pp=pp, ppk=ppk):
                        kb.psum_fence("t", eng)
                        kb.need_r("t", eng, ("owg",))
                        kb.need_r("t", eng, ("xnb",))
                        kb.need_w("t", eng, ppk)
                        for kc in range(8):
                            i = eng.matmul(pp[:, :], bf["owg"][:, kc, 128 * mc:128 * (mc + 1)],
                                           bf["xnb"][:, kc, :], start=(kc == 0), stop=(kc == 7))
                        c = kb.inc("t", i)
                        kb.mark_w(ppk, "t", c)
                        kb.mark_r(("owg",), "t", c)
                        kb.mark_r(("xnb",), "t", c)
                    kb.sec("t", t_p)

                    def v_e(eng, mc=mc, pp=pp, ppk=ppk, tmpb=tmpb, tmpk=tmpk):
                        kb.need_r("v", eng, ppk)
                        kb.need_w("v", eng, tmpk)
                        kb.need_w("v", eng, ("qsqo",))
                        eng.tensor_copy(tmpb[:, mc % 2, :], pp[:, :])
                        eng.drain()
                        c = kb.inc("v", eng.scalar_tensor_tensor(
                            bf["oqsq"][:, :], pp[:, :], 1.0, tmpb[:, mc % 2, :],
                            ALU.mult, ALU.mult))
                        kb.mark_w(tmpk, "v", c)
                        kb.mark_w(("qsqo",), "v", c)
                        kb.mark_r(ppk, "v", c)
                    kb.sec("v", v_e)

                    def t_ss(eng):
                        kb.need_r("t", eng, ("qsqo",))
                        kb.need_r("t", eng, ("consts",))
                        kb.need_w("t", eng, ("psss",))
                        c = kb.inc("t", eng.matmul(ps["ss"][0:2, :], bf["sel2t"][:, :],
                                                   bf["oqsq"][:, :], start=True, stop=True))
                        kb.mark_w(("psss",), "t", c)
                        kb.mark_r(("qsqo",), "t", c)
                    kb.sec("t", t_ss)

                    def v_acc(eng, mc=mc, kind=kind):
                        acc = bf["accq"] if kind == "q" else bf["acck"]
                        akey = ("accq",) if kind == "q" else ("acck",)
                        kb.need_r("v", eng, ("psss",))
                        kb.need_w("v", eng, ("tacc",))
                        kb.need_r("v", eng, akey)
                        eng.tensor_reduce(bf["tacc"][0:2, 0:1], ps["ss"][0:2, :],
                                          mybir.AxisListType.X, ALU.max)
                        eng.drain()
                        c = kb.inc("v", eng.tensor_tensor(
                            acc[0:2, mc:mc + 1], acc[0:2, mc:mc + 1],
                            bf["tacc"][0:2, 0:1], ALU.max))
                        kb.mark_w(akey, "v", c)
                        kb.mark_w(("tacc",), "v", c)
                        kb.mark_r(("psss",), "v", c)
                    kb.sec("v", v_acc)

                    def y_sp(eng, mc=mc, b8=b8, kind=kind, tmpb=tmpb, tmpk=tmpk):
                        sp = dram["kspill"] if kind == "k" else dram["qspill"]
                        sn = kb.dsem(kind + "sp_d" + str(mc % 2))
                        kb.need_r("y", eng, tmpk)
                        c = kb.inc(sn, eng.dma_start(
                            out=sp[mc, :, 512 * b8:512 * (b8 + 1)],
                            in_=tmpb[:, mc % 2, :]), 16)
                        kb.mark_r(tmpk, sn, c)
                        kb.mark_w((kind + "sp", mc), sn, c)
                    kb.sec("y", y_sp)
        kb.psum_close([("pp", 0), ("pp", 1), ("psss",)])


def emit_outer_attn(kb, bf, dram, pc=None):
    nc = kb.nc
    with ExitStack() as ctx:
        ps_nb = ctx.enter_context(nc.psum_tensor(kb.uid("onb"), [128, 512], F32))

        def v_b1(eng):
            kb.need_r("v", eng, ("accq",))
            kb.need_r("v", eng, ("acck",))
            kb.need_w("v", eng, ("tacc",))
            eng.memset(bf["ones21"][:, :], 1.0)
            c = kb.inc("v", eng.tensor_tensor(bf["tacc"][0:2, :], bf["accq"][0:2, :],
                                              bf["acck"][0:2, :], ALU.mult))
            kb.mark_w(("tacc",), "v", c)
            kb.mark_w(("ones21",), "v", c)
        kb.sec("v", v_b1)

        def s_b(eng):
            kb.need_r("s", eng, ("tacc",))
            eng.activation(bf["tacc2"][0:2, :], bf["tacc"][0:2, :], AF.Ln)
            eng.drain()
            c = kb.inc("s", eng.activation(bf["tacc"][0:2, :], bf["tacc2"][0:2, :],
                                           AF.Exp, scale=0.5))
            kb.mark_w(("tacc",), "s", c)
        kb.sec("s", s_b)

        def t_b(eng):
            kb.psum_fence("t", eng)
            kb.need_r("t", eng, ("tacc",))
            kb.need_r("t", eng, ("ones21",))
            c = kb.inc("t", eng.matmul(ps_nb[0:1, 0:8], bf["ones21"][:, :],
                                       bf["tacc"][0:2, :], start=True, stop=True))
            kb.mark_w(("ps_nb",), "t", c)
            kb.mark_r(("tacc",), "t", c)
        kb.sec("t", t_b)

        def v_b2(eng):
            kb.need_r("v", eng, ("ps_nb",))
            kb.need_w("v", eng, ("tacc",))
            c = kb.inc("v", eng.tensor_scalar_mul(bf["tacc"][0:1, :], ps_nb[0:1, 0:8],
                                                  -0.125 * 1.05))
            kb.mark_w(("tacc",), "v", c)
            kb.mark_r(("ps_nb",), "v", c)
        kb.sec("v", v_b2)

        def t_b2(eng):
            kb.need_r("t", eng, ("tacc",))
            kb.need_r("t", eng, ("ones1",))
            kb.need_w("t", eng, ("ps_nb",))
            c = kb.inc("t", eng.matmul(ps_nb[:, 0:8], bf["ones1"][:, :],
                                       bf["tacc"][0:1, :], start=True, stop=True))
            kb.mark_w(("ps_nb",), "t", c)
            kb.mark_r(("tacc",), "t", c)
        kb.sec("t", t_b2)

        def v_b3(eng):
            kb.need_r("v", eng, ("ps_nb",))
            kb.need_w("v", eng, ("negBo",))
            c = kb.inc("v", eng.tensor_copy(bf["negBo"][:, :], ps_nb[:, 0:8]))
            kb.mark_w(("negBo",), "v", c)
            kb.mark_r(("ps_nb",), "v", c)
        kb.sec("v", v_b3)
        kb.psum_close([("ps_nb",)])

    with ExitStack() as ctx:
        ps = dict(
            sim=[ctx.enter_context(nc.psum_tensor(kb.uid("osim"), [128, 1024], F32))
                 for _ in range(2)],
            avA=ctx.enter_context(nc.psum_tensor(kb.uid("oavA"), [128, 1024], F32)),
            avB=ctx.enter_context(nc.psum_tensor(kb.uid("oavB"), [128, 1024], F32)),
        )
        for hp in range(8):
            buf = hp % 2

            def g_stream(eng, hp=hp, buf=buf):
                for nm, dst, sh in (("kstrip", bf["kstrip"], dram["kspill"]),
                                    ("qstrip", bf["qstrip"], dram["qspill"]),
                                    ("vstrip", bf["vstrip"], dram["vspill"])):
                    kb.need_w("g", eng, (nm, buf))
                    if nm == "kstrip":
                        kb.need_r("g", eng, ("ksp", hp))
                    elif nm == "qstrip":
                        kb.need_r("g", eng, ("qsp", hp))
                    else:
                        kb.need_r("g", eng, ("vsp",))
                    sn = kb.dsem(nm + "_d" + str(buf))
                    if nm == "vstrip":
                        c = kb.inc(sn, eng.dma_start(
                            out=dst[:, buf, :],
                            in_=sh[hp, :, :, :].rearrange("p a b -> p (a b)")), 16)
                    else:
                        c = kb.inc(sn, eng.dma_start(out=dst[:, buf, :], in_=sh[hp, :, :]), 16)
                    kb.mark_w((nm, buf), sn, c)
            kb.sec("g", g_stream)

            for ih in range(2):
                if pc:
                    emit_precast_step(kb, pc, 1)
                TS, SE, TA = [], [], []
                for jc in range(32):
                    for i2 in range(2):
                        def t_sim(eng, hp=hp, buf=buf, ih=ih, jc=jc, i2=i2):
                            kb.psum_fence("t", eng)
                            kb.need_r("t", eng, ("kstrip", buf))
                            kb.need_r("t", eng, ("qstrip", buf))
                            kb.need_w("t", eng, ("pssim", i2))
                            kb.need_w("t", eng, ("psrb2",))
                            if jc == 0:
                                kb.need_w("t", eng, ("psavA",))
                                kb.need_w("t", eng, ("psavB",))
                            ib = 2 * ih + i2
                            eng.matmul(ps["sim"][i2][:, 0:512],
                                       bf["kstrip"][0:64, buf, 128 * jc:128 * (jc + 1)],
                                       bf["qstrip"][0:64, buf, 512 * ib:512 * (ib + 1)],
                                       start=True, stop=True, tile_position=(0, 0))
                            i = eng.matmul(ps["sim"][i2][:, 512:1024],
                                           bf["kstrip"][64:128, buf, 128 * jc:128 * (jc + 1)],
                                           bf["qstrip"][64:128, buf, 512 * ib:512 * (ib + 1)],
                                           start=True, stop=True, tile_position=(64, 0))
                            c = kb.inc("t", i)
                            kb.mark_w(("pssim", i2), "t", c)
                            kb.mark_r(("kstrip", buf), "t", c)
                            kb.mark_r(("qstrip", buf), "t", c)
                        TS.append(t_sim)

                        def s_exp(eng, hp=hp, i2=i2):
                            kb.need_r("s", eng, ("pssim", i2))
                            kb.need_r("s", eng, ("negBo",))
                            kb.need_w("s", eng, ("expo", i2))
                            c = kb.inc("s", eng.activation(
                                bf["expo"][:, 1024 * i2:1024 * (i2 + 1)],
                                ps["sim"][i2][:, :], AF.Exp,
                                bias=bf["negBo"][:, hp:hp + 1], scale=0.125))
                            kb.mark_w(("expo", i2), "s", c)
                            kb.mark_r(("pssim", i2), "s", c)
                        SE.append(s_exp)

                        def t_av(eng, buf=buf, jc=jc, i2=i2):
                            kb.need_r("t", eng, ("expo", i2))
                            kb.need_r("t", eng, ("vstrip", buf))
                            eng.matmul(ps["avA"][0:65, 512 * i2:512 * (i2 + 1)],
                                       bf["vstrip"][:, buf, 130 * jc:130 * jc + 65],
                                       bf["expo"][:, 1024 * i2:1024 * i2 + 512],
                                       start=(jc == 0), stop=(jc == 31))
                            i = eng.matmul(ps["avB"][0:65, 512 * i2:512 * (i2 + 1)],
                                           bf["vstrip"][:, buf, 130 * jc + 65:130 * (jc + 1)],
                                           bf["expo"][:, 1024 * i2 + 512:1024 * (i2 + 1)],
                                           start=(jc == 0), stop=(jc == 31))
                            c = kb.inc("t", i)
                            kb.mark_w(("psavA",), "t", c)
                            kb.mark_w(("psavB",), "t", c)
                            kb.mark_r(("expo", i2), "t", c)
                            kb.mark_r(("vstrip", buf), "t", c)
                        TA.append(t_av)

                nstep = len(TS)
                for k in range(nstep + 1):
                    if k < nstep:
                        kb.sec("t", TS[k])
                    if 0 <= k - 1 < nstep:
                        kb.sec("s", SE[k - 1])
                        kb.sec("t", TA[k - 1])

                for i2 in range(2):
                    ib = 2 * ih + i2

                    def s_rec(eng, i2=i2):
                        kb.need_r("s", eng, ("psavA",))
                        kb.need_r("s", eng, ("psavB",))
                        kb.need_w("s", eng, ("recdA_o",))
                        kb.need_w("s", eng, ("recdB_o",))
                        kb.need_w("s", eng, ("tmpo",))
                        kb.need_w("s", eng, ("xnb",))
                        reclA = bf["to1"][0:1, :]
                        reclB = bf["xnb"][0:1, 0:2, :].bitcast(F32).rearrange(
                            "p a b -> p (a b)")
                        eng.activation(reclA,
                                       ps["avA"][64:65, 512 * i2:512 * (i2 + 1)], AF.Ln)
                        eng.activation(reclB,
                                       ps["avB"][64:65, 512 * i2:512 * (i2 + 1)], AF.Ln)
                        eng.drain()
                        eng.activation(bf["recdA_o"][0:1, :], reclA, AF.Exp, scale=-1.0)
                        c = kb.inc("s", eng.activation(bf["recdB_o"][0:1, :], reclB,
                                                       AF.Exp, scale=-1.0))
                        kb.mark_w(("recdA_o",), "s", c)
                        kb.mark_w(("recdB_o",), "s", c)
                        kb.mark_w(("tmpo",), "s", c)
                        kb.mark_w(("xnb",), "s", c)
                        kb.mark_r(("psavA",), "s", c)
                        kb.mark_r(("psavB",), "s", c)
                    kb.sec("s", s_rec)

                    def t_dbc(eng, i2=i2):
                        kb.need_r("t", eng, ("recdA_o",))
                        kb.need_r("t", eng, ("recdB_o",))
                        kb.need_r("t", eng, ("consts",))
                        kb.need_w("t", eng, ("psrb2",))
                        kb.need_w("t", eng, ("pssim", 0))
                        eng.matmul(ps["sim"][0][:, 0:512], bf["selA"][:, :],
                                   bf["recdA_o"][0:1, :], start=True, stop=False)
                        c = kb.inc("t", eng.matmul(ps["sim"][0][:, 0:512], bf["selB"][:, :],
                                                   bf["recdB_o"][0:1, :], start=False, stop=True))
                        kb.mark_w(("psrb2",), "t", c)
                        kb.mark_r(("recdA_o",), "t", c)
                        kb.mark_r(("recdB_o",), "t", c)
                    kb.sec("t", t_dbc)

                    def v_av(eng, i2=i2):
                        kb.need_r("v", eng, ("psavA",))
                        kb.need_r("v", eng, ("psavB",))
                        kb.need_w("v", eng, ("avbf_o",))
                        eng.tensor_copy(bf["avbf_o"][0:64, :],
                                        ps["avA"][0:64, 512 * i2:512 * (i2 + 1)])
                        c = kb.inc("v", eng.tensor_copy(
                            bf["avbf_o"][64:128, :],
                            ps["avB"][0:64, 512 * i2:512 * (i2 + 1)]))
                        kb.mark_w(("avbf_o",), "v", c)
                        kb.mark_r(("psavA",), "v", c)
                        kb.mark_r(("psavB",), "v", c)
                    kb.sec("v", v_av)

                    def v_div(eng, hp=hp, ib=ib):
                        kb.need_r("v", eng, ("avbf_o",))
                        kb.need_r("v", eng, ("psrb2",))
                        kb.need_w("v", eng, ("att", hp, ib))
                        eng.drain()
                        c = kb.inc("v", eng.tensor_tensor(
                            bf["att"][:, hp, 512 * ib:512 * (ib + 1)], bf["avbf_o"][:, :],
                            ps["sim"][0][:, 0:512], ALU.mult))
                        kb.mark_w(("att", hp, ib), "v", c)
                        kb.mark_r(("avbf_o",), "v", c)
                        kb.mark_r(("psrb2",), "v", c)
                    kb.sec("v", v_div)
        kb.psum_close([("pssim", 0), ("pssim", 1), ("psavA",), ("psavB",),
                       ("psrb2",)])

    with ExitStack() as ctx:
        ps = dict(po=[ctx.enter_context(nc.psum_tensor(kb.uid("opo"), [128, 512], F32))
                      for _ in range(2)])

        def g_wo(eng):
            kb.need_w("g", eng, ("owg",))
            sn = kb.dsem("owg_d")
            for kc in range(8):
                kb.inc(sn, eng.dma_start(
                    out=bf["owg"][:, kc, :],
                    in_=dram["wo"][128 * kc:128 * (kc + 1), :]), 16)
            kb.mark_w(("owg",), sn, kb.cnt[sn])
        kb.sec("g", g_wo)

        for S in range(4):
            def y_x(eng, S=S):
                sn = kb.dsem("ox_d")
                kb.need_w("y", eng, ("xo",))
                c = kb.inc(sn, eng.dma_start(out=bf["xo"][:, :, :],
                                             in_=dram["x"][S, :, :, :]), 16)
                kb.mark_w(("xo",), sn, c)
            kb.sec("y", y_x)

            for mc in range(8):
                pok = ("po", mc % 2)

                def t_o(eng, S=S, mc=mc, pok=pok):
                    kb.psum_fence("t", eng)
                    kb.need_r("t", eng, ("owg",))
                    for hp in range(8):
                        kb.need_r("t", eng, ("att", hp, S))
                    kb.need_w("t", eng, pok)
                    for kc in range(8):
                        i = eng.matmul(ps["po"][mc % 2][:, :],
                                       bf["owg"][:, kc, 128 * mc:128 * (mc + 1)],
                                       bf["att"][:, kc, 512 * S:512 * (S + 1)],
                                       start=(kc == 0), stop=(kc == 7))
                    c = kb.inc("t", i)
                    kb.mark_w(pok, "t", c)
                    kb.mark_r(("owg",), "t", c)
                    for hp in range(8):
                        kb.mark_r(("att", hp, S), "t", c)
                kb.sec("t", t_o)

                def v_o(eng, S=S, mc=mc, pok=pok):
                    kb.need_r("v", eng, pok)
                    kb.need_r("v", eng, ("xo",))
                    kb.need_r("v", eng, ("bo",))
                    kb.need_w("v", eng, ("xres",))
                    c = kb.inc("v", eng.scalar_tensor_tensor(
                        bf["xres"][:, mc, 1 + 512 * S:1 + 512 * (S + 1)],
                        ps["po"][mc % 2][:, :], bf["bo"][:, mc:mc + 1],
                        bf["xo"][:, mc, :], ALU.add, ALU.add))
                    kb.mark_w(("xres",), "v", c)
                    kb.mark_r(pok, "v", c)
                    kb.mark_r(("xo",), "v", c)
                kb.sec("v", v_o)
        kb.psum_close([("po", 0), ("po", 1)])


def declare_outer_drams(nc):
    dram = {}
    EI = lambda n, s, d: nc.dram_tensor(n, s, d, kind="ExternalInput")
    dram["x"] = EI("x", [8, 128, 8, 512], F32)
    dram["wq"] = EI("wq", [1024, 1024], F32)
    dram["wk"] = EI("wk", [1024, 1024], F32)
    dram["wv"] = EI("wv", [1024, 1024], F32)
    dram["wo"] = EI("wo", [1024, 1024], F32)
    dram["bo_t"] = EI("bo_t", [128, 8], F32)
    dram["kspill"] = nc.dram_tensor("kspill", [8, 128, 4096], BF16)
    dram["xnspill"] = nc.dram_tensor("xnspill", [8, 128, 8, 512], BF16)
    dram["x1spill"] = nc.dram_tensor("x1spill", [4, 128, 8, 512], F32)
    dram["qspill"] = nc.dram_tensor("qspill", [8, 128, 2048], BF16)
    dram["vspill"] = nc.dram_tensor("vspill", [8, 128, 32, 130], BF16)
    return dram


def build_outer_test():
    nc = bass.Bass("TRN2", target_bir_lowering=False, debug=False)
    dram = declare_outer_drams(nc)
    EI = lambda n, s, d: nc.dram_tensor(n, s, d, kind="ExternalInput")
    dram["sel2t"] = EI("sel2t", [128, 2], BF16)
    dram["selA"] = EI("selA", [1, 128], F32)
    dram["selB"] = EI("selB", [1, 128], F32)
    out = nc.dram_tensor("out", [128, 8, T], F32, kind="ExternalOutput")

    with ExitStack() as ctx:
        block = ctx.enter_context(nc.Block())
        kb = KB(nc, block, ctx)
        bf = alloc_common_bufs(nc, ctx)
        alloc_persist_bufs(nc, ctx, bf)
        alloc_outer_bufs(nc, ctx, bf)
        emit_init_consts(kb, bf)

        def y_c(eng):
            sn = kb.dsem("const_d")
            for dst, srcn in ((bf["sel2t"], "sel2t"), (bf["selA"], "selA"),
                              (bf["selB"], "selB")):
                kb.inc(sn, eng.dma_start(out=dst[:, :], in_=dram[srcn][:, :]), 16)
            kb.mark_w(("consts",), sn, kb.cnt[sn])
        kb.sec("y", y_c)

        emit_outer(kb, bf, dram)
        emit_outer_attn(kb, bf, dram)

        def y_out(eng):
            kb.need_r("y", eng, ("xres",))
            sn = kb.dsem("out_d")
            kb.inc(sn, eng.dma_start(out=out[:, :, :], in_=bf["xres"][:, :, 1:T + 1]), 16)
            kb.wait("y", eng, sn, kb.cnt[sn])
        kb.sec("y", y_out)
    return nc


def declare_precast_drams(nc):
    dram = {}
    dram["wqkvb"] = nc.dram_tensor("wqkvb", [6, 1024, 3072], BF16)
    dram["wob"] = nc.dram_tensor("wob", [6, 1024, 1024], BF16)
    dram["w1b"] = nc.dram_tensor("w1b", [6, 1024, 5460], BF16)
    dram["w2b"] = nc.dram_tensor("w2b", [6, DI, 1024], BF16)
    return dram


def declare_ffn_drams(nc):
    dram = {}
    EI = lambda n, s, d: nc.dram_tensor(n, s, d, kind="ExternalInput")
    dram["w1"] = EI("w1", [6, 1024, 5460], F32)
    dram["w2"] = EI("w2", [6, DI, 1024], F32)
    dram["lngr"] = EI("lngr", [6, 8, 128], F32)
    dram["lng_t"] = EI("lng_t", [6, 128, 8], F32)
    dram["lnbr"] = EI("lnbr", [6, 8, 128], F32)
    dram["b1a"] = EI("b1a", [6, 128, NC_DI], F32)
    dram["b1g"] = EI("b1g", [6, 128, NC_DI], F32)
    dram["b2_t"] = EI("b2_t", [6, 128, 8], F32)
    return dram


def build_full(nranks=2, depth=DEPTH, attn_only=False, debug_taps=False):
    rec = Rec()
    _build_full_pass(nranks, depth, attn_only, debug_taps, "record", rec)
    return _build_full_pass(nranks, depth, attn_only, debug_taps, "replay", rec)


def _build_full_pass(nranks, depth, attn_only, debug_taps, mode, rec):
    global CCGROUPS
    if nranks == 1:
        CCGROUPS = [[0]]
    elif nranks == 2:
        CCGROUPS = [[0, 1]]
    else:
        CCGROUPS = [[0, 1], [2, 3], [4, 5], [6, 7]]
    nc = bass.Bass("TRN2", target_bir_lowering=False, debug=False)
    dram = {}
    dram.update(declare_outer_drams(nc))
    dram.update(declare_attn_drams(nc, nranks=min(nranks, 2)))
    dram.update(declare_ffn_drams(nc))
    if depth > 0:
        dram.update(declare_precast_drams(nc))
    out = nc.dram_tensor("out", [128, 8, T], F32, kind="ExternalOutput")

    with ExitStack() as ctx:
        block = ctx.enter_context(nc.Block())
        kb = KB(nc, block, ctx, mode=mode, rec=rec)
        bf = alloc_common_bufs(nc, ctx)
        alloc_persist_bufs(nc, ctx, bf)
        emit_init_consts(kb, bf)
        emit_attn_consts(kb, bf, dram)

        mscq, msck = _table_maxima()
        with nc.psum_tensor(kb.uid("ps_nb"), [128, 512], F32) as ps_nb:
            bf["ps_nb"] = ps_nb
            emit_negB(kb, bf, dram, mscq, msck)
            kb.psum_close([("ps_nb",)])

        pc = mk_precast_chunks(dram) if depth > 0 else []
        with ExitStack() as octx:
            alloc_outer_bufs(nc, octx, bf)
            emit_outer(kb, bf, dram, pc)
            emit_outer_attn(kb, bf, dram, pc)
            if depth > 0:
                emit_ln2_and_spill(kb, bf, dram)
        emit_precast_step(kb, pc, len(pc))
        kb.phase_barrier()

        for l in range(depth):
            with ExitStack() as actx:
                alloc_attn_bufs(nc, actx, bf)
                emit_attn_layer(kb, bf, dram, l)
            kb.phase_barrier()
            if attn_only and l == depth - 1:
                break
            with ExitStack() as fctx:
                alloc_ffn_bufs(nc, fctx, bf)
                emit_ffn_loads(kb, bf, dram, l)
                emit_ffn(kb, bf, dram, l)
            kb.phase_barrier()

        if depth > 0:
            emit_final_add(kb, bf, dram)
            kb.phase_barrier()

        if debug_taps:
            xn_d = nc.dram_tensor("xn_d", [128, 8, 512], BF16, kind="ExternalOutput")
            h_d = nc.dram_tensor("h_d", [128, 21, 512], BF16, kind="ExternalOutput")

            def y_tap(eng):
                for hc in range(21):
                    kb.need_r("y", eng, ("h", hc))
                for kc in range(8):
                    kb.need_r("y", eng, ("xn", kc))
                sn = kb.dsem("tap_d")
                kb.inc(sn, eng.dma_start(out=xn_d[:, :, :], in_=bf["xn"][:, :, :]), 16)
                kb.inc(sn, eng.dma_start(out=h_d[:, :, :], in_=bf["h"][:, 0:21, :]), 16)
                kb.wait("y", eng, sn, kb.cnt[sn])
            kb.sec("y", y_tap)

        def y_out(eng):
            kb.need_r("y", eng, ("xres",))
            sn = kb.dsem("out_d")
            kb.inc(sn, eng.dma_start(out=out[:, :, :], in_=bf["xres"][:, :, 1:T + 1]), 16)
            kb.wait("y", eng, sn, kb.cnt[sn])
        kb.sec("y", y_out)
        if mode == "replay":
            kb.flush()
    return nc


_NC_CACHE = {}

_INPUT_NAMES = [
    "x", "wq", "wk", "wv", "wo", "bo_t", "wqkv", "wo_i", "w1", "w2",
    "lngr", "lng_t", "lnbr", "b1a", "b1g", "b2_t", "csq", "ssq", "csk", "ssk",
    "qs_m", "qs_p", "ks_m", "ks_p", "qs_r", "ks_r",
    "p128", "idn", "masku", "maskl", "sel2t", "sel2", "selA", "selB",
    "flag", "hmask",
]


def kernel(**inputs):
    in_maps = stage_inputs(inputs)
    in_maps = [{k: m[k] for k in _INPUT_NAMES} for m in in_maps]
    if "full" not in _NC_CACHE:
        _NC_CACHE["full"] = build_full(nranks=8)
    nc = _NC_CACHE["full"]
    res = run_bass_kernel_spmd(nc, in_maps, core_ids=list(range(8)))
    return unstage_output(res.results)


def emit_ln2_and_spill(kb, bf, dram):
    nc = kb.nc
    with ExitStack() as ctx:
        ps = dict(
            st=ctx.enter_context(nc.psum_tensor(kb.uid("l2st"), [128, 512], F32)),
            rg=ctx.enter_context(nc.psum_tensor(kb.uid("l2rg"), [128, 512], F32)),
            nm=ctx.enter_context(nc.psum_tensor(kb.uid("l2nm"), [128, 512], F32)),
        )
        for S in range(4):
            col0 = 1 + 512 * S

            def g_sp(eng, S=S):
                sn = kb.dsem("x1sp_d")
                kb.need_r("g", eng, ("xres",))
                c = kb.inc(sn, eng.dma_start(out=dram["x1spill"][S, :, :, :],
                                             in_=bf["xres"][:, :, col0:col0 + 512]), 16)
                kb.mark_r(("xres",), sn, c)
                kb.mark_w(("x1sp", S), sn, c)
            kb.sec("g", g_sp)

            def v_cast(eng, S=S):
                kb.need_w("v", eng, ("xnb",))
                kb.need_r("v", eng, ("xres",))
                c = kb.inc("v", eng.tensor_copy(bf["xnb"][:, :, :],
                                                bf["xres"][:, :, col0:col0 + 512]))
                kb.mark_w(("xnb",), "v", c)
                kb.mark_r(("xres",), "v", c)
            kb.sec("v", v_cast)

            def s_sq(eng):
                kb.need_w("s", eng, ("xsqo",))
                kb.need_r("s", eng, ("xnb",))
                c = kb.inc("s", eng.activation(bf["xsqo"][:, :, 0:512], bf["xnb"][:, :, :],
                                               AF.Square))
                kb.mark_w(("xsqo",), "s", c)
                kb.mark_r(("xnb",), "s", c)
            kb.sec("s", s_sq)

            def t_st(eng):
                kb.psum_fence("t", eng)
                kb.need_w("t", eng, ("ps_st",))
                kb.need_r("t", eng, ("xnb",))
                kb.need_r("t", eng, ("xsqo",))
                kb.need_r("t", eng, ("ones128",))
                for kc in range(8):
                    eng.matmul(ps["st"][0:1, :], bf["ones128"][:, :], bf["xnb"][:, kc, :],
                               start=(kc == 0), stop=(kc == 7), tile_position=(0, 0))
                for kc in range(8):
                    i = eng.matmul(ps["st"][32:33, :], bf["ones128"][:, :],
                                   bf["xsqo"][:, kc, 0:512], start=(kc == 0), stop=(kc == 7),
                                   tile_position=(0, 32))
                c = kb.inc("t", i)
                kb.mark_w(("ps_st",), "t", c)
                kb.mark_r(("xnb",), "t", c)
                kb.mark_r(("xsqo",), "t", c)
            kb.sec("t", t_st)

            st = bf["st"]
            def v_f3a(eng):
                kb.need_r("v", eng, ("ps_st",))
                kb.need_w("v", eng, ("st",))
                eng.tensor_scalar_mul(st[0:1, 0, :], ps["st"][0:1, :], 1.0 / 1024.0)
                eng.drain()
                eng.scalar_tensor_tensor(st[0:1, 1, :], st[0:1, 0, :], -1.0, st[0:1, 0, :],
                                         ALU.mult, ALU.mult)
                eng.drain()
                c = kb.inc("v", eng.scalar_tensor_tensor(
                    st[0:1, 2, :], ps["st"][32:33, :], 1.0 / 1024.0, st[0:1, 1, :],
                    ALU.mult, ALU.add))
                kb.mark_r(("ps_st",), "v", c)
                kb.mark_w(("st",), "v", c)
            kb.sec("v", v_f3a)

            def s_f3(eng):
                kb.need_r("s", eng, ("st",))
                kb.need_r("s", eng, ("eps",))
                eng.activation(st[0:1, 1, :], st[0:1, 2, :], AF.Ln, bias=bf["eps"][0:1, :])
                eng.drain()
                c = kb.inc("s", eng.activation(st[0:1, 2, :], st[0:1, 1, :], AF.Exp, scale=-0.5))
                kb.mark_w(("st",), "s", c)
            kb.sec("s", s_f3)

            def v_f3b(eng):
                kb.need_r("v", eng, ("st",))
                c = kb.inc("v", eng.scalar_tensor_tensor(
                    st[0:1, 1, :], st[0:1, 0, :], -1.0, st[0:1, 2, :], ALU.mult, ALU.mult))
                kb.mark_w(("st",), "v", c)
            kb.sec("v", v_f3b)

            def t_bc(eng):
                kb.need_r("t", eng, ("st",))
                kb.need_r("t", eng, ("ones1",))
                kb.need_w("t", eng, ("ps_rg",))
                kb.need_w("t", eng, ("ps_nm",))
                eng.matmul(ps["rg"][:, :], bf["ones1"][:, :], st[0:1, 2, :],
                           start=True, stop=True)
                c = kb.inc("t", eng.matmul(ps["nm"][:, :], bf["ones1"][:, :], st[0:1, 1, :],
                                           start=True, stop=True))
                kb.mark_w(("ps_rg",), "t", c)
                kb.mark_w(("ps_nm",), "t", c)
                kb.mark_r(("st",), "t", c)
            kb.sec("t", t_bc)

            for kc in range(8):
                def v_xn(eng, kc=kc, S=S, col0=col0):
                    kb.need_r("v", eng, ("ps_rg",))
                    kb.need_r("v", eng, ("ps_nm",))
                    kb.need_r("v", eng, ("x1sp", S))
                    kb.need_w("v", eng, ("tmpo",))
                    if kc == 0:
                        kb.need_w("v", eng, ("xres",))
                    eng.tensor_tensor(bf["to1"][:, :], bf["xres"][:, kc, col0:col0 + 512],
                                      ps["rg"][:, :], ALU.mult)
                    eng.drain()
                    c = kb.inc("v", eng.tensor_tensor(
                        bf["xres"][:, kc, col0:col0 + 512], bf["to1"][:, :],
                        ps["nm"][:, :], ALU.add))
                    kb.mark_r(("ps_rg",), "v", c)
                    kb.mark_r(("ps_nm",), "v", c)
                    kb.mark_w(("xres",), "v", c)
                    kb.mark_w(("tmpo",), "v", c)
                kb.sec("v", v_xn)
        kb.psum_close([("ps_st",), ("ps_rg",), ("ps_nm",)])


def emit_final_add(kb, bf, dram):
    nc = kb.nc
    with ExitStack() as ctx:
        x1b = ctx.enter_context(nc.sbuf_tensor(_uname("x1b"), [128, 8, 512], F32))
        for S in range(4):
            col0 = 1 + 512 * S

            def y_ld(eng, S=S):
                sn = kb.dsem("x1l_d")
                kb.need_w("y", eng, ("x1b",))
                kb.need_r("y", eng, ("x1sp", S))
                c = kb.inc(sn, eng.dma_start(out=x1b[:, :, :],
                                             in_=dram["x1spill"][S, :, :, :]), 16)
                kb.mark_w(("x1b",), sn, c)
            kb.sec("y", y_ld)

            def v_add(eng, S=S, col0=col0):
                kb.need_r("v", eng, ("x1b",))
                kb.need_w("v", eng, ("xres",))
                c = kb.inc("v", eng.tensor_tensor(
                    bf["xres"][:, :, col0:col0 + 512], bf["xres"][:, :, col0:col0 + 512],
                    x1b[:, :, :], ALU.add))
                kb.mark_w(("xres",), "v", c)
                kb.mark_r(("x1b",), "v", c)
            kb.sec("v", v_add)
        kb.wflush()


# version 11
# speedup vs baseline: 1.0161x; 1.0083x over previous
import types

import numpy as np
import ml_dtypes
from contextlib import ExitStack

import concourse.bass as bass
import concourse.mybir as mybir
from concourse.bass_utils import run_bass_kernel_spmd

F32 = mybir.dt.float32
_NAMEC = [0]


def _uname(n):
    _NAMEC[0] += 1
    return f"sb_{n}_{_NAMEC[0]}"
BF16 = mybir.dt.bfloat16
AF = mybir.ActivationFunctionType
ALU = mybir.AluOpType
BF = ml_dtypes.bfloat16

BATCH, L, D = 4, 4096, 1024
T = 2048
NH, HD = 16, 64
NP = 8
DEPTH = 6
DI = 2730
NC_DI = 22
CW_LAST = DI - 21 * 128
WINDOW = 256
NEG = -1e30

NSUB = 4
SUB = 512
NKCOL = T // 128 + 2
RING = 6
CCGROUPS = [[0, 1], [2, 3], [4, 5], [6, 7]]


def _rotary_host():
    n = L
    half = np.arange(0, HD, 2, dtype=np.float64)
    inv_freq = 1.0 / 10000.0 ** (half / HD)
    t = np.arange(n, dtype=np.float64)
    freqs = t[:, None] * inv_freq[None, :]
    freqs = np.concatenate([freqs, freqs], -1)
    base_scale = (half + 0.4 * HD) / (1.4 * HD)
    power = (t - n // 2) / (WINDOW // 2)
    scale = base_scale[None, :] ** power[:, None]
    scale = np.concatenate([scale, scale], -1)
    cos = np.cos(freqs)
    sin = np.sin(freqs)
    rsign = np.where(np.arange(HD) < 32, -1.0, 1.0)
    QK = 8.0
    cosT = cos.T
    sinT = sin.T
    scT = scale.T
    csq = QK * cosT * scT
    ssq = QK * rsign[:, None] * sinT * scT
    csk = cosT / scT
    ssk = rsign[:, None] * sinT / scT
    out = []
    for tab in (csq, ssq, csk, ssk):
        dup = np.concatenate([tab, tab], 0)
        halves = np.stack([dup[:, :T], dup[:, T:]], 0)
        out.append(np.ascontiguousarray(halves.astype(np.float32)))
    return out


def _perm128():
    P = np.zeros((128, 128), dtype=np.float32)
    for m in range(128):
        k = (m + 32) % 64 + 64 * (m // 64)
        P[k, m] = 1.0
    return P.astype(BF)


def _tri_masks():
    c = np.arange(128)
    p = np.arange(128)[:, None]
    masku = np.where(p < c, NEG, 0.0).astype(BF)
    maskl = np.where(p > c, NEG, 0.0).astype(BF)
    return masku, maskl


def _hsel():
    hsel8 = np.zeros((128, 8, 16), dtype=np.float32)
    hselb = np.zeros((8, 16, 128), dtype=np.float32)
    for mc in range(8):
        for r in range(128):
            hsel8[r, mc, 2 * mc + (r >= 64)] = 1.0
            hselb[mc, 2 * mc + (r >= 64), r] = 1.0
    return hsel8.astype(BF), np.ascontiguousarray(hselb.transpose(1, 0, 2)).astype(BF)


def _sel2():
    s = np.zeros((2, 128), dtype=np.float32)
    s[0, :64] = 1.0
    s[1, 64:] = 1.0
    return s


def stage_inputs(inputs):
    x = np.asarray(inputs["L0_attn_x"])
    csq, ssq, csk, ssk = _rotary_host()
    p128 = _perm128()
    masku, maskl = _tri_masks()
    hsel8, hselb = _hsel()
    sel2 = _sel2()
    idn = np.eye(128, dtype=np.float32).astype(BF)
    onesrow = np.ones((1, 512), dtype=np.float32)

    qs = np.asarray(inputs["L0_ffn_q_scale"])
    ks = np.asarray(inputs["L0_ffn_k_scale"])
    def dup(v):
        return np.ascontiguousarray(np.tile(v, (1, 2))[:, :, None])
    def dup_perm(v):
        vp = v[:, (np.arange(64) + 32) % 64]
        return dup(vp)

    lng = np.asarray(inputs["L0_ffn_ln_g"])
    lnb = np.asarray(inputs["L0_ffn_ln_b"])
    b1 = np.asarray(inputs["L0_ffn_b1"])
    b2 = np.asarray(inputs["L0_ffn_b2"])
    bo = np.asarray(inputs["L0_attn_bo"])

    def featcol(v):
        return np.ascontiguousarray(
            v.reshape(v.shape[:-1] + (8, 128)).swapaxes(-1, -2))

    b1pad = np.zeros((6, 2, 22 * 128), dtype=np.float32)
    b1pad[:, 0, :DI] = b1[:, :DI]
    b1pad[:, 1, :DI] = b1[:, DI:]
    b1a = np.ascontiguousarray(
        b1pad[:, 0].reshape(6, 22, 128).swapaxes(-1, -2))
    b1g = np.ascontiguousarray(
        b1pad[:, 1].reshape(6, 22, 128).swapaxes(-1, -2))

    lngr = np.ascontiguousarray(lng.reshape(6, 8, 128))
    lnbr = np.ascontiguousarray(lnb.reshape(6, 8, 128))

    common = dict(
        wq=np.asarray(inputs["L0_attn_Wq"]), wk=np.asarray(inputs["L0_attn_Wk"]),
        wv=np.asarray(inputs["L0_attn_Wv"]), wo=np.asarray(inputs["L0_attn_Wo"]),
        bo_t=featcol(bo), wqkv=np.asarray(inputs["L0_ffn_Wqkv"]),
        wo_i=np.asarray(inputs["L0_ffn_Wo"]), w1=np.asarray(inputs["L0_ffn_W1"]),
        w2=np.asarray(inputs["L0_ffn_W2"]),
        qs_m=dup(qs), qs_p=dup_perm(qs), ks_m=dup(ks), ks_p=dup_perm(ks),
        lng_t=featcol(lng), lngr=lngr, lnbr=lnbr,
        b1a=b1a, b1g=b1g, b2_t=featcol(b2),
        p128=p128, masku=masku, maskl=maskl, idn=idn,
        sel2=sel2.astype(BF), sel2t=np.ascontiguousarray(sel2.T).astype(BF),
        selA=np.ascontiguousarray(sel2[0:1]), selB=np.ascontiguousarray(sel2[1:2]),
        qs_r=qs, ks_r=ks, onesrow=onesrow,
    )

    in_maps = []
    for c in range(8):
        b, h = c // 2, c % 2
        xb = x[b]
        xT = np.ascontiguousarray(xb.T)
        blocks = []
        order = list(range(4 * h, 4 * h + 4)) + list(range(4 * (1 - h), 4 * (1 - h) + 4))
        xr = xT.reshape(8, 128, 8, 512)
        for blk in order:
            blocks.append(np.ascontiguousarray(xr[:, :, blk, :].swapaxes(0, 1)))
        xs = np.stack(blocks, 0)
        m = dict(common)
        m["x"] = xs
        m["csq"] = csq[h]; m["ssq"] = ssq[h]; m["csk"] = csk[h]; m["ssk"] = ssk[h]
        m["flag"] = np.full((128, 1), float(h), dtype=np.float32)
        m["hmask"] = np.full((128, 1), 0.0 if h else -1e30, dtype=np.float32)
        in_maps.append(m)
    return in_maps


def unstage_output(results):
    out = np.empty((BATCH, L, D), dtype=np.float32)
    for c, r in enumerate(results):
        b, h = c // 2, c % 2
        o = np.asarray(r["out"])
        xT = o.transpose(1, 0, 2).reshape(D, T)
        out[b, 2048 * h: 2048 * h + 2048, :] = xT.T
    return out


class Rec:

    def __init__(self):
        self.fifo = {e: [] for e in ("t", "v", "s", "g", "y")}
        self.pos = {e: 0 for e in ("t", "v", "s", "g", "y")}
        self.sem_order = []
        self.barrier_dmasems = []


class KB:

    def __init__(self, nc, block, ctx, mode="direct", rec=None):
        self.nc = nc
        self.block = block
        self.ctx = ctx
        self.mode = mode
        self.rec = rec
        self._uid = 0
        self.cnt = {}
        self.sems = {}
        self.hwm = {e: {} for e in ("t", "v", "s", "g", "y")}
        self.last_w = {}
        self.readers = {}
        if mode == "replay":
            self.window = {e: [] for e in ("t", "v", "s", "g", "y")}
            for name in rec.sem_order:
                self.sems[name] = ctx.enter_context(nc.semaphore("sem_" + name))
                self.cnt[name] = 0
        else:
            for name in ("t", "v", "s", "g", "y", "cc"):
                self._mksem(name)

    def _mksem(self, name):
        self.sems[name] = self.ctx.enter_context(self.nc.semaphore("sem_" + name))
        self.cnt[name] = 0
        if self.mode == "record":
            self.rec.sem_order.append(name)

    def dsem(self, name):
        if name not in self.sems:
            self._mksem(name)
        return name

    def _wcompute(self, eng, sem_key, val):
        if val <= 0:
            return None
        h = self.hwm[eng]
        if h.get(sem_key, 0) >= val:
            return None
        h[sem_key] = val
        return (sem_key, val)

    def _emit(self, eng, engobj, pairs):
        if self.mode == "record":
            self.rec.fifo[eng].append(tuple(pairs))
        for sem_key, val in pairs:
            engobj.wait_ge(self.sems[sem_key], val)

    def _replay(self, eng, engobj):
        r = self.rec
        pairs = r.fifo[eng][r.pos[eng]]
        r.pos[eng] += 1
        for sem_key, val in pairs:
            engobj.wait_ge(self.sems[sem_key], val)

    def wflush(self):
        if self.mode != "replay":
            return
        for e in ("t", "v", "s", "g", "y"):
            fns = self.window[e]
            if not fns:
                continue
            self.window[e] = []
            def runner(eng, fns=fns):
                for fn in fns:
                    fn(eng)
            getattr(self.block, {"t": "tensor", "v": "vector", "s": "scalar",
                                 "g": "gpsimd", "y": "sync"}[e])(runner)

    def flush(self):
        assert self.mode == "replay"
        self.wflush()
        for e in ("t", "v", "s", "g", "y"):
            assert self.rec.pos[e] == len(self.rec.fifo[e]), \
                f"replay misalignment on {e}: {self.rec.pos[e]} != {len(self.rec.fifo[e])}"

    def phase_barrier(self):
        self._barn = getattr(self, "_barn", 0) + 1
        k = self._barn
        if "bar" not in self.sems:
            self._mksem("bar")
        bar = self.sems["bar"]
        if self.mode == "replay":
            dma_sems = self.rec.barrier_dmasems[k - 1]
        else:
            dma_sems = [n for n in self.sems
                        if n not in ("t", "v", "s", "g", "y", "bar")]
            if self.mode == "record":
                self.rec.barrier_dmasems.append(list(dma_sems))
        def mk(e):
            def f(eng):
                if e == "y":
                    for snn in dma_sems:
                        self.wait("y", eng, snn, self.cnt[snn])
                eng.drain()
                eng.nop().then_inc(bar, 1)
                eng.wait_ge(bar, 5 * k)
            return f
        for e in ("t", "v", "s", "g", "y"):
            self.sec(e, mk(e))
        self.cnt["bar"] = 5 * k
        for e in self.hwm:
            for snn, cv in self.cnt.items():
                if snn != "bar":
                    self.hwm[e][snn] = cv
            self.hwm[e]["bar"] = 5 * k
        self._pend_psum = []
        if self.mode == "replay":
            self.wflush()

    def psum_close(self, keys):
        if self.mode == "replay":
            self.wflush()
            return
        pend = getattr(self, "_pend_psum", [])
        for k in keys:
            w = self.last_w.pop(k, None)
            if w:
                pend.append(w)
            pend.extend(self.readers.pop(k, []))
        self._pend_psum = pend

    def psum_fence(self, eng, engobj):
        if self.mode == "replay":
            self._replay(eng, engobj)
            return
        pairs = []
        for sem, cnt in getattr(self, "_pend_psum", []):
            p = self._wcompute(eng, sem, cnt)
            if p:
                pairs.append(p)
        self._pend_psum = []
        self._emit(eng, engobj, pairs)

    def uid(self, base):
        self._uid += 1
        return f"{base}_{self._uid}"

    def sec(self, eng, fn):
        if self.mode == "replay":
            if fn.__closure__ is not None:
                cells = tuple(types.CellType(c.cell_contents)
                              for c in fn.__closure__)
                fn = types.FunctionType(fn.__code__, fn.__globals__,
                                        fn.__name__, fn.__defaults__, cells)
            self.window[eng].append(fn)
            return
        getattr(self.block, {"t": "tensor", "v": "vector", "s": "scalar",
                             "g": "gpsimd", "y": "sync"}[eng])(fn)

    def inc(self, eng_key, instr, n=1):
        self.cnt[eng_key] += n
        instr.then_inc(self.sems[eng_key], n)
        return self.cnt[eng_key]

    def wait(self, eng, engobj, sem_key, val):
        if self.mode == "replay":
            self._replay(eng, engobj)
            return
        p = self._wcompute(eng, sem_key, val)
        self._emit(eng, engobj, [p] if p else [])

    def mark_w(self, key, sem, cnt):
        if self.mode == "replay":
            return
        self.last_w[key] = (sem, cnt)
        self.readers[key] = []

    def mark_r(self, key, sem, cnt):
        if self.mode == "replay":
            return
        self.readers.setdefault(key, []).append((sem, cnt))

    def need_r(self, eng, engobj, key):
        if self.mode == "replay":
            self._replay(eng, engobj)
            return
        pairs = []
        w = self.last_w.get(key)
        if w:
            p = self._wcompute(eng, w[0], w[1])
            if p:
                pairs.append(p)
        self._emit(eng, engobj, pairs)

    def need_w(self, eng, engobj, key):
        if self.mode == "replay":
            self._replay(eng, engobj)
            return
        pairs = []
        w = self.last_w.get(key)
        if w:
            p = self._wcompute(eng, w[0], w[1])
            if p:
                pairs.append(p)
        for sem, cnt in self.readers.get(key, []):
            p = self._wcompute(eng, sem, cnt)
            if p:
                pairs.append(p)
        self._emit(eng, engobj, pairs)


def _ceil(a, b):
    return -(-a // b)


def mk_precast_chunks(dram):
    chunks = []
    for l in range(6):
        chunks.append((dram["wqkvb"][l, 0:512, :], dram["wqkv"][l, 0:512, :],
                       ("wqkvb", l, 0)))
        chunks.append((dram["wqkvb"][l, 512:1024, :], dram["wqkv"][l, 512:1024, :],
                       ("wqkvb", l, 1)))
        chunks.append((dram["wob"][l, :, :], dram["wo_i"][l, :, :], ("wob", l)))
        for j in range(4):
            chunks.append((dram["w1b"][l, 256 * j:256 * (j + 1), :],
                           dram["w1"][l, 256 * j:256 * (j + 1), :], ("w1b", l, j)))
        chunks.append((dram["w2b"][l, 0:1365, :], dram["w2"][l, 0:1365, :],
                       ("w2b", l, 0)))
        chunks.append((dram["w2b"][l, 1365:2730, :], dram["w2"][l, 1365:2730, :],
                       ("w2b", l, 1)))
    return chunks


def emit_precast_step(kb, chunks, n=1):
    for _ in range(n):
        if not chunks:
            return
        dst, src, key = chunks.pop(0)

        def g(eng, dst=dst, src=src, key=key):
            sn = kb.dsem("pc_d")
            c = kb.inc(sn, eng.dma_start(out=dst, in_=src), 16)
            kb.mark_w(key, sn, c)
        kb.sec("g", g)


def emit_ffn_loads(kb, bf, dram, l):
    def y(eng):
        for dst, src in (
            (bf["lng"][:, :], dram["lng_t"][l, :, :]),
            (bf["b1a"][:, :], dram["b1a"][l, :, :]),
            (bf["b1g"][:, :], dram["b1g"][l, :, :]),
            (bf["b2"][:, :], dram["b2_t"][l, :, :]),
        ):
            key = ("ffnpar",)
            kb.need_w("y", eng, key)
            c = kb.inc(kb.dsem("ffnpar_d"), eng.dma_start(out=dst, in_=src), 16)
        kb.mark_w(("ffnpar",), "ffnpar_d", kb.cnt["ffnpar_d"])
    kb.sec("y", y)

    def g_lnbr(eng):
        kb.need_w("g", eng, ("ffnparb",))
        sn = kb.dsem("ffnparb_d")
        kb.inc(sn, eng.dma_start(
            out=bf["lnbr"][0:1, :, :], in_=dram["lnbr"][l, :, :]), 16)
        c = kb.inc(sn, eng.dma_start(
            out=bf["lngr"][0:1, :, :], in_=dram["lngr"][l, :, :]), 16)
        kb.mark_w(("ffnparb",), sn, c)
    kb.sec("g", g_lnbr)

    if "w1b" in dram:
        def yw1(eng):
            kb.need_w("y", eng, ("w1",))
            for kc in range(8):
                kb.need_r("y", eng, ("w1b", l, kc // 2))
                c = kb.inc(kb.dsem("w1_d"), eng.dma_start(
                    out=bf["w1"][:, kc, :],
                    in_=dram["w1b"][l, 128 * kc:128 * (kc + 1), :]), 16)
            kb.mark_w(("w1",), "w1_d", kb.cnt["w1_d"])
        kb.sec("y", yw1)
    else:
        def g(eng):
            kb.need_w("g", eng, ("w1",))
            for kc in range(8):
                c = kb.inc(kb.dsem("w1_d"), eng.dma_start(
                    out=bf["w1"][:, kc, :], in_=dram["w1"][l, 128 * kc:128 * (kc + 1), :]), 16)
            kb.mark_w(("w1",), "w1_d", kb.cnt["w1_d"])
        kb.sec("g", g)


def emit_ffn(kb, bf, dram, l):
    nc = kb.nc
    emit_bnd_snapshot(kb, bf)

    def mk_shift_sq(tsb):
        col0 = 1 + SUB * tsb

        def v_shift(eng, tsb=tsb, col0=col0):
            kb.need_w("v", eng, ("xsb",))
            for hc in range(8, 16):
                kb.need_w("v", eng, ("h", hc))
            kb.need_r("v", eng, ("xres",))
            eng.tensor_copy(bf["xsb"][:, 8:12, :], bf["xres"][:, 0:4, col0:col0 + 512])
            if tsb == 0:
                i = eng.tensor_copy(
                    bf["xsb"][:, 12:16, :], bf["xres"][:, 4:8, col0 - 1:col0 - 1 + 512])
            else:
                kb.need_r("v", eng, ("xbnd",))
                eng.tensor_copy(bf["xsb"][:, 12:16, 0:1], bf["xbnd"][:, 4:8, tsb - 1:tsb])
                i = eng.tensor_copy(
                    bf["xsb"][:, 12:16, 1:512], bf["xres"][:, 4:8, col0:col0 + 511])
                kb.mark_r(("xbnd",), "v", kb.cnt["v"] + 1)
            c = kb.inc("v", i)
            kb.mark_w(("xsb",), "v", c)
            kb.mark_r(("xres",), "v", c)

        def s_sq(eng):
            kb.need_w("s", eng, ("xsq",))
            for hc in range(8):
                kb.need_w("s", eng, ("h", hc))
            kb.need_r("s", eng, ("xsb",))
            c = kb.inc("s", eng.activation(bf["xsq"][:, 0:8, :], bf["xsb"][:, 8:16, :], AF.Square))
            kb.mark_w(("xsq",), "s", c)
            kb.mark_r(("xsb",), "s", c)

        return v_shift, s_sq

    for tsb in range(NSUB):
        col0 = 1 + SUB * tsb
        with ExitStack() as ps_ctx:
            ps_st = ps_ctx.enter_context(nc.psum_tensor(kb.uid("ps_st"), [128, 512], F32))
            ps_rg = ps_ctx.enter_context(nc.psum_tensor(kb.uid("ps_rg"), [128, 512], F32))
            ps_nm = [ps_ctx.enter_context(nc.psum_tensor(kb.uid("ps_nm"), [128, 512], F32))
                     for _ in range(2)]
            ps_a = [ps_ctx.enter_context(nc.psum_tensor(kb.uid("ps_a"), [128, 512], F32)) for i in range(2)]
            ps_g = [ps_ctx.enter_context(nc.psum_tensor(kb.uid("ps_g"), [128, 512], F32)) for i in range(2)]

            if tsb == 0:
                vs0, sq0 = mk_shift_sq(0)
                kb.sec("v", vs0)
                kb.sec("s", sq0)

            def t_st(eng):
                kb.psum_fence("t", eng)
                kb.need_w("t", eng, ("ps_st",))
                kb.need_r("t", eng, ("xsb",))
                kb.need_r("t", eng, ("xsq",))
                kb.need_r("t", eng, ("ones128",))
                for kc in range(8):
                    eng.matmul(ps_st[0:1, :], bf["ones128"][:, :], bf["xsb"][:, 8 + kc, :],
                               start=(kc == 0), stop=(kc == 7), tile_position=(0, 0))
                for kc in range(8):
                    i = eng.matmul(ps_st[32:33, :], bf["ones128"][:, :], bf["xsq"][:, kc, :],
                                   start=(kc == 0), stop=(kc == 7), tile_position=(0, 32))
                c = kb.inc("t", i)
                kb.mark_w(("ps_st",), "t", c)
                kb.mark_r(("xsb",), "t", c)
                kb.mark_r(("xsq",), "t", c)
            kb.sec("t", t_st)

            st = bf["st"]
            def v_f3a(eng):
                kb.need_r("v", eng, ("ps_st",))
                kb.need_w("v", eng, ("st",))
                eng.tensor_scalar_mul(st[0:1, 0, :], ps_st[0:1, :], 1.0 / 1024.0)
                eng.drain()
                eng.scalar_tensor_tensor(st[0:1, 1, :], st[0:1, 0, :], -1.0, st[0:1, 0, :],
                                         ALU.mult, ALU.mult)
                eng.drain()
                c = kb.inc("v", eng.scalar_tensor_tensor(
                    st[0:1, 2, :], ps_st[32:33, :], 1.0 / 1024.0, st[0:1, 1, :],
                    ALU.mult, ALU.add))
                kb.mark_r(("ps_st",), "v", c)
                kb.mark_w(("st",), "v", c)
            kb.sec("v", v_f3a)

            def s_f3(eng):
                kb.need_r("s", eng, ("st",))
                kb.need_r("s", eng, ("eps",))
                kb.need_w("s", eng, ("st2",))
                eng.activation(st[0:1, 1, :], st[0:1, 2, :], AF.Ln, bias=bf["eps"][0:1, :])
                eng.drain()
                c = kb.inc("s", eng.activation(bf["st2"][0:1, 0, :], st[0:1, 1, :],
                                               AF.Exp, scale=-0.5))
                kb.mark_w(("st",), "s", c)
                kb.mark_w(("st2",), "s", c)
            kb.sec("s", s_f3)

            def v_f3b(eng):
                kb.need_r("v", eng, ("st",))
                kb.need_r("v", eng, ("st2",))
                kb.need_w("v", eng, ("st2b",))
                c = kb.inc("v", eng.scalar_tensor_tensor(
                    bf["st2b"][0:1, 0, :], st[0:1, 0, :], -1.0, bf["st2"][0:1, 0, :],
                    ALU.mult, ALU.mult))
                kb.mark_w(("st2b",), "v", c)
                kb.mark_r(("st2",), "v", c)
            kb.sec("v", v_f3b)

            def t_rg(eng):
                kb.need_r("t", eng, ("st2",))
                kb.need_r("t", eng, ("ones1",))
                kb.need_w("t", eng, ("ps_rg",))
                c = kb.inc("t", eng.matmul(ps_rg[:, :], bf["ones1"][:, :],
                                           bf["st2"][0:1, 0, :], start=True, stop=True))
                kb.mark_w(("ps_rg",), "t", c)
                kb.mark_r(("st2",), "t", c)
            kb.sec("t", t_rg)

            def s_rg(eng):
                kb.need_r("s", eng, ("ps_rg",))
                kb.need_w("s", eng, ("rgs",))
                c = kb.inc("s", eng.activation(bf["rgs"][:, :], ps_rg[:, :], AF.Copy))
                kb.mark_w(("rgs",), "s", c)
                kb.mark_r(("ps_rg",), "s", c)
            kb.sec("s", s_rg)

            for kc in range(8):
                def t_f4(eng, kc=kc):
                    kb.need_r("v", eng, ("st2b",)) if False else None
                    kb.need_r("t", eng, ("st2b",))
                    kb.need_w("t", eng, ("ps_nm", kc % 2))
                    kb.need_r("t", eng, ("ffnpar",))
                    kb.need_r("t", eng, ("ffnparb",))
                    kb.need_r("t", eng, ("onesrow",))
                    eng.matmul(ps_nm[kc % 2][:, :], bf["lngr"][0:1, kc, :],
                               bf["st2b"][0:1, 0, :], start=True, stop=False)
                    c = kb.inc("t", eng.matmul(ps_nm[kc % 2][:, :], bf["lnbr"][0:1, kc, :],
                                               bf["onesrow"][:, :], start=False, stop=True))
                    kb.mark_w(("ps_nm", kc % 2), "t", c)
                    kb.mark_r(("ffnpar",), "t", c)
                    kb.mark_r(("ffnparb",), "t", c)
                    kb.mark_r(("st2b",), "t", c)
                kb.sec("t", t_f4)

                def g_f4(eng, kc=kc):
                    tmpc = bf["h"][:, 2 * (kc % 2):2 * (kc % 2) + 2, :].bitcast(
                        F32).rearrange("p a b -> p (a b)")
                    kb.need_r("g", eng, ("rgs",))
                    kb.need_r("g", eng, ("xsb",))
                    kb.need_w("g", eng, ("tmpc", kc % 2))
                    if kc < 2:
                        for hc in range(4):
                            kb.need_w("g", eng, ("h", hc))
                        kb.need_w("g", eng, ("xsq",))
                    c = kb.inc("g", eng.tensor_mul(tmpc, bf["xsb"][:, 8 + kc, :], bf["rgs"][:, :]))
                    kb.mark_w(("tmpc", kc % 2), "g", c)
                    kb.mark_r(("xsb",), "g", c)
                    kb.mark_r(("rgs",), "g", c)
                kb.sec("g", g_f4)

                def v_f4(eng, kc=kc):
                    tmpc = bf["h"][:, 2 * (kc % 2):2 * (kc % 2) + 2, :].bitcast(
                        F32).rearrange("p a b -> p (a b)")
                    kb.need_r("v", eng, ("tmpc", kc % 2))
                    kb.need_r("v", eng, ("ps_nm", kc % 2))
                    kb.need_r("v", eng, ("ffnpar",))
                    c = kb.inc("v", eng.scalar_tensor_tensor(
                        bf["xn"][:, kc, :], tmpc, bf["lng"][:, kc:kc + 1],
                        ps_nm[kc % 2][:, :], ALU.mult, ALU.add))
                    kb.mark_r(("tmpc", kc % 2), "v", c)
                    kb.mark_r(("ps_nm", kc % 2), "v", c)
                    kb.mark_w(("xn", kc), "v", c)
                kb.sec("v", v_f4)

            for c_i in range(NC_DI):
                cw = CW_LAST if c_i == NC_DI - 1 else 128
                def t_f5(eng, c_i=c_i, cw=cw):
                    kb.need_r("t", eng, ("w1",))
                    for kc in range(8):
                        kb.need_r("t", eng, ("xn", kc))
                    kb.need_w("t", eng, ("ps_a", c_i % 2))
                    kb.need_w("t", eng, ("ps_g", c_i % 2))
                    for kc in range(8):
                        eng.matmul(ps_a[c_i % 2][0:cw, :],
                                   bf["w1"][:, kc, 128 * c_i:128 * c_i + cw],
                                   bf["xn"][:, kc, :], start=(kc == 0), stop=(kc == 7))
                    for kc in range(8):
                        i = eng.matmul(ps_g[c_i % 2][0:cw, :],
                                       bf["w1"][:, kc, DI + 128 * c_i:DI + 128 * c_i + cw],
                                       bf["xn"][:, kc, :], start=(kc == 0), stop=(kc == 7))
                    c = kb.inc("t", i)
                    kb.mark_w(("ps_a", c_i % 2), "t", c)
                    kb.mark_w(("ps_g", c_i % 2), "t", c)
                    kb.mark_r(("w1",), "t", c)
                kb.sec("t", t_f5)

                def s_f5(eng, c_i=c_i, cw=cw):
                    kb.need_r("s", eng, ("ps_g", c_i % 2))
                    kb.need_w("s", eng, ("gsb", c_i % 2))
                    kb.need_r("s", eng, ("ffnpar",))
                    c = kb.inc("s", eng.activation(
                        bf["gsb"][0:cw, c_i % 2, :], ps_g[c_i % 2][0:cw, :], AF.Gelu,
                        bias=bf["b1g"][0:cw, c_i:c_i + 1]))
                    kb.mark_w(("gsb", c_i % 2), "s", c)
                    kb.mark_r(("ps_g", c_i % 2), "s", c)
                    kb.mark_r(("ffnpar",), "s", c)
                kb.sec("s", s_f5)

                def v_f5(eng, c_i=c_i, cw=cw):
                    if c_i < 2:
                        kb.need_w("v", eng, ("tmpc",))
                    if c_i < 8:
                        kb.need_w("v", eng, ("xsq",))
                    if 8 <= c_i < 16:
                        kb.need_w("v", eng, ("xsb",))
                    kb.need_r("v", eng, ("ps_a", c_i % 2))
                    kb.need_r("v", eng, ("gsb", c_i % 2))
                    kb.need_w("v", eng, ("h", c_i))
                    c = kb.inc("v", eng.scalar_tensor_tensor(
                        bf["h"][0:cw, c_i, :], ps_a[c_i % 2][0:cw, :],
                        bf["b1a"][0:cw, c_i:c_i + 1], bf["gsb"][0:cw, c_i % 2, :],
                        ALU.add, ALU.mult))
                    kb.mark_w(("h", c_i), "v", c)
                    kb.mark_r(("ps_a", c_i % 2), "v", c)
                    kb.mark_r(("gsb", c_i % 2), "v", c)
                kb.sec("v", v_f5)

        kb.psum_close([("ps_st",), ("ps_rg",), ("ps_nm", 0), ("ps_nm", 1),
                       ("ps_a", 0), ("ps_a", 1), ("ps_g", 0), ("ps_g", 1)])
        with ExitStack() as ps_ctx:
            ps_y = ps_ctx.enter_context(nc.psum_tensor(kb.uid("ps_y"), [128, 8, 512], F32))

            def t_fence(eng):
                kb.psum_fence("t", eng)
            kb.sec("t", t_fence)

            def g_w2(c_i):
                cw = CW_LAST if c_i == NC_DI - 1 else 128
                if "w2b" in dram:
                    def yld(eng):
                        kb.need_w("y", eng, ("w2", c_i % 2))
                        if 128 * c_i < 1365:
                            kb.need_r("y", eng, ("w2b", l, 0))
                        if 128 * c_i + cw > 1365:
                            kb.need_r("y", eng, ("w2b", l, 1))
                        sname = kb.dsem(f"w2_d{c_i % 2}")
                        c = kb.inc(sname, eng.dma_start(
                            out=bf["w2"][0:cw, c_i % 2, :],
                            in_=dram["w2b"][l, 128 * c_i:128 * c_i + cw, :]), 16)
                        kb.mark_w(("w2", c_i % 2), sname, c)
                    kb.sec("y", yld)
                    return
                def g(eng):
                    kb.need_w("g", eng, ("w2", c_i % 2))
                    sname = kb.dsem(f"w2_d{c_i % 2}")
                    c = kb.inc(sname, eng.dma_start(
                        out=bf["w2"][0:cw, c_i % 2, :],
                        in_=dram["w2"][l, 128 * c_i:128 * c_i + cw, :]), 16)
                    kb.mark_w(("w2", c_i % 2), sname, c)
                kb.sec("g", g)

            def t_w2(c_i):
                cw = CW_LAST if c_i == NC_DI - 1 else 128
                def t(eng):
                    kb.need_r("t", eng, ("w2", c_i % 2))
                    kb.need_r("t", eng, ("h", c_i))
                    if c_i == 0:
                        kb.need_w("t", eng, ("ps_y",))
                    for dc in range(8):
                        i = eng.matmul(ps_y[:, dc, :],
                                       bf["w2"][0:cw, c_i % 2, 128 * dc:128 * (dc + 1)],
                                       bf["h"][0:cw, c_i, :],
                                       start=(c_i == 0), stop=(c_i == NC_DI - 1))
                    c = kb.inc("t", i)
                    kb.mark_r(("w2", c_i % 2), "t", c)
                    kb.mark_r(("h", c_i), "t", c)
                    if c_i == NC_DI - 1:
                        kb.mark_w(("ps_y",), "t", c)
                kb.sec("t", t)

            g_w2(0)
            g_w2(1)
            for c_i in range(NC_DI):
                t_w2(c_i)
                if c_i + 2 < NC_DI:
                    g_w2(c_i + 2)
                if c_i == 15 and tsb + 1 < NSUB:
                    vs1, sq1 = mk_shift_sq(tsb + 1)
                    kb.sec("v", vs1)
                    kb.sec("s", sq1)

            def v_f6(eng):
                kb.need_r("v", eng, ("ps_y",))
                kb.need_w("v", eng, ("xres",))
                for dc in range(8):
                    i = eng.scalar_tensor_tensor(
                        bf["xres"][:, dc, col0:col0 + 512], ps_y[:, dc, :],
                        bf["b2"][:, dc:dc + 1], bf["xres"][:, dc, col0:col0 + 512],
                        ALU.add, ALU.add)
                c = kb.inc("v", i)
                kb.mark_w(("xres",), "v", c)
                kb.mark_r(("ps_y",), "v", c)
            kb.sec("v", v_f6)
        kb.psum_close([("ps_y",)])


def alloc_common_bufs(nc, ctx):
    bf = {}
    A = lambda name, shape, dt: ctx.enter_context(nc.sbuf_tensor(_uname(name), shape, dt))
    bf["xres"] = A("xres", [128, 8, T + 1], F32)
    bf["st"] = A("st", [1, 3, 512], F32)
    bf["ones128"] = A("ones128", [128, 1], BF16)
    bf["onesrow"] = A("onesrow", [1, 512], BF16)
    bf["eps"] = A("eps", [1, 1], F32)
    return bf


def alloc_ffn_bufs(nc, ctx, bf):
    A = lambda name, shape, dt: ctx.enter_context(nc.sbuf_tensor(_uname(name), shape, dt))
    bf["w1"] = A("w1", [128, 8, 5460], BF16)
    bf["w2"] = A("w2", [128, 2, 1024], BF16)
    bf["xn"] = A("xn", [128, 8, 512], BF16)
    bf["gsb"] = A("gsb", [128, 2, 512], BF16)
    bf["h"] = A("h", [128, NC_DI, 512], BF16)
    bf["xsq"] = bf["h"]
    bf["xsb"] = bf["h"]
    bf["tmpc"] = None
    bf["st2"] = A("st2", [1, 2, 512], F32)
    bf["lngr"] = A("lngr", [1, 8, 128], BF16)
    bf["lnbr"] = A("lnbr", [1, 8, 128], BF16)
    bf["lng"] = A("lng", [128, 8], F32)
    bf["st2b"] = A("st2b", [1, 1, 512], BF16)
    bf["rgs"] = A("rgs", [128, 512], F32)
    bf["b1a"] = A("b1a", [128, NC_DI], F32)
    bf["b1g"] = A("b1g", [128, NC_DI], F32)
    bf["b2"] = A("b2", [128, 8], F32)
    return bf


def emit_init_consts(kb, bf):
    def g_init(eng):
        c = kb.inc("g", eng.memset(bf["ones128"][:, :], 1.0))
        kb.mark_w(("ones128",), "g", c)
    kb.sec("g", g_init)

    def v_init(eng):
        eng.memset(bf["onesrow"][:, :], 1.0)
        if "ones1" in bf:
            eng.memset(bf["ones1"][:, :], 1.0)
            kb.mark_w(("ones1",), "v", kb.cnt["v"] + 1)
        c = kb.inc("v", eng.memset(bf["eps"][:, :], 1e-5))
        kb.mark_w(("onesrow",), "v", c)
        kb.mark_w(("eps",), "v", c)
    kb.sec("v", v_init)


def build_ffn_test():
    nc = bass.Bass("TRN2", target_bir_lowering=False, debug=False)
    dram = {}
    dram["w1"] = nc.dram_tensor("w1", [6, 1024, 5460], F32, kind="ExternalInput")
    dram["w2"] = nc.dram_tensor("w2", [6, DI, 1024], F32, kind="ExternalInput")
    dram["lngr"] = nc.dram_tensor("lngr", [6, 8, 128], F32, kind="ExternalInput")
    dram["lng_t"] = nc.dram_tensor("lng_t", [6, 128, 8], F32, kind="ExternalInput")
    dram["lnbr"] = nc.dram_tensor("lnbr", [6, 8, 128], F32, kind="ExternalInput")
    dram["b1a"] = nc.dram_tensor("b1a", [6, 128, NC_DI], F32, kind="ExternalInput")
    dram["b1g"] = nc.dram_tensor("b1g", [6, 128, NC_DI], F32, kind="ExternalInput")
    dram["b2_t"] = nc.dram_tensor("b2_t", [6, 128, 8], F32, kind="ExternalInput")
    x_in = nc.dram_tensor("x_in", [128, 8, T + 1], F32, kind="ExternalInput")
    out = nc.dram_tensor("out", [128, 8, T + 1], F32, kind="ExternalOutput")

    with ExitStack() as ctx:
        block = ctx.enter_context(nc.Block())
        kb = KB(nc, block, ctx)
        bf = alloc_common_bufs(nc, ctx)
        bf["xbnd"] = ctx.enter_context(nc.sbuf_tensor(_uname("xbnd"), [128, 8, 3], F32))
        bf["ones1"] = ctx.enter_context(nc.sbuf_tensor(_uname("ones1"), [1, 128], F32))
        alloc_ffn_bufs(nc, ctx, bf)
        emit_init_consts(kb, bf)

        def y_in(eng):
            c = kb.inc(kb.dsem("xin_d"), eng.dma_start(out=bf["xres"][:, :, :], in_=x_in[:, :, :]), 16)
            kb.mark_w(("xres",), "xin_d", c)
        kb.sec("y", y_in)

        emit_ffn_loads(kb, bf, dram, 0)
        emit_ffn(kb, bf, dram, 0)

        def y_out(eng):
            kb.need_r("y", eng, ("xres",))
            sn = kb.dsem("out_d")
            kb.inc(sn, eng.dma_start(out=out[:, :, :], in_=bf["xres"][:, :, :]), 16)
            kb.wait("y", eng, sn, kb.cnt[sn])
        kb.sec("y", y_out)
    return nc


def alloc_persist_bufs(nc, ctx, bf):
    A = lambda name, shape, dt: ctx.enter_context(nc.sbuf_tensor(_uname(name), shape, dt))
    bf["xrow"] = A("xrow", [128, 8], F32)
    bf["xrow2"] = A("xrow2", [128, 8], F32)
    bf["qsl"] = A("qsl", [128, 4], F32)
    bf["negB"] = A("negB", [128, 6], F32)
    bf["p128"] = A("p128", [128, 128], BF16)
    bf["idn"] = A("idn", [128, 128], BF16)
    bf["masku"] = A("masku", [128, 128], BF16)
    bf["maskl"] = A("maskl", [128, 128], BF16)
    bf["sel2t"] = A("sel2t", [128, 2], BF16)
    bf["sel2"] = A("sel2", [2, 128], BF16)
    bf["selA"] = A("selA", [1, 128], F32)
    bf["selB"] = A("selB", [1, 128], F32)
    bf["flag"] = A("flag", [128, 1], F32)
    bf["hmask"] = A("hmask", [128, 1], F32)
    bf["ones1"] = A("ones1", [1, 128], F32)
    bf["qsraw"] = A("qsraw", [1, 2, 64], F32)
    bf["xbnd"] = A("xbnd", [128, 8, 3], F32)
    return bf


def emit_bnd_snapshot(kb, bf):
    def v(eng):
        kb.need_r("v", eng, ("xres",))
        kb.need_w("v", eng, ("xbnd",))
        c = kb.inc("v", eng.tensor_copy(bf["xbnd"][:, :, :],
                                        bf["xres"][:, :, 512:1537:512]))
        kb.mark_w(("xbnd",), "v", c)
        kb.mark_r(("xres",), "v", c)
    kb.sec("v", v)


def alloc_attn_bufs(nc, ctx, bf):
    A = lambda name, shape, dt: ctx.enter_context(nc.sbuf_tensor(_uname(name), shape, dt))
    bf["wqg"] = A("wqg", [128, 8, 1024], BF16)
    bf["wkg"] = A("wkg", [128, 8, 1024], BF16)
    bf["wvg"] = A("wvg", [128, 8, 1024], BF16)
    bf["wog"] = A("wog", [128, 8, 1024], BF16)
    bf["krr"] = A("krr", [128, 8, RING * 128], BF16)
    bf["vrr"] = A("vrr", [128, RING, NH * 65], BF16)
    bf["qr"] = A("qr", [128, 8, 512], BF16)
    bf["xsba"] = A("xsba", [128, 8, 512], BF16)
    bf["avun"] = bf["xsba"]
    bf["kqbf"] = A("kqbf", [128, 512], BF16)
    bf["avbf"] = A("avbf", [128, 512], BF16)
    bf["qsq"] = A("qsq", [128, 512], BF16)
    bf["t1"] = A("t1", [128, 512], F32)
    bf["t2"] = A("t2", [128, 512], F32)
    bf["t3"] = A("t3", [128, 512], F32)
    bf["rqa"] = A("rqa", [2, 512], BF16)
    bf["rqb"] = A("rqb", [2, 512], F32)
    bf["eps30"] = A("eps30", [2, 1], F32)
    bf["recdA"] = A("recdA", [1, 512], F32)
    bf["recdB"] = A("recdB", [1, 512], F32)
    bf["expt"] = A("expt", [128, 2, 1536], BF16)
    bf["tab"] = A("tab", [128, 2, 512], F32)
    return bf


def emit_attn_consts(kb, bf, dram):
    def y(eng):
        sn = kb.dsem("const_d")
        for dst, src in (
            (bf["p128"][:, :], dram["p128"][:, :]),
            (bf["idn"][:, :], dram["idn"][:, :]),
            (bf["masku"][:, :], dram["masku"][:, :]),
            (bf["maskl"][:, :], dram["maskl"][:, :]),
            (bf["sel2t"][:, :], dram["sel2t"][:, :]),
            (bf["sel2"][:, :], dram["sel2"][:, :]),
            (bf["selA"][:, :], dram["selA"][:, :]),
            (bf["selB"][:, :], dram["selB"][:, :]),
            (bf["flag"][:, :], dram["flag"][:, :]),
            (bf["hmask"][:, :], dram["hmask"][:, :]),
        ):
            c = kb.inc(sn, eng.dma_start(out=dst, in_=src), 16)
        kb.mark_w(("consts",), sn, kb.cnt[sn])
    kb.sec("y", y)



def emit_vrr_init(kb, bf):
    def v(eng):
        kb.need_w("v", eng, ("eps30",))
        eng.memset(bf["eps30"][:, :], 1e-30)
        kb.mark_w(("eps30",), "v", kb.cnt["v"] + 1)
        for s in range(RING):
            for half in range(2):
                kb.need_w("v", eng, ("vrs", s, half))
        c = kb.inc("v", eng.memset(bf["vrr"][:, :, :], 1.0))
        for s in range(RING):
            for half in range(2):
                kb.mark_w(("vrs", s, half), "v", c)
    kb.sec("v", v)


def emit_negB(kb, bf, dram, mscq, msck):
    for l in range(DEPTH):
        def y(eng, l=l):
            sn = kb.dsem("qsl_d")
            kb.need_w("y", eng, ("qsraw",))
            kb.inc(sn, eng.dma_start(out=bf["qsraw"][0:1, 0, :], in_=dram["qs_r"][l:l + 1, :]), 16)
            c = kb.inc(sn, eng.dma_start(out=bf["qsraw"][0:1, 1, :], in_=dram["ks_r"][l:l + 1, :]), 16)
            kb.mark_w(("qsraw",), sn, c)
        kb.sec("y", y)

        def v(eng, l=l):
            kb.need_r("v", eng, ("qsraw",))
            eng.tensor_reduce(bf["st"][0:1, 0, 0:1], bf["qsraw"][0:1, 0, :],
                              mybir.AxisListType.X, ALU.max, apply_absolute_value=True)
            eng.tensor_reduce(bf["st"][0:1, 0, 1:2], bf["qsraw"][0:1, 1, :],
                              mybir.AxisListType.X, ALU.max, apply_absolute_value=True)
            eng.drain()
            c = kb.inc("v", eng.scalar_tensor_tensor(
                bf["st"][0:1, 0, 2:3], bf["st"][0:1, 0, 0:1], -8.0 * mscq * msck,
                bf["st"][0:1, 0, 1:2], ALU.mult, ALU.mult))
            kb.mark_w(("st",), "v", c)
            kb.mark_r(("qsraw",), "v", c)
        kb.sec("v", v)

        def t(eng, l=l):
            kb.need_r("t", eng, ("st",))
            kb.need_r("t", eng, ("ones1",))
            kb.need_w("t", eng, ("ps_nb",))
            c = kb.inc("t", eng.matmul(bf["ps_nb"][:, 0:1], bf["ones1"][:, :],
                                       bf["st"][0:1, 0, 2:3], start=True, stop=True))
            kb.mark_w(("ps_nb",), "t", c)
            kb.mark_r(("st",), "t", c)
        kb.sec("t", t)

        def v2(eng, l=l):
            kb.need_r("v", eng, ("ps_nb",))
            c = kb.inc("v", eng.tensor_copy(bf["negB"][:, l:l + 1], bf["ps_nb"][:, 0:1]))
            kb.mark_w(("negB",), "v", c)
            kb.mark_r(("ps_nb",), "v", c)
        kb.sec("v", v2)


def _ring_runs(kcols):
    runs = []
    i = 0
    while i < len(kcols):
        j = i
        while j + 1 < len(kcols) and (kcols[j + 1] % RING) == (kcols[j] % RING) + 1:
            j += 1
        runs.append((i, j - i + 1, kcols[i] % RING))
        i = j + 1
    return runs


def emit_attn_loads(kb, bf, dram, l):
    if "wqkvb" in dram:
        for gname, key, coff in (("wqg", "wq_g", 0), ("wkg", "wk_g", 1024),
                                 ("wvg", "wv_g", 2048)):
            def yld(eng, gname=gname, key=key, coff=coff):
                kb.need_w("y", eng, (key,))
                kb.need_r("y", eng, ("wqkvb", l, 0))
                kb.need_r("y", eng, ("wqkvb", l, 1))
                sn = kb.dsem(key + "_d")
                for kc in range(8):
                    c = kb.inc(sn, eng.dma_start(
                        out=bf[gname][:, kc, :],
                        in_=dram["wqkvb"][l, 128 * kc:128 * (kc + 1), coff:coff + 1024]), 16)
                kb.mark_w((key,), sn, kb.cnt[sn])
            kb.sec("y", yld)

        def y_wo(eng):
            kb.need_w("y", eng, ("wo_g",))
            kb.need_r("y", eng, ("wob", l))
            sn = kb.dsem("wo_g_d")
            for kc in range(8):
                c = kb.inc(sn, eng.dma_start(
                    out=bf["wog"][:, kc, :],
                    in_=dram["wob"][l, 128 * kc:128 * (kc + 1), :]), 16)
            kb.mark_w(("wo_g",), sn, kb.cnt[sn])
        kb.sec("y", y_wo)
    else:
        for gname, key, coff in (("wqg", "wq_g", 0), ("wkg", "wk_g", 1024),
                                 ("wvg", "wv_g", 2048)):
            def g(eng, gname=gname, key=key, coff=coff):
                kb.need_w("g", eng, (key,))
                sn = kb.dsem(key + "_d")
                for kc in range(8):
                    c = kb.inc(sn, eng.dma_start(
                        out=bf[gname][:, kc, :],
                        in_=dram["wqkv"][l, 128 * kc:128 * (kc + 1), coff:coff + 1024]), 16)
                kb.mark_w((key,), sn, kb.cnt[sn])
            kb.sec("g", g)

        def g_wo(eng):
            kb.need_w("g", eng, ("wo_g",))
            sn = kb.dsem("wo_g_d")
            for kc in range(8):
                c = kb.inc(sn, eng.dma_start(
                    out=bf["wog"][:, kc, :],
                    in_=dram["wo_i"][l, 128 * kc:128 * (kc + 1), :]), 16)
            kb.mark_w(("wo_g",), sn, kb.cnt[sn])
        kb.sec("g", g_wo)

    def y_qsl(eng):
        kb.need_w("y", eng, ("qsl",))
        sn = kb.dsem("qsl2_d")
        for i, nm in enumerate(("qs_m", "qs_p", "ks_m", "ks_p")):
            c = kb.inc(sn, eng.dma_start(out=bf["qsl"][:, i:i + 1], in_=dram[nm][l, :, :]), 16)
        kb.mark_w(("qsl",), sn, kb.cnt[sn])
    kb.sec("y", y_qsl)


def emit_tab_load(kb, bf, dram, kind, toff, w):
    t0, t1n = ("csq", "ssq") if kind == "q" else ("csk", "ssk")
    def y(eng):
        kb.need_w("y", eng, ("tab",))
        sn = kb.dsem("tab_d")
        kb.inc(sn, eng.dma_start(out=bf["tab"][:, 0, 0:w], in_=dram[t0][:, toff:toff + w]), 16)
        c = kb.inc(sn, eng.dma_start(out=bf["tab"][:, 1, 0:w], in_=dram[t1n][:, toff:toff + w]), 16)
        kb.mark_w(("tab",), sn, c)
    kb.sec("y", y)


def emit_shift_cast_attn(kb, bf, toff, w, bnd=None):
    def v(eng):
        kb.need_w("v", eng, ("xsba",))
        kb.need_r("v", eng, ("xres",))
        eng.tensor_copy(bf["xsba"][:, 0:4, 0:w], bf["xres"][:, 0:4, 1 + toff:1 + toff + w])
        if bnd is None:
            i = eng.tensor_copy(bf["xsba"][:, 4:8, 0:w], bf["xres"][:, 4:8, toff:toff + w])
        else:
            kb.need_r("v", eng, ("xbnd",))
            eng.tensor_copy(bf["xsba"][:, 4:8, 0:1], bf["xbnd"][:, 4:8, bnd:bnd + 1])
            i = eng.tensor_copy(bf["xsba"][:, 4:8, 1:w],
                                bf["xres"][:, 4:8, toff + 1:toff + w])
            kb.mark_r(("xbnd",), "v", kb.cnt["v"] + 1)
        c = kb.inc("v", i)
        kb.mark_w(("xsba",), "v", c)
        kb.mark_r(("xres",), "v", c)
    kb.sec("v", v)


def emit_proj_postproc(kb, bf, kind, l, toff, w, kcols, ps):
    wg = bf["wqg"] if kind == "q" else bf["wkg"]
    wkey = "wq_g" if kind == "q" else "wk_g"
    tab = bf["tab"]
    tabkey = "tab"
    s_m, s_p = (0, 1) if kind == "q" else (2, 3)
    runs = _ring_runs(kcols) if kind == "k" else None

    P, VP, R, SQ, RB, VR, GA, VR2 = [], [], [], [], [], [], [], []
    for mc in range(8):
        par = mc % 2
        pp = ps["pp"][par]
        ppk = ("pp", par)
        rot, rotk = ps["rot"][par], ("psrot", par)
        ss, ssk = ps["ss"][par], ("psss", par)
        rb, rbk = ps["rb"][par], ("psrb", par)

        def t_proj(eng, mc=mc, pp=pp, ppk=ppk):
            kb.psum_fence("t", eng)
            kb.need_r("t", eng, (wkey,))
            kb.need_r("t", eng, ("xsba",))
            kb.need_w("t", eng, ppk)
            for kc in range(8):
                i = eng.matmul(pp[:, 0:w], wg[:, kc, 128 * mc:128 * (mc + 1)],
                               bf["xsba"][:, kc, 0:w], start=(kc == 0), stop=(kc == 7))
            c = kb.inc("t", i)
            kb.mark_w(ppk, "t", c)
            kb.mark_r((wkey,), "t", c)
            kb.mark_r(("xsba",), "t", c)
        P.append(t_proj)

        def v_pp(eng, pp=pp, ppk=ppk):
            kb.need_r("v", eng, ppk)
            kb.need_w("v", eng, ("kqbf",))
            kb.need_w("v", eng, ("qsq",))
            eng.tensor_copy(bf["kqbf"][:, 0:w], pp[:, 0:w])
            eng.drain()
            c = kb.inc("v", eng.scalar_tensor_tensor(
                bf["qsq"][:, 0:w], pp[:, 0:w], 1.0, bf["kqbf"][:, 0:w],
                ALU.mult, ALU.mult))
            kb.mark_w(("kqbf",), "v", c)
            kb.mark_w(("qsq",), "v", c)
            kb.mark_r(ppk, "v", c)
        VP.append(v_pp)

        def t_rot(eng, rot=rot, rotk=rotk, ss=ss, ssk=ssk):
            kb.need_r("t", eng, ("kqbf",))
            kb.need_r("t", eng, ("qsq",))
            kb.need_r("t", eng, ("consts",))
            kb.need_w("t", eng, rotk)
            kb.need_w("t", eng, ssk)
            eng.matmul(rot[:, 0:w], bf["p128"][:, :], bf["kqbf"][:, 0:w],
                       start=True, stop=True)
            c = kb.inc("t", eng.matmul(ss[0:2, 0:w], bf["sel2t"][:, :],
                                       bf["qsq"][:, 0:w], start=True, stop=True))
            kb.mark_w(rotk, "t", c)
            kb.mark_w(ssk, "t", c)
            kb.mark_r(("kqbf",), "t", c)
            kb.mark_r(("qsq",), "t", c)
        R.append(t_rot)

        def s_rq(eng, ss=ss, ssk=ssk):
            kb.need_r("s", eng, ssk)
            kb.need_w("s", eng, ("rqa",))
            kb.need_w("s", eng, ("rqb",))
            kb.need_r("s", eng, ("eps30",))
            eng.activation(bf["rqb"][0:2, 0:w], ss[0:2, 0:w], AF.Ln,
                           bias=bf["eps30"][0:2, :])
            eng.drain()
            c = kb.inc("s", eng.activation(bf["rqa"][0:2, 0:w], bf["rqb"][0:2, 0:w],
                                           AF.Exp, scale=-0.5))
            kb.mark_w(("rqa",), "s", c)
            kb.mark_w(("rqb",), "s", c)
            kb.mark_r(ssk, "s", c)
        SQ.append(s_rq)

        def t_rb(eng, rb=rb, rbk=rbk):
            kb.need_r("t", eng, ("rqa",))
            kb.need_r("t", eng, ("consts",))
            kb.need_w("t", eng, rbk)
            c = kb.inc("t", eng.matmul(rb[:, 0:w], bf["sel2"][:, :],
                                       bf["rqa"][0:2, 0:w], start=True, stop=True))
            kb.mark_w(rbk, "t", c)
            kb.mark_r(("rqa",), "t", c)
        RB.append(t_rb)

        def v_rt(eng, mc=mc, pp=pp, ppk=ppk, rot=rot, rotk=rotk):
            kb.need_r("v", eng, ppk)
            kb.need_r("v", eng, rotk)
            kb.need_r("v", eng, (tabkey,))
            kb.need_r("v", eng, ("qsl",))
            kb.need_w("v", eng, ("t1",))
            kb.need_w("v", eng, ("t2",))
            i = eng.scalar_tensor_tensor(bf["t1"][:, 0:w], pp[:, 0:w],
                                         bf["qsl"][:, s_m:s_m + 1], tab[:, 0, 0:w],
                                         ALU.mult, ALU.mult)
            c12 = kb.inc("v", eng.scalar_tensor_tensor(
                bf["t2"][:, 0:w], rot[:, 0:w],
                bf["qsl"][:, s_p:s_p + 1], tab[:, 1, 0:w], ALU.mult, ALU.mult))
            kb.mark_w(("t1",), "v", c12)
            kb.mark_w(("t2",), "v", c12)
            kb.mark_r(ppk, "v", c12)
            kb.mark_r(rotk, "v", c12)
            kb.mark_r((tabkey,), "v", c12)
        VR.append(v_rt)

        def g_add(eng, mc=mc):
            kb.need_r("g", eng, ("t1",))
            kb.need_r("g", eng, ("t2",))
            kb.need_w("g", eng, ("t3",))
            c = kb.inc("g", eng.tensor_add(bf["t3"][:, 0:w], bf["t1"][:, 0:w], bf["t2"][:, 0:w]))
            kb.mark_w(("t3",), "g", c)
            kb.mark_r(("t1",), "g", c)
            kb.mark_r(("t2",), "g", c)
        GA.append(g_add)

        def v_rt2(eng, mc=mc, rb=rb, rbk=rbk):
            kb.need_r("v", eng, ("t3",))
            kb.need_r("v", eng, rbk)
            if kind == "q":
                i = eng.tensor_tensor(bf["qr"][:, mc, 0:w], bf["t3"][:, 0:w],
                                      rb[:, 0:w], ALU.mult)
                c = kb.inc("v", i)
                kb.mark_w(("qr",), "v", c)
            else:
                for kcol in kcols:
                    kb.need_w("v", eng, ("krs", kcol % RING, mc))
                for (toff128, n, slot0) in runs:
                    i = eng.tensor_tensor(
                        bf["krr"][:, mc, 128 * slot0:128 * (slot0 + n)],
                        bf["t3"][:, 128 * toff128:128 * (toff128 + n)],
                        rb[:, 128 * toff128:128 * (toff128 + n)], ALU.mult)
                c = kb.inc("v", i)
                for kcol in kcols:
                    kb.mark_w(("krs", kcol % RING, mc), "v", c)
            kb.mark_r(rbk, "v", c)
            kb.mark_r(("t3",), "v", c)
        VR2.append(v_rt2)

    n8 = 8
    for it in range(n8 + 2):
        if it < n8:
            kb.sec("t", P[it])
        if 0 <= it - 1 < n8:
            kb.sec("t", R[it - 1])
        if 0 <= it - 2 < n8:
            kb.sec("t", RB[it - 2])
        if 0 <= it - 1 < n8:
            kb.sec("s", SQ[it - 1])
        if it < n8:
            kb.sec("v", VP[it])
        if 0 <= it - 1 < n8:
            kb.sec("v", VR[it - 1])
        if 0 <= it - 2 < n8:
            kb.sec("v", VR2[it - 2])
        if 0 <= it - 1 < n8:
            kb.sec("g", GA[it - 1])


def emit_v_block(kb, bf, toff, w, kcols, ps):
    for ti, kcol in enumerate(kcols):
        slot = kcol % RING
        for half in range(2):
            pp = ps["pp"][(2 * ti + half) % 2]
            ppk = ("pp", (2 * ti + half) % 2)

            def t_v(eng, ti=ti, half=half, pp=pp, ppk=ppk):
                kb.need_r("t", eng, ("wv_g",))
                kb.need_r("t", eng, ("xsba",))
                kb.need_w("t", eng, ppk)
                for kc in range(8):
                    i = eng.matmul(pp[:, :], bf["xsba"][:, kc, 128 * ti:128 * (ti + 1)],
                                   bf["wvg"][:, kc, 512 * half:512 * (half + 1)],
                                   start=(kc == 0), stop=(kc == 7))
                c = kb.inc("t", i)
                kb.mark_w(ppk, "t", c)
                kb.mark_r(("wv_g",), "t", c)
                kb.mark_r(("xsba",), "t", c)
            kb.sec("t", t_v)

            def v_v(eng, slot=slot, half=half, pp=pp, ppk=ppk):
                kb.need_r("v", eng, ppk)
                kb.need_w("v", eng, ("vrs", slot, half))
                dst = bf["vrr"][:, slot, :].rearrange(
                    "p (h c) -> p h c", c=65)[:, 8 * half:8 * (half + 1), 0:64]
                src = pp[:, :].rearrange("p (h c) -> p h c", c=64)
                c = kb.inc("v", eng.tensor_copy(dst, src))
                kb.mark_w(("vrs", slot, half), "v", c)
                kb.mark_r(ppk, "v", c)
            kb.sec("v", v_v)


def emit_attn_core(kb, bf, l, S, ps):
    for hp in range(8):
        def t_sim(eng, hp=hp):
            kb.psum_fence("t", eng)
            kb.need_w("t", eng, ("psrb2",))
            kb.need_r("t", eng, ("qr",))
            kb.need_r("t", eng, ("consts",))
            kb.need_w("t", eng, ("pssim",))
            for Irel in range(4):
                I = 4 * S + Irel
                for Jrel in range(3):
                    kcol = I + Jrel
                    slot = kcol % RING
                    kb.need_r("t", eng, ("krs", slot, hp))
                    off = 384 * Irel + 128 * Jrel
                    masked = Jrel != 1
                    eng.matmul(ps["sim"][:, off:off + 128],
                               bf["krr"][0:64, hp, 128 * slot:128 * (slot + 1)],
                               bf["qr"][0:64, hp, 128 * Irel:128 * (Irel + 1)],
                               start=True, stop=not masked, tile_position=(0, 0))
                    eng.matmul(ps["sim"][:, 1536 + off:1536 + off + 128],
                               bf["krr"][64:128, hp, 128 * slot:128 * (slot + 1)],
                               bf["qr"][64:128, hp, 128 * Irel:128 * (Irel + 1)],
                               start=True, stop=not masked, tile_position=(64, 0))
                    if masked:
                        m = bf["masku"] if Jrel == 0 else bf["maskl"]
                        eng.matmul(ps["sim"][:, off:off + 128], bf["idn"][:, :],
                                   m[:, :], start=False, stop=True)
                        i = eng.matmul(ps["sim"][:, 1536 + off:1536 + off + 128],
                                       bf["idn"][:, :], m[:, :], start=False, stop=True)
                    else:
                        i = None
            c = kb.inc("t", eng.nop() if i is None else i)
            kb.mark_w(("pssim",), "t", c)
            kb.mark_r(("qr",), "t", c)
        kb.sec("t", t_sim)

        if S == 0:
            def v_flag(eng, hp=hp):
                kb.need_r("v", eng, ("pssim",))
                kb.need_r("v", eng, ("consts",))
                i = None
                for (Irel, Jrel) in ((0, 0), (0, 1), (1, 0)):
                    off = 384 * Irel + 128 * Jrel
                    for par in range(2):
                        sl = ps["sim"][:, 1536 * par + off:1536 * par + off + 128]
                        i = eng.tensor_scalar_add(sl, sl, bf["hmask"][:, 0:1])
                c = kb.inc("v", i)
                kb.mark_w(("pssim",), "v", c)
            kb.sec("v", v_flag)

        def s_exp(eng, hp=hp):
            kb.need_r("s", eng, ("pssim",))
            kb.need_r("s", eng, ("negB",))
            kb.need_w("s", eng, ("expt",))
            eng.activation(bf["expt"][:, 0, :], ps["sim"][:, 0:1536], AF.Exp,
                           bias=bf["negB"][:, l:l + 1])
            c = kb.inc("s", eng.activation(bf["expt"][:, 1, :], ps["sim"][:, 1536:3072],
                                           AF.Exp, bias=bf["negB"][:, l:l + 1]))
            kb.mark_w(("expt",), "s", c)
            kb.mark_r(("pssim",), "s", c)
        kb.sec("s", s_exp)

        def t_av(eng, hp=hp):
            kb.need_r("t", eng, ("expt",))
            kb.need_w("t", eng, ("psavA",))
            kb.need_w("t", eng, ("psavB",))
            for Irel in range(4):
                I = 4 * S + Irel
                for Jrel in range(3):
                    kcol = I + Jrel
                    slot = kcol % RING
                    for half in range(2):
                        kb.need_r("t", eng, ("vrs", slot, half))
                    off = 384 * Irel + 128 * Jrel
                    hA, hB = 2 * hp, 2 * hp + 1
                    eng.matmul(ps["avA"][0:65, 128 * Irel:128 * (Irel + 1)],
                               bf["vrr"][:, slot, 65 * hA:65 * (hA + 1)],
                               bf["expt"][:, 0, off:off + 128],
                               start=(Jrel == 0), stop=(Jrel == 2))
                    i = eng.matmul(ps["avB"][0:65, 128 * Irel:128 * (Irel + 1)],
                                   bf["vrr"][:, slot, 65 * hB:65 * (hB + 1)],
                                   bf["expt"][:, 1, off:off + 128],
                                   start=(Jrel == 0), stop=(Jrel == 2))
            c = kb.inc("t", i)
            kb.mark_w(("psavA",), "t", c)
            kb.mark_w(("psavB",), "t", c)
            kb.mark_r(("expt",), "t", c)
            for Irel in range(4):
                for Jrel in range(3):
                    slot = (4 * S + Irel + Jrel) % RING
                    for half in range(2):
                        kb.mark_r(("vrs", slot, half), "t", c)
        kb.sec("t", t_av)

        def s_rec(eng, hp=hp):
            kb.need_r("s", eng, ("psavA",))
            kb.need_r("s", eng, ("psavB",))
            kb.need_w("s", eng, ("recdA",))
            kb.need_w("s", eng, ("recdB",))
            kb.need_w("s", eng, ("t1",))
            kb.need_w("s", eng, ("t2",))
            eng.activation(bf["t1"][0:1, :], ps["avA"][64:65, :], AF.Ln)
            eng.activation(bf["t2"][0:1, :], ps["avB"][64:65, :], AF.Ln)
            eng.drain()
            eng.activation(bf["recdA"][0:1, :], bf["t1"][0:1, :], AF.Exp, scale=-1.0)
            c = kb.inc("s", eng.activation(bf["recdB"][0:1, :], bf["t2"][0:1, :],
                                           AF.Exp, scale=-1.0))
            kb.mark_w(("recdA",), "s", c)
            kb.mark_w(("recdB",), "s", c)
            kb.mark_w(("t1",), "s", c)
            kb.mark_w(("t2",), "s", c)
            kb.mark_r(("psavA",), "s", c)
            kb.mark_r(("psavB",), "s", c)
        kb.sec("s", s_rec)

        def t_dbc(eng, hp=hp):
            kb.need_r("t", eng, ("recdA",))
            kb.need_r("t", eng, ("recdB",))
            kb.need_r("t", eng, ("consts",))
            kb.need_w("t", eng, ("psrb2",))
            eng.matmul(ps["rb2"][:, :], bf["selA"][:, :], bf["recdA"][0:1, :],
                       start=True, stop=False)
            c = kb.inc("t", eng.matmul(ps["rb2"][:, :], bf["selB"][:, :],
                                       bf["recdB"][0:1, :], start=False, stop=True))
            kb.mark_w(("psrb2",), "t", c)
            kb.mark_r(("recdA",), "t", c)
            kb.mark_r(("recdB",), "t", c)
        kb.sec("t", t_dbc)

        def v_av(eng, hp=hp):
            kb.need_r("v", eng, ("psavA",))
            kb.need_r("v", eng, ("psavB",))
            kb.need_w("v", eng, ("avbf",))
            eng.tensor_copy(bf["avbf"][0:64, :], ps["avA"][0:64, :])
            c = kb.inc("v", eng.tensor_copy(bf["avbf"][64:128, :], ps["avB"][0:64, :]))
            kb.mark_w(("avbf",), "v", c)
            kb.mark_r(("psavA",), "v", c)
            kb.mark_r(("psavB",), "v", c)
        kb.sec("v", v_av)

        def v_div(eng, hp=hp):
            kb.need_r("v", eng, ("avbf",))
            kb.need_r("v", eng, ("psrb2",))
            kb.need_w("v", eng, ("xsba",))
            eng.drain()
            c = kb.inc("v", eng.tensor_tensor(bf["avun"][:, hp, :], bf["avbf"][:, :],
                                              ps["rb2"][:, :], ALU.mult))
            kb.mark_w(("xsba",), "v", c)
            kb.mark_r(("avbf",), "v", c)
            kb.mark_r(("psrb2",), "v", c)
        kb.sec("v", v_div)


def emit_outproj(kb, bf, S, ps):
    toff = SUB * S
    for mc in range(8):
        pok = ("po", mc % 2)

        def t_o(eng, mc=mc, pok=pok):
            if mc == 0:
                kb.psum_fence("t", eng)
            kb.need_r("t", eng, ("wo_g",))
            kb.need_r("t", eng, ("xsba",))
            kb.need_w("t", eng, pok)
            for kc in range(8):
                i = eng.matmul(ps["po"][mc % 2][:, :], bf["wog"][:, kc, 128 * mc:128 * (mc + 1)],
                               bf["avun"][:, kc, :], start=(kc == 0), stop=(kc == 7))
            c = kb.inc("t", i)
            kb.mark_w(pok, "t", c)
            kb.mark_r(("wo_g",), "t", c)
            kb.mark_r(("xsba",), "t", c)
        kb.sec("t", t_o)

        def v_o(eng, mc=mc, pok=pok):
            kb.need_r("v", eng, pok)
            kb.need_w("v", eng, ("xres",))
            c = kb.inc("v", eng.tensor_tensor(
                bf["xres"][:, mc, 1 + toff:1 + toff + 512], ps["po"][mc % 2][:, :],
                bf["xres"][:, mc, 1 + toff:1 + toff + 512], ALU.add))
            kb.mark_w(("xres",), "v", c)
            kb.mark_r(pok, "v", c)
        kb.sec("v", v_o)


def emit_exchange_e1(kb, bf, dram, l):
    def v_pack(eng):
        kb.need_r("v", eng, ("xres",))
        kb.need_w("v", eng, ("xrow2",))
        c = kb.inc("v", eng.tensor_copy(bf["xrow2"][:, :].unsqueeze(2),
                                        bf["xres"][:, :, T:T + 1]))
        kb.mark_w(("xrow2",), "v", c)
        kb.mark_r(("xres",), "v", c)
    kb.sec("v", v_pack)

    def g_pack(eng):
        sn = kb.dsem("e1p_d")
        kb.need_w("g", eng, ("e1io",))
        for mc in range(8):
            kb.need_r("g", eng, ("krs", 4, mc))
            kb.need_r("g", eng, ("krs", 5, mc))
        for s in (4, 5):
            for half in range(2):
                kb.need_r("g", eng, ("vrs", s, half))
        kb.need_r("g", eng, ("xres",))
        kb.inc(sn, eng.dma_start(out=dram["e1_in"][:, 0:2048], in_=bf["krr"][:, :, 512:768]), 16)
        kb.inc(sn, eng.dma_start(out=dram["e1_in"][:, 2048:4128], in_=bf["vrr"][:, 4:6, :]), 16)
        kb.need_r("g", eng, ("xrow2",))
        c = kb.inc(sn, eng.dma_start(
            out=dram["e1f_in"][:, :], in_=bf["xrow2"][:, :]), 16)
        kb.mark_r(("xrow2",), sn, c)
        for mc in range(8):
            kb.mark_r(("krs", 4, mc), sn, c)
            kb.mark_r(("krs", 5, mc), sn, c)
        for s in (4, 5):
            for half in range(2):
                kb.mark_r(("vrs", s, half), sn, c)
        kb.mark_r(("xres",), sn, c)

        eng.wait_ge(kb.sems[sn], c)
        kb.inc("cc", eng.collective_compute(
            "AllGather", ALU.bypass, replica_groups=CCGROUPS,
            ins=[dram["e1_in"][:, :].opt()], outs=[dram["e1_out"][:, :, :].opt()]))
        cc = kb.inc("cc", eng.collective_compute(
            "AllGather", ALU.bypass, replica_groups=CCGROUPS,
            ins=[dram["e1f_in"][:, :].opt()], outs=[dram["e1f_out"][:, :, :].opt()]))

        eng.wait_ge(kb.sems["cc"], cc)
        sn2 = kb.dsem("e1u_d")
        for mc in range(8):
            kb.need_w("g", eng, ("krs", 0, mc))
            kb.need_w("g", eng, ("krs", 1, mc))
        for s in (0, 1):
            for half in range(2):
                kb.need_w("g", eng, ("vrs", s, half))
        kb.need_w("g", eng, ("xrow",))
        kb.inc(sn2, eng.dma_start(out=bf["krr"][:, :, 0:256], in_=dram["e1_out"][0, :, 0:2048]), 16)
        kb.inc(sn2, eng.dma_start(out=bf["vrr"][:, 0:2, :], in_=dram["e1_out"][0, :, 2048:4128]), 16)
        c2 = kb.inc(sn2, eng.dma_start(
            out=bf["xrow"][:, :], in_=dram["e1f_out"][0, :, :]), 16)
        for mc in range(8):
            kb.mark_w(("krs", 0, mc), sn2, c2)
            kb.mark_w(("krs", 1, mc), sn2, c2)
        for s in (0, 1):
            for half in range(2):
                kb.mark_w(("vrs", s, half), sn2, c2)
        kb.mark_w(("xrow",), sn2, c2)
        kb.mark_w(("e1io",), sn2, c2)
    kb.sec("g", g_pack)

    def v_row(eng):
        kb.need_r("v", eng, ("xrow",))
        kb.need_r("v", eng, ("consts",))
        kb.need_w("v", eng, ("xres",))
        c = kb.inc("v", eng.tensor_scalar_mul(
            bf["xres"][:, :, 0:1], bf["xrow"][:, :].unsqueeze(2), bf["flag"][:, 0:1]))
        kb.mark_w(("xres",), "v", c)
        kb.mark_r(("xrow",), "v", c)
    kb.sec("v", v_row)


def emit_exchange_e2(kb, bf, dram, l):
    def v_pack(eng):
        kb.need_r("v", eng, ("xres",))
        kb.need_w("v", eng, ("xrow2",))
        c = kb.inc("v", eng.tensor_copy(bf["xrow2"][:, :].unsqueeze(2),
                                        bf["xres"][:, :, T:T + 1]))
        kb.mark_w(("xrow2",), "v", c)
        kb.mark_r(("xres",), "v", c)
    kb.sec("v", v_pack)

    def g(eng):
        sn = kb.dsem("e2p_d")
        kb.need_w("g", eng, ("e2io",))
        kb.need_r("g", eng, ("xres",))
        kb.need_r("g", eng, ("xrow2",))
        c = kb.inc(sn, eng.dma_start(
            out=dram["e2_in"][:, :], in_=bf["xrow2"][:, :]), 16)
        kb.mark_r(("xrow2",), sn, c)
        kb.mark_r(("xres",), sn, c)
        eng.wait_ge(kb.sems[sn], c)
        cc = kb.inc("cc", eng.collective_compute(
            "AllGather", ALU.bypass,
            replica_groups=CCGROUPS,
            ins=[dram["e2_in"][:, :].opt()],
            outs=[dram["e2_out"][:, :, :].opt()]))
        eng.wait_ge(kb.sems["cc"], cc)
        sn2 = kb.dsem("e2u_d")
        kb.need_w("g", eng, ("xrow",))
        c2 = kb.inc(sn2, eng.dma_start(
            out=bf["xrow"][:, :], in_=dram["e2_out"][0, :, :]), 16)
        kb.mark_w(("xrow",), sn2, c2)
        kb.mark_w(("e2io",), sn2, c2)
    kb.sec("g", g)

    def v_row(eng):
        kb.need_r("v", eng, ("xrow",))
        kb.need_w("v", eng, ("xres",))
        c = kb.inc("v", eng.tensor_scalar_mul(
            bf["xres"][:, :, 0:1], bf["xrow"][:, :].unsqueeze(2), bf["flag"][:, 0:1]))
        kb.mark_w(("xres",), "v", c)
        kb.mark_r(("xrow",), "v", c)
    kb.sec("v", v_row)


def emit_attn_layer(kb, bf, dram, l):
    nc = kb.nc
    emit_vrr_init(kb, bf)
    emit_attn_loads(kb, bf, dram, l)

    def mkps(ctx):
        return dict(
            pp=[ctx.enter_context(nc.psum_tensor(kb.uid("pp"), [128, 512], F32))
                for _ in range(2)],
            rot=[ctx.enter_context(nc.psum_tensor(kb.uid("prot"), [128, 512], F32))
                 for _ in range(2)],
            ss=[ctx.enter_context(nc.psum_tensor(kb.uid("pss"), [128, 512], F32))
                for _ in range(2)],
            rb=[ctx.enter_context(nc.psum_tensor(kb.uid("prb"), [128, 512], F32))
                for _ in range(2)],
        )
    PSK_QKV = [("pp", 0), ("pp", 1), ("psrot", 0), ("psrot", 1),
               ("psss", 0), ("psss", 1), ("psrb", 0), ("psrb", 1)]

    with ExitStack() as ctx:
        ps = mkps(ctx)
        emit_tab_load(kb, bf, dram, "k", T - 256, 256)
        emit_shift_cast_attn(kb, bf, T - 256, 256)
        emit_proj_postproc(kb, bf, "k", l, T - 256, 256, [16, 17], ps)
        emit_v_block(kb, bf, T - 256, 256, [16, 17], ps)
        kb.psum_close(PSK_QKV)
    emit_exchange_e1(kb, bf, dram, l)
    emit_bnd_snapshot(kb, bf)

    for S in range(NSUB):
        toff = SUB * S
        kcols = list(range(4 * S + 2, 4 * S + 6))
        with ExitStack() as ctx:
            ps = mkps(ctx)
            emit_tab_load(kb, bf, dram, "k", toff, SUB)
            emit_shift_cast_attn(kb, bf, toff, SUB, bnd=None if S == 0 else S - 1)
            emit_proj_postproc(kb, bf, "k", l, toff, SUB, kcols, ps)
            emit_v_block(kb, bf, toff, SUB, kcols, ps)
            emit_tab_load(kb, bf, dram, "q", toff, SUB)
            emit_proj_postproc(kb, bf, "q", l, toff, SUB, None, ps)
            kb.psum_close(PSK_QKV)
        with ExitStack() as ctx:
            ps2 = dict(
                sim=ctx.enter_context(nc.psum_tensor(kb.uid("psim"), [128, 3072], F32)),
                avA=ctx.enter_context(nc.psum_tensor(kb.uid("pavA"), [128, 512], F32)),
                avB=ctx.enter_context(nc.psum_tensor(kb.uid("pavB"), [128, 512], F32)),
            )
            ps2["rb2"] = ps2["sim"][:, 0:512]
            emit_attn_core(kb, bf, l, S, ps2)
            kb.psum_close([("pssim",), ("psavA",), ("psavB",), ("psrb2",)])
        with ExitStack() as ctx:
            ps3 = dict(po=[ctx.enter_context(
                nc.psum_tensor(kb.uid("po"), [128, 512], F32)) for _ in range(2)])
            emit_outproj(kb, bf, S, ps3)
            kb.psum_close([("po", 0), ("po", 1)])
    emit_exchange_e2(kb, bf, dram, l)
    kb.wflush()


def declare_attn_drams(nc, nranks=2):
    dram = {}
    EI = lambda n, s, d: nc.dram_tensor(n, s, d, kind="ExternalInput")
    dram["wqkv"] = EI("wqkv", [6, 1024, 3072], F32)
    dram["wo_i"] = EI("wo_i", [6, 1024, 1024], F32)
    dram["csq"] = EI("csq", [128, T], F32)
    dram["ssq"] = EI("ssq", [128, T], F32)
    dram["csk"] = EI("csk", [128, T], F32)
    dram["ssk"] = EI("ssk", [128, T], F32)
    dram["qs_m"] = EI("qs_m", [6, 128, 1], F32)
    dram["qs_p"] = EI("qs_p", [6, 128, 1], F32)
    dram["ks_m"] = EI("ks_m", [6, 128, 1], F32)
    dram["ks_p"] = EI("ks_p", [6, 128, 1], F32)
    dram["qs_r"] = EI("qs_r", [6, 64], F32)
    dram["ks_r"] = EI("ks_r", [6, 64], F32)
    dram["p128"] = EI("p128", [128, 128], BF16)
    dram["idn"] = EI("idn", [128, 128], BF16)
    dram["masku"] = EI("masku", [128, 128], BF16)
    dram["maskl"] = EI("maskl", [128, 128], BF16)
    dram["sel2t"] = EI("sel2t", [128, 2], BF16)
    dram["sel2"] = EI("sel2", [2, 128], BF16)
    dram["selA"] = EI("selA", [1, 128], F32)
    dram["selB"] = EI("selB", [1, 128], F32)
    dram["flag"] = EI("flag", [128, 1], F32)
    dram["hmask"] = EI("hmask", [128, 1], F32)
    dram["e1_in"] = nc.dram_tensor("e1_in", [128, 4128], BF16)
    dram["e1_out"] = nc.dram_tensor("e1_out", [nranks, 128, 4128], BF16)
    dram["e1f_in"] = nc.dram_tensor("e1f_in", [128, 8], F32)
    dram["e1f_out"] = nc.dram_tensor("e1f_out", [nranks, 128, 8], F32)
    dram["e2_in"] = nc.dram_tensor("e2_in", [128, 8], F32)
    dram["e2_out"] = nc.dram_tensor("e2_out", [nranks, 128, 8], F32)
    return dram


def _table_maxima():
    return 1.05, 1.0


def build_attn_test(nranks=1):
    global CCGROUPS
    CCGROUPS = [[0]] if nranks == 1 else [[0, 1]]
    nc = bass.Bass("TRN2", target_bir_lowering=False, debug=False)
    dram = declare_attn_drams(nc, nranks=nranks)
    x_in = nc.dram_tensor("x_in", [128, 8, T + 1], F32, kind="ExternalInput")
    out = nc.dram_tensor("out", [128, 8, T + 1], F32, kind="ExternalOutput")

    with ExitStack() as ctx:
        block = ctx.enter_context(nc.Block())
        kb = KB(nc, block, ctx)
        bf = alloc_common_bufs(nc, ctx)
        alloc_persist_bufs(nc, ctx, bf)
        alloc_attn_bufs(nc, ctx, bf)
        emit_init_consts(kb, bf)
        emit_attn_consts(kb, bf, dram)

        def y_in(eng):
            c = kb.inc(kb.dsem("xin_d"), eng.dma_start(out=bf["xres"][:, :, :], in_=x_in[:, :, :]), 16)
            kb.mark_w(("xres",), "xin_d", c)
        kb.sec("y", y_in)

        mscq, msck = _table_maxima()
        with nc.psum_tensor(kb.uid("ps_nb"), [128, 512], F32) as ps_nb:
            bf["ps_nb"] = ps_nb
            emit_negB(kb, bf, dram, mscq, msck)
            kb.psum_close([("ps_nb",)])

        emit_attn_layer(kb, bf, dram, 0)

        def y_out(eng):
            kb.need_r("y", eng, ("xres",))
            sn = kb.dsem("out_d")
            kb.inc(sn, eng.dma_start(out=out[:, :, :], in_=bf["xres"][:, :, :]), 16)
            kb.wait("y", eng, sn, kb.cnt[sn])
        kb.sec("y", y_out)
    return nc


def alloc_outer_bufs(nc, ctx, bf):
    A = lambda name, shape, dt: ctx.enter_context(nc.sbuf_tensor(_uname(name), shape, dt))
    bf["xo"] = A("xo", [128, 8, 512], F32)
    bf["xnb"] = A("xnb", [128, 8, 512], BF16)
    bf["att"] = A("att", [128, 8, 2048], BF16)
    bf["xsqo"] = bf["att"]
    bf["ktmp"] = A("ktmp", [128, 2, 512], BF16)
    bf["qtmp"] = A("qtmp", [128, 2, 512], BF16)
    bf["vtmp"] = A("vtmp", [128, 8, 130], BF16)
    bf["kstrip"] = A("kstrip", [128, 2, 4096], BF16)
    bf["vstrip"] = A("vstrip", [128, 2, 32 * 130], BF16)
    bf["qstrip"] = A("qstrip", [128, 2, 2048], BF16)
    bf["expo"] = A("expo", [128, 2048], BF16)
    bf["avbf_o"] = A("avbf_o", [128, 512], BF16)
    bf["to1"] = A("to1", [128, 512], F32)
    bf["oqsq"] = A("oqsq", [128, 512], BF16)
    bf["owg"] = A("owg", [128, 8, 1024], BF16)
    bf["recdA_o"] = A("recdA_o", [1, 512], F32)
    bf["recdB_o"] = A("recdB_o", [1, 512], F32)
    bf["accq"] = A("accq", [2, 8], F32)
    bf["acck"] = A("acck", [2, 8], F32)
    bf["tacc"] = A("tacc", [2, 8], F32)
    bf["tacc2"] = A("tacc2", [2, 8], F32)
    bf["negBo"] = A("negBo", [128, 8], F32)
    bf["ones21"] = A("ones21", [2, 1], F32)
    bf["bo"] = A("bo", [128, 8], F32)
    return bf


def emit_outer_ln(kb, bf, b8, ps):
    def v_cast(eng):
        kb.need_w("v", eng, ("xnb",))
        kb.need_r("v", eng, ("xo",))
        c = kb.inc("v", eng.tensor_copy(bf["xnb"][:, :, :], bf["xo"][:, :, :]))
        kb.mark_w(("xnb",), "v", c)
        kb.mark_r(("xo",), "v", c)
    kb.sec("v", v_cast)

    def s_sq(eng):
        kb.need_w("s", eng, ("xsqo",))
        kb.need_r("s", eng, ("xnb",))
        c = kb.inc("s", eng.activation(bf["xsqo"][:, :, 0:512], bf["xnb"][:, :, :],
                                       AF.Square))
        kb.mark_w(("xsqo",), "s", c)
        kb.mark_r(("xnb",), "s", c)
    kb.sec("s", s_sq)

    def t_st(eng):
        kb.psum_fence("t", eng)
        kb.need_w("t", eng, ("ps_st",))
        kb.need_r("t", eng, ("xnb",))
        kb.need_r("t", eng, ("xsqo",))
        kb.need_r("t", eng, ("ones128",))
        for kc in range(8):
            eng.matmul(ps["st"][0:1, :], bf["ones128"][:, :], bf["xnb"][:, kc, :],
                       start=(kc == 0), stop=(kc == 7), tile_position=(0, 0))
        for kc in range(8):
            i = eng.matmul(ps["st"][32:33, :], bf["ones128"][:, :],
                           bf["xsqo"][:, kc, 0:512], start=(kc == 0), stop=(kc == 7),
                           tile_position=(0, 32))
        c = kb.inc("t", i)
        kb.mark_w(("ps_st",), "t", c)
        kb.mark_r(("xnb",), "t", c)
        kb.mark_r(("xsqo",), "t", c)
    kb.sec("t", t_st)

    st = bf["st"]
    def v_f3a(eng):
        kb.need_r("v", eng, ("ps_st",))
        kb.need_w("v", eng, ("st",))
        eng.tensor_scalar_mul(st[0:1, 0, :], ps["st"][0:1, :], 1.0 / 1024.0)
        eng.drain()
        eng.scalar_tensor_tensor(st[0:1, 1, :], st[0:1, 0, :], -1.0, st[0:1, 0, :],
                                 ALU.mult, ALU.mult)
        eng.drain()
        c = kb.inc("v", eng.scalar_tensor_tensor(
            st[0:1, 2, :], ps["st"][32:33, :], 1.0 / 1024.0, st[0:1, 1, :],
            ALU.mult, ALU.add))
        kb.mark_r(("ps_st",), "v", c)
        kb.mark_w(("st",), "v", c)
    kb.sec("v", v_f3a)

    def s_f3(eng):
        kb.need_r("s", eng, ("st",))
        kb.need_r("s", eng, ("eps",))
        eng.activation(st[0:1, 1, :], st[0:1, 2, :], AF.Ln, bias=bf["eps"][0:1, :])
        eng.drain()
        c = kb.inc("s", eng.activation(st[0:1, 2, :], st[0:1, 1, :], AF.Exp, scale=-0.5))
        kb.mark_w(("st",), "s", c)
    kb.sec("s", s_f3)

    def v_f3b(eng):
        kb.need_r("v", eng, ("st",))
        c = kb.inc("v", eng.scalar_tensor_tensor(
            st[0:1, 1, :], st[0:1, 0, :], -1.0, st[0:1, 2, :], ALU.mult, ALU.mult))
        kb.mark_w(("st",), "v", c)
    kb.sec("v", v_f3b)

    def t_bc(eng):
        kb.need_r("t", eng, ("st",))
        kb.need_r("t", eng, ("ones1",))
        kb.need_w("t", eng, ("ps_rg",))
        kb.need_w("t", eng, ("ps_nm",))
        eng.matmul(ps["rg"][:, :], bf["ones1"][:, :], st[0:1, 2, :], start=True, stop=True)
        c = kb.inc("t", eng.matmul(ps["nm"][:, :], bf["ones1"][:, :], st[0:1, 1, :],
                                   start=True, stop=True))
        kb.mark_w(("ps_rg",), "t", c)
        kb.mark_w(("ps_nm",), "t", c)
        kb.mark_r(("st",), "t", c)
    kb.sec("t", t_bc)

    for kc in range(8):
        def v_xn(eng, kc=kc):
            kb.need_r("v", eng, ("ps_rg",))
            kb.need_r("v", eng, ("ps_nm",))
            kb.need_r("v", eng, ("xo",))
            kb.need_w("v", eng, ("tmpo",))
            if kc == 0:
                kb.need_w("v", eng, ("xnb",))
            eng.tensor_tensor(bf["to1"][:, :], bf["xo"][:, kc, :], ps["rg"][:, :], ALU.mult)
            eng.drain()
            c = kb.inc("v", eng.tensor_tensor(
                bf["xnb"][:, kc, :], bf["to1"][:, :], ps["nm"][:, :], ALU.add))
            kb.mark_r(("ps_rg",), "v", c)
            kb.mark_r(("ps_nm",), "v", c)
            kb.mark_r(("xo",), "v", c)
            kb.mark_w(("xnb",), "v", c)
            kb.mark_w(("tmpo",), "v", c)
        kb.sec("v", v_xn)


def emit_outer(kb, bf, dram, pc=None):
    nc = kb.nc

    def y_bo(eng):
        sn = kb.dsem("bo_d")
        c = kb.inc(sn, eng.dma_start(out=bf["bo"][:, :], in_=dram["bo_t"][:, :]), 16)
        kb.mark_w(("bo",), sn, c)
    kb.sec("y", y_bo)

    def v_init(eng):
        eng.memset(bf["vtmp"][:, :, :], 1.0)
        eng.memset(bf["accq"][:, :], 0.0)
        c = kb.inc("v", eng.memset(bf["acck"][:, :], 0.0))
        kb.mark_w(("vtmp", 0), "v", c)
        kb.mark_w(("vtmp", 1), "v", c)
        kb.mark_w(("accq",), "v", c)
        kb.mark_w(("acck",), "v", c)
    kb.sec("v", v_init)

    with ExitStack() as ctx:
        ps = dict(
            st=ctx.enter_context(nc.psum_tensor(kb.uid("ost"), [128, 512], F32)),
            rg=ctx.enter_context(nc.psum_tensor(kb.uid("org"), [128, 512], F32)),
            nm=ctx.enter_context(nc.psum_tensor(kb.uid("onm"), [128, 512], F32)),
        )
        for b8 in range(8):
            def y_x(eng, b8=b8):
                sn = kb.dsem("ox_d")
                kb.need_w("y", eng, ("xo",))
                c = kb.inc(sn, eng.dma_start(out=bf["xo"][:, :, :],
                                             in_=dram["x"][b8, :, :, :]), 16)
                kb.mark_w(("xo",), sn, c)
            kb.sec("y", y_x)

            emit_outer_ln(kb, bf, b8, ps)

            def g_xn(eng, b8=b8):
                sn = kb.dsem("xnsp_d")
                kb.need_r("g", eng, ("xnb",))
                c = kb.inc(sn, eng.dma_start(out=dram["xnspill"][b8, :, :, :],
                                             in_=bf["xnb"][:, :, :]), 16)
                kb.mark_r(("xnb",), sn, c)
                kb.mark_w(("xnsp", b8), sn, c)
            kb.sec("g", g_xn)
            if pc:
                emit_precast_step(kb, pc, 1)
        kb.psum_close([("ps_st",), ("ps_rg",), ("ps_nm",)])

    with ExitStack() as ctx:
        ps = dict(
            pp=[ctx.enter_context(nc.psum_tensor(kb.uid("opp"), [128, 512], F32))
                for _ in range(2)],
            ss=ctx.enter_context(nc.psum_tensor(kb.uid("oss"), [128, 512], F32)),
        )
        for kind, wsrc in (("k", "wk"), ("v", "wv"), ("q", "wq")):
            def g_w(eng, wsrc=wsrc):
                kb.need_w("g", eng, ("owg",))
                sn = kb.dsem("owg_d")
                for kc in range(8):
                    kb.inc(sn, eng.dma_start(
                        out=bf["owg"][:, kc, :],
                        in_=dram[wsrc][128 * kc:128 * (kc + 1), :]), 16)
                kb.mark_w(("owg",), sn, kb.cnt[sn])
            kb.sec("g", g_w)

            nb8 = 4 if kind == "q" else 8
            for b8 in range(nb8):
                def y_xn(eng, b8=b8):
                    sn = kb.dsem("xnl_d")
                    kb.need_w("y", eng, ("xnb",))
                    kb.need_r("y", eng, ("xnsp", b8))
                    c = kb.inc(sn, eng.dma_start(out=bf["xnb"][:, :, :],
                                                 in_=dram["xnspill"][b8, :, :, :]), 16)
                    kb.mark_w(("xnb",), sn, c)
                kb.sec("y", y_xn)
                if pc:
                    emit_precast_step(kb, pc, 2)

                if kind == "v":
                    for tc in range(4):
                        for half in range(2):
                            pp = ps["pp"][half]
                            ppk = ("pp", half)

                            def t_v(eng, tc=tc, half=half, pp=pp, ppk=ppk):
                                kb.psum_fence("t", eng)
                                kb.need_r("t", eng, ("owg",))
                                kb.need_r("t", eng, ("xnb",))
                                kb.need_w("t", eng, ppk)
                                for kc in range(8):
                                    i = eng.matmul(
                                        pp[:, :], bf["xnb"][:, kc, 128 * tc:128 * (tc + 1)],
                                        bf["owg"][:, kc, 512 * half:512 * (half + 1)],
                                        start=(kc == 0), stop=(kc == 7))
                                c = kb.inc("t", i)
                                kb.mark_w(ppk, "t", c)
                                kb.mark_r(("owg",), "t", c)
                                kb.mark_r(("xnb",), "t", c)
                            kb.sec("t", t_v)

                            def v_v(eng, half=half, pp=pp, ppk=ppk):
                                kb.need_r("v", eng, ppk)
                                kb.need_w("v", eng, ("vtmp", half))
                                dst = bf["vtmp"][:, 4 * half:4 * (half + 1), :].rearrange(
                                    "p a (b c) -> p a b c", c=65)[:, :, :, 0:64]
                                src = pp[:, :].rearrange("p (a b c) -> p a b c", b=2, c=64)
                                c = kb.inc("v", eng.tensor_copy(dst, src))
                                kb.mark_w(("vtmp", half), "v", c)
                                kb.mark_r(ppk, "v", c)
                            kb.sec("v", v_v)

                        def g_vsp(eng, tc=tc, b8=b8):
                            sn = kb.dsem("vsp_d")
                            for half in range(2):
                                kb.need_r("g", eng, ("vtmp", half))
                            for hp in range(8):
                                c = kb.inc(sn, eng.dma_start(
                                    out=dram["vspill"][hp, :, 4 * b8 + tc, :],
                                    in_=bf["vtmp"][:, hp, :]), 16)
                            kb.mark_w(("vsp",), sn, c)
                            for half in range(2):
                                kb.mark_r(("vtmp", half), sn, kb.cnt[sn])
                        kb.sec("g", g_vsp)
                    continue

                for mc in range(8):
                    pp = ps["pp"][mc % 2]
                    ppk = ("pp", mc % 2)
                    tmpb = bf["ktmp"] if kind == "k" else bf["qtmp"]
                    tmpk = ("ktmp", mc % 2) if kind == "k" else ("qtmp", mc % 2)

                    def t_p(eng, mc=mc, pp=pp, ppk=ppk):
                        kb.psum_fence("t", eng)
                        kb.need_r("t", eng, ("owg",))
                        kb.need_r("t", eng, ("xnb",))
                        kb.need_w("t", eng, ppk)
                        for kc in range(8):
                            i = eng.matmul(pp[:, :], bf["owg"][:, kc, 128 * mc:128 * (mc + 1)],
                                           bf["xnb"][:, kc, :], start=(kc == 0), stop=(kc == 7))
                        c = kb.inc("t", i)
                        kb.mark_w(ppk, "t", c)
                        kb.mark_r(("owg",), "t", c)
                        kb.mark_r(("xnb",), "t", c)
                    kb.sec("t", t_p)

                    def v_e(eng, mc=mc, pp=pp, ppk=ppk, tmpb=tmpb, tmpk=tmpk):
                        kb.need_r("v", eng, ppk)
                        kb.need_w("v", eng, tmpk)
                        kb.need_w("v", eng, ("qsqo",))
                        eng.tensor_copy(tmpb[:, mc % 2, :], pp[:, :])
                        eng.drain()
                        c = kb.inc("v", eng.scalar_tensor_tensor(
                            bf["oqsq"][:, :], pp[:, :], 1.0, tmpb[:, mc % 2, :],
                            ALU.mult, ALU.mult))
                        kb.mark_w(tmpk, "v", c)
                        kb.mark_w(("qsqo",), "v", c)
                        kb.mark_r(ppk, "v", c)
                    kb.sec("v", v_e)

                    def t_ss(eng):
                        kb.need_r("t", eng, ("qsqo",))
                        kb.need_r("t", eng, ("consts",))
                        kb.need_w("t", eng, ("psss",))
                        c = kb.inc("t", eng.matmul(ps["ss"][0:2, :], bf["sel2t"][:, :],
                                                   bf["oqsq"][:, :], start=True, stop=True))
                        kb.mark_w(("psss",), "t", c)
                        kb.mark_r(("qsqo",), "t", c)
                    kb.sec("t", t_ss)

                    def v_acc(eng, mc=mc, kind=kind):
                        acc = bf["accq"] if kind == "q" else bf["acck"]
                        akey = ("accq",) if kind == "q" else ("acck",)
                        kb.need_r("v", eng, ("psss",))
                        kb.need_w("v", eng, ("tacc",))
                        kb.need_r("v", eng, akey)
                        eng.tensor_reduce(bf["tacc"][0:2, 0:1], ps["ss"][0:2, :],
                                          mybir.AxisListType.X, ALU.max)
                        eng.drain()
                        c = kb.inc("v", eng.tensor_tensor(
                            acc[0:2, mc:mc + 1], acc[0:2, mc:mc + 1],
                            bf["tacc"][0:2, 0:1], ALU.max))
                        kb.mark_w(akey, "v", c)
                        kb.mark_w(("tacc",), "v", c)
                        kb.mark_r(("psss",), "v", c)
                    kb.sec("v", v_acc)

                    def y_sp(eng, mc=mc, b8=b8, kind=kind, tmpb=tmpb, tmpk=tmpk):
                        sp = dram["kspill"] if kind == "k" else dram["qspill"]
                        sn = kb.dsem(kind + "sp_d" + str(mc % 2))
                        kb.need_r("y", eng, tmpk)
                        c = kb.inc(sn, eng.dma_start(
                            out=sp[mc, :, 512 * b8:512 * (b8 + 1)],
                            in_=tmpb[:, mc % 2, :]), 16)
                        kb.mark_r(tmpk, sn, c)
                        kb.mark_w((kind + "sp", mc), sn, c)
                    kb.sec("y", y_sp)
        kb.psum_close([("pp", 0), ("pp", 1), ("psss",)])


def emit_outer_attn(kb, bf, dram, pc=None):
    nc = kb.nc
    with ExitStack() as ctx:
        ps_nb = ctx.enter_context(nc.psum_tensor(kb.uid("onb"), [128, 512], F32))

        def v_b1(eng):
            kb.need_r("v", eng, ("accq",))
            kb.need_r("v", eng, ("acck",))
            kb.need_w("v", eng, ("tacc",))
            eng.memset(bf["ones21"][:, :], 1.0)
            c = kb.inc("v", eng.tensor_tensor(bf["tacc"][0:2, :], bf["accq"][0:2, :],
                                              bf["acck"][0:2, :], ALU.mult))
            kb.mark_w(("tacc",), "v", c)
            kb.mark_w(("ones21",), "v", c)
        kb.sec("v", v_b1)

        def s_b(eng):
            kb.need_r("s", eng, ("tacc",))
            eng.activation(bf["tacc2"][0:2, :], bf["tacc"][0:2, :], AF.Ln)
            eng.drain()
            c = kb.inc("s", eng.activation(bf["tacc"][0:2, :], bf["tacc2"][0:2, :],
                                           AF.Exp, scale=0.5))
            kb.mark_w(("tacc",), "s", c)
        kb.sec("s", s_b)

        def t_b(eng):
            kb.psum_fence("t", eng)
            kb.need_r("t", eng, ("tacc",))
            kb.need_r("t", eng, ("ones21",))
            c = kb.inc("t", eng.matmul(ps_nb[0:1, 0:8], bf["ones21"][:, :],
                                       bf["tacc"][0:2, :], start=True, stop=True))
            kb.mark_w(("ps_nb",), "t", c)
            kb.mark_r(("tacc",), "t", c)
        kb.sec("t", t_b)

        def v_b2(eng):
            kb.need_r("v", eng, ("ps_nb",))
            kb.need_w("v", eng, ("tacc",))
            c = kb.inc("v", eng.tensor_scalar_mul(bf["tacc"][0:1, :], ps_nb[0:1, 0:8],
                                                  -0.125 * 1.05))
            kb.mark_w(("tacc",), "v", c)
            kb.mark_r(("ps_nb",), "v", c)
        kb.sec("v", v_b2)

        def t_b2(eng):
            kb.need_r("t", eng, ("tacc",))
            kb.need_r("t", eng, ("ones1",))
            kb.need_w("t", eng, ("ps_nb",))
            c = kb.inc("t", eng.matmul(ps_nb[:, 0:8], bf["ones1"][:, :],
                                       bf["tacc"][0:1, :], start=True, stop=True))
            kb.mark_w(("ps_nb",), "t", c)
            kb.mark_r(("tacc",), "t", c)
        kb.sec("t", t_b2)

        def v_b3(eng):
            kb.need_r("v", eng, ("ps_nb",))
            kb.need_w("v", eng, ("negBo",))
            c = kb.inc("v", eng.tensor_copy(bf["negBo"][:, :], ps_nb[:, 0:8]))
            kb.mark_w(("negBo",), "v", c)
            kb.mark_r(("ps_nb",), "v", c)
        kb.sec("v", v_b3)
        kb.psum_close([("ps_nb",)])

    with ExitStack() as ctx:
        ps = dict(
            sim=[ctx.enter_context(nc.psum_tensor(kb.uid("osim"), [128, 1024], F32))
                 for _ in range(2)],
            avA=ctx.enter_context(nc.psum_tensor(kb.uid("oavA"), [128, 1024], F32)),
            avB=ctx.enter_context(nc.psum_tensor(kb.uid("oavB"), [128, 1024], F32)),
        )
        for hp in range(8):
            buf = hp % 2

            def g_stream(eng, hp=hp, buf=buf):
                for nm, dst, sh in (("kstrip", bf["kstrip"], dram["kspill"]),
                                    ("qstrip", bf["qstrip"], dram["qspill"]),
                                    ("vstrip", bf["vstrip"], dram["vspill"])):
                    kb.need_w("g", eng, (nm, buf))
                    if nm == "kstrip":
                        kb.need_r("g", eng, ("ksp", hp))
                    elif nm == "qstrip":
                        kb.need_r("g", eng, ("qsp", hp))
                    else:
                        kb.need_r("g", eng, ("vsp",))
                    sn = kb.dsem(nm + "_d" + str(buf))
                    if nm == "vstrip":
                        c = kb.inc(sn, eng.dma_start(
                            out=dst[:, buf, :],
                            in_=sh[hp, :, :, :].rearrange("p a b -> p (a b)")), 16)
                    else:
                        c = kb.inc(sn, eng.dma_start(out=dst[:, buf, :], in_=sh[hp, :, :]), 16)
                    kb.mark_w((nm, buf), sn, c)
            kb.sec("g", g_stream)

            for ih in range(2):
                if pc:
                    emit_precast_step(kb, pc, 1)
                TS, SE, TA = [], [], []
                for jc in range(32):
                    for i2 in range(2):
                        def t_sim(eng, hp=hp, buf=buf, ih=ih, jc=jc, i2=i2):
                            kb.psum_fence("t", eng)
                            kb.need_r("t", eng, ("kstrip", buf))
                            kb.need_r("t", eng, ("qstrip", buf))
                            kb.need_w("t", eng, ("pssim", i2))
                            kb.need_w("t", eng, ("psrb2",))
                            if jc == 0:
                                kb.need_w("t", eng, ("psavA",))
                                kb.need_w("t", eng, ("psavB",))
                            ib = 2 * ih + i2
                            eng.matmul(ps["sim"][i2][:, 0:512],
                                       bf["kstrip"][0:64, buf, 128 * jc:128 * (jc + 1)],
                                       bf["qstrip"][0:64, buf, 512 * ib:512 * (ib + 1)],
                                       start=True, stop=True, tile_position=(0, 0))
                            i = eng.matmul(ps["sim"][i2][:, 512:1024],
                                           bf["kstrip"][64:128, buf, 128 * jc:128 * (jc + 1)],
                                           bf["qstrip"][64:128, buf, 512 * ib:512 * (ib + 1)],
                                           start=True, stop=True, tile_position=(64, 0))
                            c = kb.inc("t", i)
                            kb.mark_w(("pssim", i2), "t", c)
                            kb.mark_r(("kstrip", buf), "t", c)
                            kb.mark_r(("qstrip", buf), "t", c)
                        TS.append(t_sim)

                        def s_exp(eng, hp=hp, i2=i2):
                            kb.need_r("s", eng, ("pssim", i2))
                            kb.need_r("s", eng, ("negBo",))
                            kb.need_w("s", eng, ("expo", i2))
                            c = kb.inc("s", eng.activation(
                                bf["expo"][:, 1024 * i2:1024 * (i2 + 1)],
                                ps["sim"][i2][:, :], AF.Exp,
                                bias=bf["negBo"][:, hp:hp + 1], scale=0.125))
                            kb.mark_w(("expo", i2), "s", c)
                            kb.mark_r(("pssim", i2), "s", c)
                        SE.append(s_exp)

                        def t_av(eng, buf=buf, jc=jc, i2=i2):
                            kb.need_r("t", eng, ("expo", i2))
                            kb.need_r("t", eng, ("vstrip", buf))
                            eng.matmul(ps["avA"][0:65, 512 * i2:512 * (i2 + 1)],
                                       bf["vstrip"][:, buf, 130 * jc:130 * jc + 65],
                                       bf["expo"][:, 1024 * i2:1024 * i2 + 512],
                                       start=(jc == 0), stop=(jc == 31))
                            i = eng.matmul(ps["avB"][0:65, 512 * i2:512 * (i2 + 1)],
                                           bf["vstrip"][:, buf, 130 * jc + 65:130 * (jc + 1)],
                                           bf["expo"][:, 1024 * i2 + 512:1024 * (i2 + 1)],
                                           start=(jc == 0), stop=(jc == 31))
                            c = kb.inc("t", i)
                            kb.mark_w(("psavA",), "t", c)
                            kb.mark_w(("psavB",), "t", c)
                            kb.mark_r(("expo", i2), "t", c)
                            kb.mark_r(("vstrip", buf), "t", c)
                        TA.append(t_av)

                nstep = len(TS)
                for k in range(nstep + 1):
                    if k < nstep:
                        kb.sec("t", TS[k])
                    if 0 <= k - 1 < nstep:
                        kb.sec("s", SE[k - 1])
                        kb.sec("t", TA[k - 1])

                for i2 in range(2):
                    ib = 2 * ih + i2

                    def s_rec(eng, i2=i2):
                        kb.need_r("s", eng, ("psavA",))
                        kb.need_r("s", eng, ("psavB",))
                        kb.need_w("s", eng, ("recdA_o",))
                        kb.need_w("s", eng, ("recdB_o",))
                        kb.need_w("s", eng, ("tmpo",))
                        kb.need_w("s", eng, ("xnb",))
                        reclA = bf["to1"][0:1, :]
                        reclB = bf["xnb"][0:1, 0:2, :].bitcast(F32).rearrange(
                            "p a b -> p (a b)")
                        eng.activation(reclA,
                                       ps["avA"][64:65, 512 * i2:512 * (i2 + 1)], AF.Ln)
                        eng.activation(reclB,
                                       ps["avB"][64:65, 512 * i2:512 * (i2 + 1)], AF.Ln)
                        eng.drain()
                        eng.activation(bf["recdA_o"][0:1, :], reclA, AF.Exp, scale=-1.0)
                        c = kb.inc("s", eng.activation(bf["recdB_o"][0:1, :], reclB,
                                                       AF.Exp, scale=-1.0))
                        kb.mark_w(("recdA_o",), "s", c)
                        kb.mark_w(("recdB_o",), "s", c)
                        kb.mark_w(("tmpo",), "s", c)
                        kb.mark_w(("xnb",), "s", c)
                        kb.mark_r(("psavA",), "s", c)
                        kb.mark_r(("psavB",), "s", c)
                    kb.sec("s", s_rec)

                    def t_dbc(eng, i2=i2):
                        kb.need_r("t", eng, ("recdA_o",))
                        kb.need_r("t", eng, ("recdB_o",))
                        kb.need_r("t", eng, ("consts",))
                        kb.need_w("t", eng, ("psrb2",))
                        kb.need_w("t", eng, ("pssim", 0))
                        eng.matmul(ps["sim"][0][:, 0:512], bf["selA"][:, :],
                                   bf["recdA_o"][0:1, :], start=True, stop=False)
                        c = kb.inc("t", eng.matmul(ps["sim"][0][:, 0:512], bf["selB"][:, :],
                                                   bf["recdB_o"][0:1, :], start=False, stop=True))
                        kb.mark_w(("psrb2",), "t", c)
                        kb.mark_r(("recdA_o",), "t", c)
                        kb.mark_r(("recdB_o",), "t", c)
                    kb.sec("t", t_dbc)

                    def v_av(eng, i2=i2):
                        kb.need_r("v", eng, ("psavA",))
                        kb.need_r("v", eng, ("psavB",))
                        kb.need_w("v", eng, ("avbf_o",))
                        eng.tensor_copy(bf["avbf_o"][0:64, :],
                                        ps["avA"][0:64, 512 * i2:512 * (i2 + 1)])
                        c = kb.inc("v", eng.tensor_copy(
                            bf["avbf_o"][64:128, :],
                            ps["avB"][0:64, 512 * i2:512 * (i2 + 1)]))
                        kb.mark_w(("avbf_o",), "v", c)
                        kb.mark_r(("psavA",), "v", c)
                        kb.mark_r(("psavB",), "v", c)
                    kb.sec("v", v_av)

                    def v_div(eng, hp=hp, ib=ib):
                        kb.need_r("v", eng, ("avbf_o",))
                        kb.need_r("v", eng, ("psrb2",))
                        kb.need_w("v", eng, ("att", hp, ib))
                        eng.drain()
                        c = kb.inc("v", eng.tensor_tensor(
                            bf["att"][:, hp, 512 * ib:512 * (ib + 1)], bf["avbf_o"][:, :],
                            ps["sim"][0][:, 0:512], ALU.mult))
                        kb.mark_w(("att", hp, ib), "v", c)
                        kb.mark_r(("avbf_o",), "v", c)
                        kb.mark_r(("psrb2",), "v", c)
                    kb.sec("v", v_div)
        kb.psum_close([("pssim", 0), ("pssim", 1), ("psavA",), ("psavB",),
                       ("psrb2",)])

    with ExitStack() as ctx:
        ps = dict(po=[ctx.enter_context(nc.psum_tensor(kb.uid("opo"), [128, 512], F32))
                      for _ in range(2)])

        def g_wo(eng):
            kb.need_w("g", eng, ("owg",))
            sn = kb.dsem("owg_d")
            for kc in range(8):
                kb.inc(sn, eng.dma_start(
                    out=bf["owg"][:, kc, :],
                    in_=dram["wo"][128 * kc:128 * (kc + 1), :]), 16)
            kb.mark_w(("owg",), sn, kb.cnt[sn])
        kb.sec("g", g_wo)

        for S in range(4):
            def y_x(eng, S=S):
                sn = kb.dsem("ox_d")
                kb.need_w("y", eng, ("xo",))
                c = kb.inc(sn, eng.dma_start(out=bf["xo"][:, :, :],
                                             in_=dram["x"][S, :, :, :]), 16)
                kb.mark_w(("xo",), sn, c)
            kb.sec("y", y_x)

            for mc in range(8):
                pok = ("po", mc % 2)

                def t_o(eng, S=S, mc=mc, pok=pok):
                    kb.psum_fence("t", eng)
                    kb.need_r("t", eng, ("owg",))
                    for hp in range(8):
                        kb.need_r("t", eng, ("att", hp, S))
                    kb.need_w("t", eng, pok)
                    for kc in range(8):
                        i = eng.matmul(ps["po"][mc % 2][:, :],
                                       bf["owg"][:, kc, 128 * mc:128 * (mc + 1)],
                                       bf["att"][:, kc, 512 * S:512 * (S + 1)],
                                       start=(kc == 0), stop=(kc == 7))
                    c = kb.inc("t", i)
                    kb.mark_w(pok, "t", c)
                    kb.mark_r(("owg",), "t", c)
                    for hp in range(8):
                        kb.mark_r(("att", hp, S), "t", c)
                kb.sec("t", t_o)

                def v_o(eng, S=S, mc=mc, pok=pok):
                    kb.need_r("v", eng, pok)
                    kb.need_r("v", eng, ("xo",))
                    kb.need_r("v", eng, ("bo",))
                    kb.need_w("v", eng, ("xres",))
                    c = kb.inc("v", eng.scalar_tensor_tensor(
                        bf["xres"][:, mc, 1 + 512 * S:1 + 512 * (S + 1)],
                        ps["po"][mc % 2][:, :], bf["bo"][:, mc:mc + 1],
                        bf["xo"][:, mc, :], ALU.add, ALU.add))
                    kb.mark_w(("xres",), "v", c)
                    kb.mark_r(pok, "v", c)
                    kb.mark_r(("xo",), "v", c)
                kb.sec("v", v_o)
        kb.psum_close([("po", 0), ("po", 1)])


def declare_outer_drams(nc):
    dram = {}
    EI = lambda n, s, d: nc.dram_tensor(n, s, d, kind="ExternalInput")
    dram["x"] = EI("x", [8, 128, 8, 512], F32)
    dram["wq"] = EI("wq", [1024, 1024], F32)
    dram["wk"] = EI("wk", [1024, 1024], F32)
    dram["wv"] = EI("wv", [1024, 1024], F32)
    dram["wo"] = EI("wo", [1024, 1024], F32)
    dram["bo_t"] = EI("bo_t", [128, 8], F32)
    dram["kspill"] = nc.dram_tensor("kspill", [8, 128, 4096], BF16)
    dram["xnspill"] = nc.dram_tensor("xnspill", [8, 128, 8, 512], BF16)
    dram["x1spill"] = nc.dram_tensor("x1spill", [4, 128, 8, 512], F32)
    dram["qspill"] = nc.dram_tensor("qspill", [8, 128, 2048], BF16)
    dram["vspill"] = nc.dram_tensor("vspill", [8, 128, 32, 130], BF16)
    return dram


def build_outer_test():
    nc = bass.Bass("TRN2", target_bir_lowering=False, debug=False)
    dram = declare_outer_drams(nc)
    EI = lambda n, s, d: nc.dram_tensor(n, s, d, kind="ExternalInput")
    dram["sel2t"] = EI("sel2t", [128, 2], BF16)
    dram["selA"] = EI("selA", [1, 128], F32)
    dram["selB"] = EI("selB", [1, 128], F32)
    out = nc.dram_tensor("out", [128, 8, T], F32, kind="ExternalOutput")

    with ExitStack() as ctx:
        block = ctx.enter_context(nc.Block())
        kb = KB(nc, block, ctx)
        bf = alloc_common_bufs(nc, ctx)
        alloc_persist_bufs(nc, ctx, bf)
        alloc_outer_bufs(nc, ctx, bf)
        emit_init_consts(kb, bf)

        def y_c(eng):
            sn = kb.dsem("const_d")
            for dst, srcn in ((bf["sel2t"], "sel2t"), (bf["selA"], "selA"),
                              (bf["selB"], "selB")):
                kb.inc(sn, eng.dma_start(out=dst[:, :], in_=dram[srcn][:, :]), 16)
            kb.mark_w(("consts",), sn, kb.cnt[sn])
        kb.sec("y", y_c)

        emit_outer(kb, bf, dram)
        emit_outer_attn(kb, bf, dram)

        def y_out(eng):
            kb.need_r("y", eng, ("xres",))
            sn = kb.dsem("out_d")
            kb.inc(sn, eng.dma_start(out=out[:, :, :], in_=bf["xres"][:, :, 1:T + 1]), 16)
            kb.wait("y", eng, sn, kb.cnt[sn])
        kb.sec("y", y_out)
    return nc


def declare_precast_drams(nc):
    dram = {}
    dram["wqkvb"] = nc.dram_tensor("wqkvb", [6, 1024, 3072], BF16)
    dram["wob"] = nc.dram_tensor("wob", [6, 1024, 1024], BF16)
    dram["w1b"] = nc.dram_tensor("w1b", [6, 1024, 5460], BF16)
    dram["w2b"] = nc.dram_tensor("w2b", [6, DI, 1024], BF16)
    return dram


def declare_ffn_drams(nc):
    dram = {}
    EI = lambda n, s, d: nc.dram_tensor(n, s, d, kind="ExternalInput")
    dram["w1"] = EI("w1", [6, 1024, 5460], F32)
    dram["w2"] = EI("w2", [6, DI, 1024], F32)
    dram["lngr"] = EI("lngr", [6, 8, 128], F32)
    dram["lng_t"] = EI("lng_t", [6, 128, 8], F32)
    dram["lnbr"] = EI("lnbr", [6, 8, 128], F32)
    dram["b1a"] = EI("b1a", [6, 128, NC_DI], F32)
    dram["b1g"] = EI("b1g", [6, 128, NC_DI], F32)
    dram["b2_t"] = EI("b2_t", [6, 128, 8], F32)
    return dram


def build_full(nranks=2, depth=DEPTH, attn_only=False, debug_taps=False):
    rec = Rec()
    _build_full_pass(nranks, depth, attn_only, debug_taps, "record", rec)
    return _build_full_pass(nranks, depth, attn_only, debug_taps, "replay", rec)


def _build_full_pass(nranks, depth, attn_only, debug_taps, mode, rec):
    global CCGROUPS
    if nranks == 1:
        CCGROUPS = [[0]]
    elif nranks == 2:
        CCGROUPS = [[0, 1]]
    else:
        CCGROUPS = [[0, 1], [2, 3], [4, 5], [6, 7]]
    nc = bass.Bass("TRN2", target_bir_lowering=False, debug=False)
    dram = {}
    dram.update(declare_outer_drams(nc))
    dram.update(declare_attn_drams(nc, nranks=min(nranks, 2)))
    dram.update(declare_ffn_drams(nc))
    if depth > 0:
        dram.update(declare_precast_drams(nc))
    out = nc.dram_tensor("out", [128, 8, T], F32, kind="ExternalOutput")

    with ExitStack() as ctx:
        block = ctx.enter_context(nc.Block())
        kb = KB(nc, block, ctx, mode=mode, rec=rec)
        bf = alloc_common_bufs(nc, ctx)
        alloc_persist_bufs(nc, ctx, bf)
        emit_init_consts(kb, bf)
        emit_attn_consts(kb, bf, dram)

        mscq, msck = _table_maxima()
        with nc.psum_tensor(kb.uid("ps_nb"), [128, 512], F32) as ps_nb:
            bf["ps_nb"] = ps_nb
            emit_negB(kb, bf, dram, mscq, msck)
            kb.psum_close([("ps_nb",)])

        pc = mk_precast_chunks(dram) if depth > 0 else []
        with ExitStack() as octx:
            alloc_outer_bufs(nc, octx, bf)
            emit_outer(kb, bf, dram, pc)
            emit_outer_attn(kb, bf, dram, pc)
            if depth > 0:
                emit_ln2_and_spill(kb, bf, dram)
        emit_precast_step(kb, pc, len(pc))
        kb.phase_barrier()

        for l in range(depth):
            with ExitStack() as actx:
                alloc_attn_bufs(nc, actx, bf)
                emit_attn_layer(kb, bf, dram, l)
            kb.phase_barrier()
            if attn_only and l == depth - 1:
                break
            with ExitStack() as fctx:
                alloc_ffn_bufs(nc, fctx, bf)
                emit_ffn_loads(kb, bf, dram, l)
                emit_ffn(kb, bf, dram, l)
            kb.phase_barrier()

        if depth > 0:
            emit_final_add(kb, bf, dram)
            kb.phase_barrier()

        if debug_taps:
            xn_d = nc.dram_tensor("xn_d", [128, 8, 512], BF16, kind="ExternalOutput")
            h_d = nc.dram_tensor("h_d", [128, 21, 512], BF16, kind="ExternalOutput")

            def y_tap(eng):
                for hc in range(21):
                    kb.need_r("y", eng, ("h", hc))
                for kc in range(8):
                    kb.need_r("y", eng, ("xn", kc))
                sn = kb.dsem("tap_d")
                kb.inc(sn, eng.dma_start(out=xn_d[:, :, :], in_=bf["xn"][:, :, :]), 16)
                kb.inc(sn, eng.dma_start(out=h_d[:, :, :], in_=bf["h"][:, 0:21, :]), 16)
                kb.wait("y", eng, sn, kb.cnt[sn])
            kb.sec("y", y_tap)

        def y_out(eng):
            kb.need_r("y", eng, ("xres",))
            sn = kb.dsem("out_d")
            kb.inc(sn, eng.dma_start(out=out[:, :, :], in_=bf["xres"][:, :, 1:T + 1]), 16)
            kb.wait("y", eng, sn, kb.cnt[sn])
        kb.sec("y", y_out)
        if mode == "replay":
            kb.flush()
    return nc


_NC_CACHE = {}

_INPUT_NAMES = [
    "x", "wq", "wk", "wv", "wo", "bo_t", "wqkv", "wo_i", "w1", "w2",
    "lngr", "lng_t", "lnbr", "b1a", "b1g", "b2_t", "csq", "ssq", "csk", "ssk",
    "qs_m", "qs_p", "ks_m", "ks_p", "qs_r", "ks_r",
    "p128", "idn", "masku", "maskl", "sel2t", "sel2", "selA", "selB",
    "flag", "hmask",
]


def kernel(**inputs):
    in_maps = stage_inputs(inputs)
    in_maps = [{k: m[k] for k in _INPUT_NAMES} for m in in_maps]
    if "full" not in _NC_CACHE:
        _NC_CACHE["full"] = build_full(nranks=8)
    nc = _NC_CACHE["full"]
    res = run_bass_kernel_spmd(nc, in_maps, core_ids=list(range(8)))
    return unstage_output(res.results)


def emit_ln2_and_spill(kb, bf, dram):
    nc = kb.nc
    with ExitStack() as ctx:
        ps = dict(
            st=ctx.enter_context(nc.psum_tensor(kb.uid("l2st"), [128, 512], F32)),
            rg=ctx.enter_context(nc.psum_tensor(kb.uid("l2rg"), [128, 512], F32)),
            nm=ctx.enter_context(nc.psum_tensor(kb.uid("l2nm"), [128, 512], F32)),
        )
        for S in range(4):
            col0 = 1 + 512 * S

            def g_sp(eng, S=S):
                sn = kb.dsem("x1sp_d")
                kb.need_r("g", eng, ("xres",))
                c = kb.inc(sn, eng.dma_start(out=dram["x1spill"][S, :, :, :],
                                             in_=bf["xres"][:, :, col0:col0 + 512]), 16)
                kb.mark_r(("xres",), sn, c)
                kb.mark_w(("x1sp", S), sn, c)
            kb.sec("g", g_sp)

            def v_cast(eng, S=S):
                kb.need_w("v", eng, ("xnb",))
                kb.need_r("v", eng, ("xres",))
                c = kb.inc("v", eng.tensor_copy(bf["xnb"][:, :, :],
                                                bf["xres"][:, :, col0:col0 + 512]))
                kb.mark_w(("xnb",), "v", c)
                kb.mark_r(("xres",), "v", c)
            kb.sec("v", v_cast)

            def s_sq(eng):
                kb.need_w("s", eng, ("xsqo",))
                kb.need_r("s", eng, ("xnb",))
                c = kb.inc("s", eng.activation(bf["xsqo"][:, :, 0:512], bf["xnb"][:, :, :],
                                               AF.Square))
                kb.mark_w(("xsqo",), "s", c)
                kb.mark_r(("xnb",), "s", c)
            kb.sec("s", s_sq)

            def t_st(eng):
                kb.psum_fence("t", eng)
                kb.need_w("t", eng, ("ps_st",))
                kb.need_r("t", eng, ("xnb",))
                kb.need_r("t", eng, ("xsqo",))
                kb.need_r("t", eng, ("ones128",))
                for kc in range(8):
                    eng.matmul(ps["st"][0:1, :], bf["ones128"][:, :], bf["xnb"][:, kc, :],
                               start=(kc == 0), stop=(kc == 7), tile_position=(0, 0))
                for kc in range(8):
                    i = eng.matmul(ps["st"][32:33, :], bf["ones128"][:, :],
                                   bf["xsqo"][:, kc, 0:512], start=(kc == 0), stop=(kc == 7),
                                   tile_position=(0, 32))
                c = kb.inc("t", i)
                kb.mark_w(("ps_st",), "t", c)
                kb.mark_r(("xnb",), "t", c)
                kb.mark_r(("xsqo",), "t", c)
            kb.sec("t", t_st)

            st = bf["st"]
            def v_f3a(eng):
                kb.need_r("v", eng, ("ps_st",))
                kb.need_w("v", eng, ("st",))
                eng.tensor_scalar_mul(st[0:1, 0, :], ps["st"][0:1, :], 1.0 / 1024.0)
                eng.drain()
                eng.scalar_tensor_tensor(st[0:1, 1, :], st[0:1, 0, :], -1.0, st[0:1, 0, :],
                                         ALU.mult, ALU.mult)
                eng.drain()
                c = kb.inc("v", eng.scalar_tensor_tensor(
                    st[0:1, 2, :], ps["st"][32:33, :], 1.0 / 1024.0, st[0:1, 1, :],
                    ALU.mult, ALU.add))
                kb.mark_r(("ps_st",), "v", c)
                kb.mark_w(("st",), "v", c)
            kb.sec("v", v_f3a)

            def s_f3(eng):
                kb.need_r("s", eng, ("st",))
                kb.need_r("s", eng, ("eps",))
                eng.activation(st[0:1, 1, :], st[0:1, 2, :], AF.Ln, bias=bf["eps"][0:1, :])
                eng.drain()
                c = kb.inc("s", eng.activation(st[0:1, 2, :], st[0:1, 1, :], AF.Exp, scale=-0.5))
                kb.mark_w(("st",), "s", c)
            kb.sec("s", s_f3)

            def v_f3b(eng):
                kb.need_r("v", eng, ("st",))
                c = kb.inc("v", eng.scalar_tensor_tensor(
                    st[0:1, 1, :], st[0:1, 0, :], -1.0, st[0:1, 2, :], ALU.mult, ALU.mult))
                kb.mark_w(("st",), "v", c)
            kb.sec("v", v_f3b)

            def t_bc(eng):
                kb.need_r("t", eng, ("st",))
                kb.need_r("t", eng, ("ones1",))
                kb.need_w("t", eng, ("ps_rg",))
                kb.need_w("t", eng, ("ps_nm",))
                eng.matmul(ps["rg"][:, :], bf["ones1"][:, :], st[0:1, 2, :],
                           start=True, stop=True)
                c = kb.inc("t", eng.matmul(ps["nm"][:, :], bf["ones1"][:, :], st[0:1, 1, :],
                                           start=True, stop=True))
                kb.mark_w(("ps_rg",), "t", c)
                kb.mark_w(("ps_nm",), "t", c)
                kb.mark_r(("st",), "t", c)
            kb.sec("t", t_bc)

            for kc in range(8):
                def v_xn(eng, kc=kc, S=S, col0=col0):
                    kb.need_r("v", eng, ("ps_rg",))
                    kb.need_r("v", eng, ("ps_nm",))
                    kb.need_r("v", eng, ("x1sp", S))
                    kb.need_w("v", eng, ("tmpo",))
                    if kc == 0:
                        kb.need_w("v", eng, ("xres",))
                    eng.tensor_tensor(bf["to1"][:, :], bf["xres"][:, kc, col0:col0 + 512],
                                      ps["rg"][:, :], ALU.mult)
                    eng.drain()
                    c = kb.inc("v", eng.tensor_tensor(
                        bf["xres"][:, kc, col0:col0 + 512], bf["to1"][:, :],
                        ps["nm"][:, :], ALU.add))
                    kb.mark_r(("ps_rg",), "v", c)
                    kb.mark_r(("ps_nm",), "v", c)
                    kb.mark_w(("xres",), "v", c)
                    kb.mark_w(("tmpo",), "v", c)
                kb.sec("v", v_xn)
        kb.psum_close([("ps_st",), ("ps_rg",), ("ps_nm",)])


def emit_final_add(kb, bf, dram):
    nc = kb.nc
    with ExitStack() as ctx:
        x1b = ctx.enter_context(nc.sbuf_tensor(_uname("x1b"), [128, 8, 512], F32))
        for S in range(4):
            col0 = 1 + 512 * S

            def y_ld(eng, S=S):
                sn = kb.dsem("x1l_d")
                kb.need_w("y", eng, ("x1b",))
                kb.need_r("y", eng, ("x1sp", S))
                c = kb.inc(sn, eng.dma_start(out=x1b[:, :, :],
                                             in_=dram["x1spill"][S, :, :, :]), 16)
                kb.mark_w(("x1b",), sn, c)
            kb.sec("y", y_ld)

            def v_add(eng, S=S, col0=col0):
                kb.need_r("v", eng, ("x1b",))
                kb.need_w("v", eng, ("xres",))
                c = kb.inc("v", eng.tensor_tensor(
                    bf["xres"][:, :, col0:col0 + 512], bf["xres"][:, :, col0:col0 + 512],
                    x1b[:, :, :], ALU.add))
                kb.mark_w(("xres",), "v", c)
                kb.mark_r(("x1b",), "v", c)
            kb.sec("v", v_add)
        kb.wflush()
